# version 16
# baseline (speedup 1.0000x reference)
import numpy as np
import ml_dtypes
import concourse.bass as bass
import concourse.mybir as mybir
from concourse.bass_utils import run_bass_kernel_spmd
from concourse.bass_core_rust import IndirectOffsetOnAxis

F32 = mybir.dt.float32
BF16 = mybir.dt.bfloat16
I32 = mybir.dt.int32
U32 = mybir.dt.uint32
AF = mybir.ActivationFunctionType
ALU = mybir.AluOpType
AX = mybir.AxisListType

D = 1024
CGRP = 160
LN_EPS = 1e-5
DEN_EPS = 1e-6
L1_H = 4096

ENGINES = ("sync", "tensor", "vector", "scalar", "gpsimd")


class Sem:
    def __init__(self, nc, name):
        self.h = nc.alloc_semaphore(name)
        self.n = 0
        self.name = name
        self.last = None

    def bump(self, k):
        self.n += k
        return self.n

    def tok(self):
        return (self, self.n)


class B:

    def __init__(self, nc, cfg):
        self.nc = nc
        self.cfg = cfg
        self.streams = {e: [] for e in ENGINES}
        self.maxwait = {e: {} for e in ENGINES}
        self.eP = Sem(nc, "eP")
        self.eV = Sem(nc, "eV")
        self.eA = Sem(nc, "eA")
        self.eG = Sem(nc, "eG")
        self.sq = [Sem(nc, f"sq{i}") for i in range(8)]
        self.gq = [Sem(nc, f"gq{i}") for i in range(8)]
        self.aq = [Sem(nc, f"aq{i}") for i in range(6)]
        self.psum_f = [nc.alloc_psum_tensor(f"psf{i}", [128, 512], F32) for i in range(6)]
        self.psum_b = [nc.alloc_psum_tensor(f"psb{i}", [128, 1024], BF16) for i in range(2)]
        self.pbd = {0: None, 1: None}

    def make_arenas(self, bf_cols, f32_cols):
        self.ar_bf = self.nc.alloc_sbuf_tensor("ar_bf", [128, bf_cols], BF16)
        self.ar_f = self.nc.alloc_sbuf_tensor("ar_f", [128, f32_cols], F32)
        self.ar_off = {"bf": 0, "f": 0}

    def arena_reset(self):
        self.ar_off = {"bf": 0, "f": 0}

    def abf(self, cols):
        o = self.ar_off["bf"]
        assert o + cols <= self.ar_bf.shape[1], f"bf arena overflow {o}+{cols}"
        self.ar_off["bf"] = o + cols
        return self.ar_bf[:, o:o + cols]

    def af32(self, cols):
        o = self.ar_off["f"]
        assert o + cols <= self.ar_f.shape[1], f"f32 arena overflow {o}+{cols}"
        self.ar_off["f"] = o + cols
        return self.ar_f[:, o:o + cols]

    def _waits(self, eng, deps):
        out = []
        mw = self.maxwait[eng]
        for d in deps:
            if d is None:
                continue
            sem, val = d
            if val <= 0:
                continue
            if mw.get(sem.name, -1) >= val:
                continue
            mw[sem.name] = val
            out.append((sem.h, val))
        return out

    def op(self, eng, f, deps=(), inc=None, inc_by=1):
        waits = self._waits(eng, deps)
        tok = None
        if inc is not None:
            tok = (inc, inc.bump(inc_by))

        def run(e, waits=waits, f=f, inc=inc, inc_by=inc_by):
            for (h, v) in waits:
                e.wait_ge(h, v)
            ins = f(e)
            if inc is not None and ins is not None:
                ins.then_inc(inc.h, inc_by)
        self.streams[eng].append(run)
        return tok

    def pe(self, f, deps=(), inc=True):
        return self.op("tensor", f, deps, self.eP if inc else None)

    def dve(self, f, deps=()):
        return self.op("vector", f, deps, self.eV)

    def act(self, f, deps=()):
        return self.op("scalar", f, deps, self.eA)

    def gp(self, f, deps=()):
        return self.op("gpsimd", f, deps, self.eG)

    def dma(self, queue, slot, f, deps=()):
        deps = list(deps)
        if slot.last is not None:
            deps.append(slot.last)
        tok = self.op(queue, f, deps, slot, inc_by=16)
        slot.last = tok
        return tok

    def fence(self):
        toks = [self.eP.tok(), self.eV.tok(), self.eA.tok(), self.eG.tok()]
        toks += [s.tok() for s in self.sq + self.gq + self.aq]
        for eng in ENGINES:
            waits = self._waits(eng, toks)
            if waits:
                def run(e, waits=waits):
                    for (h, v) in waits:
                        e.wait_ge(h, v)
                self.streams[eng].append(run)

    def assemble(self):
        block = self.nc.Block()
        with block:
            for eng in ENGINES:
                ops = self.streams[eng]

                def make(ops):
                    def body(e):
                        for f in ops:
                            f(e)
                    return body
                getattr(block, eng)(make(ops))



def preamble(b, W):
    nc = b.nc
    cfg = b.cfg
    st = {}

    consts = [LN_EPS, -0.5, 0.5, 1.0 / 1024.0, DEN_EPS, 1e9, 1e4,
              float(cfg["CAP"]), 0.125, 1e-10, 1.0 + LN_EPS]
    consts = sorted(set(consts))
    cst = nc.alloc_sbuf_tensor("cst", [128, len(consts)], F32)
    tok_cst = []
    for i, v in enumerate(consts):
        nc.const_aps.register(F32, v, cst[:, i:i + 1])
        tok_cst.append(b.gp(lambda e, _i=i, _v=v: e.memset(cst[:, _i:_i + 1], _v)))
    st["tok_cst"] = tok_cst[-1]

    b.arena_reset()
    iv = b.af32(128)
    st["id_f"] = nc.alloc_sbuf_tensor("id_f", [128, 128], F32)
    st["id_b"] = nc.alloc_sbuf_tensor("id_b", [128, 128], BF16)
    st["u01"] = nc.alloc_sbuf_tensor("u01", [128, 128], F32)
    t_iv = b.gp(lambda e: e.iota(iv[:, :], pattern=[[-1, 128]], base=0,
                                 channel_multiplier=1, allow_small_or_imprecise_dtypes=True))
    t1 = b.dve(lambda e: e.tensor_single_scalar(st["id_f"][:, :], iv[:, :], 0.0, ALU.is_equal), [t_iv, st["tok_cst"]])
    t2 = b.dve(lambda e: e.tensor_single_scalar(st["id_b"][:, :], iv[:, :], 0.0, ALU.is_equal), [t_iv])
    t3 = b.dve(lambda e: e.tensor_single_scalar(st["u01"][:, :], iv[:, :], 0.0, ALU.is_lt), [t_iv])
    st["tok_id"] = [t1, t2, t3]

    miv = b.af32(512)
    st["mk_b"] = nc.alloc_sbuf_tensor("mk_b", [128, 4 * 512], BF16)
    tmks = []
    for a in range(4):
        ti = b.gp(lambda e, _a=a: e.iota(miv[:, :], pattern=[[1, 512]], base=-128 * _a,
                                         channel_multiplier=-1, allow_small_or_imprecise_dtypes=True),
                  deps=tmks[-1:])
        tmks.append(b.dve(lambda e, _a=a: e.tensor_single_scalar(
            st["mk_b"][:, _a * 512:(_a + 1) * 512], miv[:, :], 0.0, ALU.is_ge), [ti]))
    st["tok_mk"] = tmks[-1:]

    st["iota8"] = nc.alloc_sbuf_tensor("iota8", [128, 8], F32)
    st["tok_iota8"] = b.gp(lambda e: e.iota(st["iota8"][:, :], pattern=[[1, 8]], base=0,
                                            channel_multiplier=0, allow_small_or_imprecise_dtypes=True))

    st["ones128_f"] = nc.alloc_sbuf_tensor("ones128_f", [128, 128], F32)
    st["tok_ones128"] = b.gp(lambda e: e.memset(st["ones128_f"][:, :], 1.0))
    st["ones_row_f"] = nc.alloc_sbuf_tensor("ones_row_f", [1, 128], F32)
    st["ones_row_b"] = nc.alloc_sbuf_tensor("ones_row_b", [1, 128], BF16)
    st["ones_col_f"] = nc.alloc_sbuf_tensor("ones_col_f", [128, 1], F32)
    st["ones_col_b"] = nc.alloc_sbuf_tensor("ones_col_b", [128, 1], BF16)
    st["tok_ones"] = [
        b.gp(lambda e: e.memset(st["ones_row_f"][:, :], 1.0)),
        b.gp(lambda e: e.memset(st["ones_row_b"][:, :], 1.0)),
        b.gp(lambda e: e.memset(st["ones_col_f"][:, :], 1.0)),
        b.gp(lambda e: e.memset(st["ones_col_b"][:, :], 1.0)),
    ]

    ps_t = b.psum_f[0]
    brow = b.af32(1024)[0:1, :]
    brow_state = {"tok": None}

    def bias_cols(name, row_ap, F):
        cols = nc.alloc_sbuf_tensor(f"{name}_col", [128, F // 128], F32)
        tev = None
        for c0 in range(0, F, 1024):
            w = min(1024, F - c0)
            tl = b.dma("sync", b.sq[7], lambda e, _c0=c0, _w=w: e.dma_start(
                out=brow[:, 0:_w], in_=row_ap[:, _c0:_c0 + _w]), deps=[brow_state["tok"]])
            for c in range(w // 128):
                cc = c0 // 128 + c
                tt = b.pe(lambda e, _c=c, _cc=cc: e.transpose(
                    ps_t[:, (_cc % 4) * 128:(_cc % 4) * 128 + 1],
                    brow[:, _c * 128:(_c + 1) * 128],
                    st["id_f"][0:1, 0:1]),
                    deps=[tl, t1, tev])
                tev = b.dve(lambda e, _cc=cc: e.tensor_copy(
                    cols[:, _cc:_cc + 1], ps_t[:, (_cc % 4) * 128:(_cc % 4) * 128 + 1]), [tt])
            brow_state["tok"] = tev
        return cols, tev

    if "L1_attn_b_qkv" in W:
        st["bv_row"] = nc.alloc_sbuf_tensor("bv_row", [1, 1024], F32)
        tr0 = b.dma("sync", b.sq[7], lambda e: e.dma_start(
            out=st["bv_row"][:, :], in_=W["L1_attn_b_qkv"][None, 2048:3072]))
        st["bv_row_b"] = nc.alloc_sbuf_tensor("bv_row_b", [1, 1024], BF16)
        st["tok_bvb"] = b.dve(lambda e: e.tensor_copy(st["bv_row_b"][:, :], st["bv_row"][:, :]), [tr0])
        st["bq_col"], tq = bias_cols("bq", W["L1_attn_b_qkv"][None, 0:1024], 1024)
        st["bk_col"], tk = bias_cols("bk", W["L1_attn_b_qkv"][None, 1024:2048], 1024)
        st["bg_col"], tg = bias_cols("bg", W["L1_attn_b_gate"][None, :], 1024)
        st["b_proj_row"] = nc.alloc_sbuf_tensor("b_proj_row", [1, 1024], F32)
        tr1 = b.dma("sync", b.sq[7], lambda e: e.dma_start(
            out=st["b_proj_row"][:, :], in_=W["L1_attn_b_proj"][None, :]))
        st["lng_col"], tlg = bias_cols("lng", W["L1_attn_ln_g"][None, :], 1024)
        st["lnb_col"], tlb = bias_cols("lnb", W["L1_attn_ln_b"][None, :], 1024)
        st["nbq_col"] = nc.alloc_sbuf_tensor("nbq_col", [128, 8], F32)
        tnq = b.dve(lambda e: e.tensor_scalar_mul(st["nbq_col"][:, :], st["bq_col"][:, :], -1.0), [tq])
        st["tok_l1attn_bias"] = [tq, tk, tg, tlg, tlb, tr0, tr1, tnq]
    if "L1_ffn_b1" in W:
        st["l1b1_col"], tb1 = bias_cols("l1b1", W["L1_ffn_b1"][None, :], 4096)
        st["l1b2_row"] = nc.alloc_sbuf_tensor("l1b2_row", [1, 1024], F32)
        tr2 = b.dma("sync", b.sq[7], lambda e: e.dma_start(
            out=st["l1b2_row"][:, :], in_=W["L1_ffn_b2"][None, :]))
        st["scale_t"] = nc.alloc_sbuf_tensor("scale_t", [1, 1], F32)
        tr3 = b.dma("sync", b.sq[7], lambda e: e.dma_start(
            out=st["scale_t"][:, :], in_=W["L1_ffn_scale"][None, :]))
        st["tok_l1ffn_bias"] = [tb1, tr2, tr3]
    if "L0_ffn_b1" in W:
        st["l0b1_col"], tb = bias_cols("l0b1", W["L0_ffn_b1"].rearrange("e h -> (e h)")[None, :], 8 * 2048)
        st["tok_l0ffn_bias"] = [tb]

    if "L0_ffn_wg" in W:
        wgt = nc.alloc_sbuf_tensor("wgt", [8, 16], F32)
        wg_ss = nc.alloc_sbuf_tensor("wg_ss", [8, 1], F32)
        wg_sq = nc.alloc_sbuf_tensor("wg_sq", [8, 16], F32)
        st["wg_nT"] = nc.alloc_sbuf_tensor("wg_nT", [16, 8], F32)
        st["w_red"] = nc.alloc_sbuf_tensor("w_red", [128, 8 * 16], F32)
        tw = b.dma("sync", b.sq[7], lambda e: e.dma_start(out=wgt[:, :], in_=W["L0_ffn_wg"][:, :]))
        twr = None
        for c in range(8):
            twr = b.dma("sync", b.sq[7], lambda e, _c=c: e.dma_start(
                out=st["w_red"][:, _c * 16:(_c + 1) * 16],
                in_=W["L0_ffn_w_red"][_c * 128:(_c + 1) * 128, :]))
        t_sq = b.act(lambda e: e.activation(wg_sq[:, :], wgt[:, :], AF.Square,
                                            accum_out=wg_ss[:, :]), [tw, st["tok_cst"]])
        t_ln = b.act(lambda e: e.activation(wg_ss[:, :], wg_ss[:, :], AF.Ln), [t_sq])
        t_ex = b.act(lambda e: e.activation(wg_ss[:, :], wg_ss[:, :], AF.Exp, scale=-0.5), [t_ln])
        t_cl = b.dve(lambda e: e.tensor_scalar_min(wg_ss[:, :], wg_ss[:, :], 1e4), [t_ex])
        t_mu = b.dve(lambda e: e.tensor_scalar(wgt[:, :], wgt[:, :], wg_ss[:, :], None, ALU.mult), [t_cl])
        t_tr = b.pe(lambda e: e.transpose(ps_t[0:16, 0:8], wgt[:, :], st["id_f"][0:8, 0:8]),
                    [t_mu, t1])
        tok_wg = b.dve(lambda e: e.tensor_copy(st["wg_nT"][:, :], ps_t[0:16, 0:8]), [t_tr])
        st["wg_nT_b"] = nc.alloc_sbuf_tensor("wg_nT_b", [16, 8], BF16)
        st["w_red_b"] = nc.alloc_sbuf_tensor("w_red_b", [128, 8 * 16], BF16)
        st["tok_wg"] = b.dve(lambda e: e.tensor_copy(st["wg_nT_b"][:, :], st["wg_nT"][:, :]), [tok_wg])
        st["tok_wred"] = b.dve(lambda e: e.tensor_copy(st["w_red_b"][:, :], st["w_red"][:, :]), [twr])

    b.st = st
    return st



class TStep:

    def __init__(self, b, name, dtype, share=None):
        nc = b.nc
        self.b = b
        self.dtype = dtype
        if share is None:
            self.xt = [nc.alloc_sbuf_tensor(f"{name}_xt{i}", [128, 1024], F32) for i in range(2)]
            self.sqd = nc.alloc_sbuf_tensor(f"{name}_sqd", [128, 1024], F32)
        else:
            self.xt = share.xt
            self.sqd = share.sqd
        self.xn = [nc.alloc_sbuf_tensor(f"{name}_xn{i}", [128, 1024], dtype) for i in range(2)]
        self.stats = [nc.alloc_sbuf_tensor(f"{name}_st{i}", [128, 8], F32) for i in range(2)]
        self.xt_done = [None, None]
        self.xn_done = [None, None]
        self.st_done = [(), ()]

    def tile(self, x_dram, tok0, slot, extra_x_deps=(), ln2=False):
        b = self.b
        xt = self.xt[slot]
        xn = self.xn[slot]
        s = self.stats[slot]
        prev = list(self.st_done[slot])
        tl = b.dma("sync", b.sq[slot], lambda e: e.dma_start(
            out=xt[:, :], in_=x_dram[tok0:tok0 + 128, :]),
            deps=list(extra_x_deps) + [self.xt_done[slot]])
        t_sq = b.act(lambda e: e.activation(self.sqd[:, :], xt[:, :], AF.Square,
                                            accum_out=s[:, 0:1]), [tl] + prev)
        t_s = b.dve(lambda e: e.reduce_sum(s[:, 1:2], xt[:, :], axis=AX.X), [tl] + prev)
        t_m = b.dve(lambda e: e.tensor_scalar_mul(s[:, 2:3], s[:, 1:2], 1.0 / 1024.0), [t_s])
        t_e = b.dve(lambda e: e.tensor_scalar_mul(s[:, 3:4], s[:, 0:1], 1.0 / 1024.0), [t_sq])
        t_ms = b.dve(lambda e: e.tensor_mul(s[:, 4:5], s[:, 2:3], s[:, 2:3]), [t_m])
        t_v = b.dve(lambda e: e.tensor_sub(s[:, 5:6], s[:, 3:4], s[:, 4:5]), [t_e, t_ms])
        if ln2:
            t_ln = b.act(lambda e: e.activation(s[:, 6:7], s[:, 5:6], AF.Ln,
                                                bias=1e-10, scale=1.0 + LN_EPS), [t_v])
        else:
            t_ln = b.act(lambda e: e.activation(s[:, 6:7], s[:, 5:6], AF.Ln, bias=LN_EPS), [t_v])
        t_rs = b.act(lambda e: e.activation(s[:, 7:8], s[:, 6:7], AF.Exp, scale=-0.5), [t_ln])
        t_nmr = b.dve(lambda e: e.tensor_scalar(s[:, 4:5], s[:, 2:3], s[:, 7:8], -1.0,
                                                ALU.mult, ALU.mult), [t_rs, t_m])
        t_xn = b.act(lambda e: e.activation(xn[:, :], xt[:, :], AF.Identity,
                                            scale=s[:, 7:8], bias=s[:, 4:5]),
                     [t_nmr, tl, self.xn_done[slot]])
        self.st_done[slot] = (t_xn, t_nmr)
        return xn, t_xn, xt, tl

    def transpose_to(self, xn_ap, t_xn, dst_view, bank_done, extra=(), gain=None, beta=None, pb=0):
        b = self.b
        st = b.st
        if self.dtype == BF16:
            bank = b.psum_b[pb]
            ident = st["id_b"]
            tt = None
            for c in range(8):
                tt = b.pe(lambda e, _c=c: e.transpose(
                    bank[:, _c * 128:(_c + 1) * 128], xn_ap[:, _c * 128:(_c + 1) * 128],
                    ident[:, :]), deps=[t_xn, bank_done, b.pbd[pb]] + list(extra), inc=(c == 7))
            if gain is None:
                tev = b.dve(lambda e: e.tensor_copy(
                    dst_view, bank[:, :].rearrange("p (c t) -> p c t", c=8)), [tt])
            else:
                tev = None
                for c in range(8):
                    tev = b.dve(lambda e, _c=c: e.tensor_scalar(
                        dst_view[:, _c, :], bank[:, _c * 128:(_c + 1) * 128],
                        gain[:, _c:_c + 1], beta[:, _c:_c + 1], ALU.mult, ALU.add),
                        [tt])
            b.pbd[pb] = tev
            return tev
        else:
            ident = st["id_f"]
            banks = (b.psum_f[4], b.psum_f[5])
            tev = None
            tts = []
            for half in range(2):
                bank = banks[half]
                tt = None
                for c in range(4):
                    cc = half * 4 + c
                    tt = b.pe(lambda e, _cc=cc, _c=c, _bank=bank: e.transpose(
                        _bank[:, _c * 128:(_c + 1) * 128], xn_ap[:, _cc * 128:(_cc + 1) * 128],
                        ident[:, :]), deps=[t_xn, bank_done] + list(extra), inc=(c == 3))
                tts.append(tt)
            tev = b.dve(lambda e: e.tensor_copy(
                dst_view[:, 0:4, :], banks[0][:, :].rearrange("p (c t) -> p c t", c=4)), [tts[0]])
            tev2 = b.dve(lambda e: e.tensor_copy(
                dst_view[:, 4:8, :], banks[1][:, :].rearrange("p (c t) -> p c t", c=4)), [tts[1]])
            return tev2



def emit_moe(b, x_in, x_out, W, scratch, ph):
    nc = b.nc
    st = b.st
    cfg = b.cfg
    NG, NT, CAP = cfg["NG"], cfg["NT"], cfg["CAP"]
    CG = cfg["CGRP"]
    NS = 8 * CAP
    xn_sorted = scratch["xn_sorted"]
    osort = scratch["osort"]
    if not hasattr(b, "moe_shared"):
        b.moe_shared = {
            "mi8": nc.alloc_sbuf_tensor("mi8_sh", [128, 16], U32),
            "slots": nc.alloc_sbuf_tensor("slots_sh", [128, NT], I32),
            "bnd": nc.gpsimd.alloc_register("moebnd"),
        }
        b.gp(lambda e: e.reg_mov(b.moe_shared["bnd"], NS - 1))

    b.arena_reset()
    xnf2 = [b.abf(8 * 512) for _ in range(2)]
    xredT = b.abf(512)
    xnbf = [b.abf(1024) for _ in range(4)]
    oh_sb = b.af32(4 * 8)
    sc_half = b.af32(NT)
    esb2 = [b.af32(8) for _ in range(2)]
    mx82 = [b.af32(8) for _ in range(2)]
    tmp82 = [b.af32(8) for _ in range(2)]
    scol2 = [b.af32(8) for _ in range(2)]
    b2row = b.af32(1024)[0:1, :]
    mi8 = b.moe_shared["mi8"]
    slots = b.moe_shared["slots"]
    base_sb = b.af32(8)[0:1, :]
    zero_sb = b.af32(8)[0:1, :]
    t_z = b.gp(lambda e: e.memset(zero_sb[:, :], 0.0))

    ts = b.ts_bf
    resx = [ts.xt[0], ts.xt[1], b.af32(1024)]
    outr = [b.af32(1024) for _ in range(3)]
    pre_deps = [st["tok_wg"], st["tok_wred"], st["tok_iota8"], st["tok_cst"],
                st["tok_ones"][-1], st["tok_l0ffn_bias"][-1]] + st["tok_id"]

    bank_g = {0: [], (1, 0): [], (1, 1): [], (2, 0): [], (2, 1): []}
    xnf_done = [None, None]
    scatter_toks = []
    scat_prev = [None] * 4
    cnt_prev = None
    cnt_last = None
    base_rd = None
    mi_done = [None, None]

    for g in range(NG):
        gsl = g % 2
        xnf_fm = xnf2[gsl]
        xnf_r = xnf_fm.rearrange("p (c t) -> p c t", c=8)
        tile_toks = []
        for j in range(4):
            gt = g * 4 + j
            slot = gt % 2
            xn, t_xn, xt, tl = ts.tile(x_in, g * 512 + j * 128, slot, extra_x_deps=pre_deps)
            t_bf = b.dve(lambda e, _j=j, _xn=xn: e.tensor_copy(xnbf[_j][:, :], _xn[:, :]),
                         [t_xn, scat_prev[j]])
            tev = ts.transpose_to(xn, t_xn, xnf_r[:, :, j * 128:(j + 1) * 128],
                                  None, extra=[xnf_done[gsl]], pb=j % 2)
            ts.xt_done[slot] = t_xn
            ts.xn_done[slot] = tev
            tile_toks.append((gt, slot, t_bf, tev))
        t_fm = tile_toks[-1][3]

        pxr = b.psum_f[0]
        tm = None
        for c in range(8):
            tm = b.pe(lambda e, _c=c, _x=xnf_fm: e.matmul(pxr[0:16, :], st["w_red_b"][:, _c * 16:(_c + 1) * 16],
                                                          _x[:, _c * 512:(_c + 1) * 512],
                                                          start=(_c == 0), stop=(_c == 7)),
                      deps=[t_fm, st["tok_wred"], st["tok_wg"], st["tok_ones128"]] + bank_g[0],
                      inc=(c == 7))
        t_xr = b.dve(lambda e: e.tensor_copy(xredT[0:16, :], pxr[0:16, :]), [tm])
        bank_g[0] = [t_xr]
        xnf_done[gsl] = t_xr

        if g == 0:
            t_base = None
        else:
            t_base = b.dve(lambda e: e.tensor_copy(base_sb[0:1, :], b.psum_f[3][0:1, 0:8]),
                           [cnt_last, base_rd])
        for (gt, slot, t_bf, tev) in tile_toks:
            j = gt % 4
            jp = j % 2
            esb, mx8, tmp8, scol = esb2[jp], mx82[jp], tmp82[jp], scol2[jp]
            plg = b.psum_f[1] if jp == 0 else b.psum_f[4]
            t_lg = b.pe(lambda e, _j=j, _p=plg: e.matmul(_p[:, 0:8], xredT[0:16, _j * 128:(_j + 1) * 128],
                                                 st["wg_nT_b"][:, :], start=True, stop=True),
                        deps=[t_xr] + bank_g[(1, jp)])
            t_nm = b.dve(lambda e, _p=plg, _s=scol: e.reduce_max(_s[:, 0:1], _p[:, 0:8], axis=AX.X, negate=True), [t_lg])
            t_ex = b.act(lambda e, _p=plg, _s=scol, _e=esb: e.activation(_e[:, :], _p[:, 0:8], AF.Exp,
                                                bias=_s[:, 0:1], accum_out=_s[:, 1:2]), [t_nm, t_lg])
            t_rc = b.dve(lambda e, _s=scol: e.reciprocal(_s[:, 2:3], _s[:, 1:2]), [t_ex])
            t_sh = b.dve(lambda e, _gt=gt, _s=scol: e.tensor_scalar_mul(sc_half[:, _gt:_gt + 1], _s[:, 2:3], 0.5), [t_rc])
            t_mx = b.dve(lambda e, _p=plg, _m=mx8: e.max(_m[:, :], _p[:, 0:8]), [t_lg, mi_done[jp]])
            t_mi = b.dve(lambda e, _p=plg, _m=mx8, _jp=jp: e.max_index(
                mi8[:, _jp * 8:(_jp + 1) * 8], _m[:, :], _p[:, 0:8]), [t_mx])
            t_if = b.dve(lambda e, _s=scol, _jp=jp: e.tensor_copy(_s[:, 3:4], mi8[:, _jp * 8:_jp * 8 + 1]), [t_mi])
            mi_done[jp] = t_if
            bank_g[(1, jp)] = [t_if, t_ex]
            t_oh = b.dve(lambda e, _j=j, _s=scol: e.tensor_scalar(oh_sb[:, _j * 8:(_j + 1) * 8], st["iota8"][:, :],
                                                         _s[:, 3:4], None, ALU.is_equal),
                         [t_if, cnt_prev])
            prk = b.psum_f[2] if jp == 0 else b.psum_f[5]
            tr = None
            for i in range(j + 1):
                lhs = st["ones128_f"] if i < j else st["u01"]
                tr = b.pe(lambda e, _i=i, _j=j, _lhs=lhs, _p=prk, _st=(i == j and g == 0): e.matmul(
                    _p[:, 0:8], _lhs[:, :], oh_sb[:, _i * 8:(_i + 1) * 8],
                    start=(_i == 0), stop=_st),
                    deps=[t_oh] + bank_g[(2, jp)], inc=(i == j and g == 0))
            if g > 0:
                tr = b.pe(lambda e, _p=prk: e.matmul(
                    _p[:, 0:8], st["ones_row_f"][:, :], base_sb[0:1, :],
                    start=False, stop=True), deps=[t_base], inc=True)
            t_rm = b.dve(lambda e, _j=j, _t=tmp8, _p=prk: e.tensor_mul(_t[:, :], _p[:, 0:8], oh_sb[:, _j * 8:(_j + 1) * 8]), [tr])
            t_ra = b.dve(lambda e, _s=scol, _t=tmp8: e.reduce_sum(_s[:, 4:5], _t[:, :], axis=AX.X), [t_rm])
            bank_g[(2, jp)] = [t_rm]
            t_s1 = b.dve(lambda e, _s=scol: e.scalar_tensor_tensor(_s[:, 5:6], _s[:, 3:4], float(CAP),
                                                          _s[:, 4:5], ALU.mult, ALU.add), [t_ra, t_if])
            t_ov = b.dve(lambda e, _s=scol: e.tensor_single_scalar(_s[:, 6:7], _s[:, 4:5], float(CAP), ALU.is_ge), [t_ra])
            t_s2 = b.dve(lambda e, _s=scol: e.scalar_tensor_tensor(_s[:, 7:8], _s[:, 6:7], 1e9,
                                                          _s[:, 5:6], ALU.mult, ALU.add), [t_ov, t_s1])
            t_si = b.dve(lambda e, _gt=gt, _s=scol: e.tensor_copy(slots[:, _gt:_gt + 1], _s[:, 7:8]), [t_s2])
            t_sc = b.op("gpsimd", lambda e, _gt=gt, _j=j: e.indirect_dma_start(
                out=xn_sorted[:, :], out_offset=IndirectOffsetOnAxis(slots[:, _gt:_gt + 1], 0),
                in_=xnbf[_j][:, :], in_offset=None,
                bounds_check=b.moe_shared["bnd"], oob_is_err=False),
                deps=[t_si, t_bf], inc=b.gq[0], inc_by=16)
            scatter_toks.append(t_sc)
        b.drain_casts(1)
        for j in range(4):
            scat_prev[j] = b.gq[0].tok()
        if g < NG - 1:
            for j in range(4):
                cnt_last = b.pe(lambda e, _j=j, _g=g, _st=(g == 0 and j == 3): e.matmul(
                    b.psum_f[3][0:1, 0:8], st["ones_col_f"][:, :], oh_sb[:, _j * 8:(_j + 1) * 8],
                    start=(_j == 0), stop=_st),
                    deps=[t_base] if (j == 0 and g > 0) else [], inc=(g == 0 and j == 3))
            if g > 0:
                cnt_last = b.pe(lambda e: e.matmul(
                    b.psum_f[3][0:1, 0:8], st["ones_row_f"][0:1, 0:1], base_sb[0:1, :],
                    start=False, stop=True), deps=[t_base], inc=True)
                base_rd = cnt_last
        cnt_prev = cnt_last

    b.drain_casts(len(b.cast_sched))
    b.fence()

    b.ar_off["bf"] = 0
    w1sb = b.abf(8 * 2048)
    w2sb = b.abf(16 * 1024)
    hs_fm2 = [b.abf(16 * 512) for _ in range(2)]
    xs_fm2 = [b.abf(8 * 512) for _ in range(2)]
    xs_tm = [b.abf(1024) for _ in range(2)]
    og_tm = [b.abf(1024) for _ in range(3)]
    bank_tr = [None, None]
    NSG = CAP // 512
    w1_free = None
    w2_free = None
    b2_free = None
    bank_h = {0: None, 1: None, 2: None}
    bank_o = {3: None, 4: None, 5: None}
    xsf_done = [None, None]
    og_store_prev = [None, None, None]
    ostore_toks = []
    wload = {}
    w1b = scratch["w1e_b"]
    w2b = scratch["w2e_b"]
    def load_w1(e):
        tw1 = None
        for dc in range(0, 8, 2):
            tw1 = b.dma("sync", b.sq[5], lambda e_, _dc=dc, _e=e: e_.dma_start(
                out=w1sb[:, _dc * 2048:(_dc + 2) * 2048].rearrange("p (c k) -> p c k", c=2),
                in_=w1b.rearrange("e (c p) h -> e p c h", p=128)[_e, :, _dc:_dc + 2, :]),
                deps=[w1_free, b.wcast.get(f"w1_{e}")])
        return tw1
    def load_w2(e):
        tw2 = None
        for hc in range(0, 16, 4):
            tw2 = b.dma("sync", b.sq[6], lambda e_, _hc=hc, _e=e: e_.dma_start(
                out=w2sb[:, _hc * 1024:(_hc + 4) * 1024].rearrange("p (c k) -> p c k", c=4),
                in_=w2b.rearrange("e (c p) o -> e p c o", p=128)[_e, :, _hc:_hc + 4, :]),
                deps=[w2_free, b.wcast.get(f"w2_{e}")])
        tb2 = b.dma("sync", b.sq[4], lambda e_, _e=e: e_.dma_start(
            out=b2row[:, :], in_=W["L0_ffn_b2"][_e:_e + 1, :]), deps=[b2_free])
        return tw2, tb2
    wload[0] = (load_w1(0), load_w2(0))
    pairs = []
    sg0 = 0
    while sg0 < NSG:
        pairs.append(tuple(range(sg0, min(sg0 + 2, NSG))))
        sg0 += 2
    for e in range(8):
        tw1, (tw2, tb2) = wload[e]
        for ip, pair in enumerate(pairs):
            t_xs = {}
            for a, sg in enumerate(pair):
                row0 = e * CAP + sg * 512
                xs_fm = xs_fm2[a]
                xs_r = xs_fm.rearrange("p (c t) -> p c t", c=8)
                for t in range(4):
                    slot = t % 2
                    tl = b.dma("sync", b.sq[2 + slot], lambda e_, _r=row0 + t * 128, _s=slot: e_.dma_start(
                        out=xs_tm[_s][:, :], in_=xn_sorted[_r:_r + 128, :]), deps=[bank_tr[t % 2]])
                    tt = None
                    for c in range(8):
                        tt = b.pe(lambda e_, _c=c, _s=slot, _b=t % 2: e_.transpose(
                            b.psum_b[_b][:, _c * 128:(_c + 1) * 128], xs_tm[_s][:, _c * 128:(_c + 1) * 128],
                            st["id_b"][:, :]), deps=[tl, b.pbd[t % 2], xsf_done[a]], inc=(c == 7))
                    tev = b.dve(lambda e_, _t=t, _b=t % 2, _r=xs_r: e_.tensor_copy(
                        _r[:, :, _t * 128:(_t + 1) * 128],
                        b.psum_b[_b][:, :].rearrange("p (c t) -> p c t", c=8)), [tt])
                    b.pbd[t % 2] = tev
                    bank_tr[t % 2] = tev
                t_xs[a] = tev
            t_h = {}
            tm_h = {}
            for hc in range(16):
                for a, sg in enumerate(pair):
                    pb = (hc * len(pair) + a) % 3
                    tm = None
                    for dc in range(8):
                        tm = b.pe(lambda e_, _dc=dc, _hc=hc, _pb=pb, _a=a: e_.matmul(
                            b.psum_f[_pb][:, :], w1sb[:, _dc * 2048 + _hc * 128:_dc * 2048 + _hc * 128 + 128],
                            xs_fm2[_a][:, _dc * 512:(_dc + 1) * 512],
                            start=(_dc == 0), stop=(_dc == 7)),
                            deps=[t_xs[a], tw1, bank_h[pb]], inc=(dc == 7))
                    tg = b.act(lambda e_, _hc=hc, _e=e, _pb=pb, _a=a: e_.activation(
                        hs_fm2[_a][:, _hc * 512:(_hc + 1) * 512], b.psum_f[_pb][:, :], AF.Gelu,
                        bias=st["l0b1_col"][:, _e * 16 + _hc:_e * 16 + _hc + 1]), [tm])
                    bank_h[pb] = tg
                    t_h[a] = tg
                    tm_h[a] = tm
            for a, sg in enumerate(pair):
                xsf_done[a] = tm_h[a]
            if ip == len(pairs) - 1:
                w1_free = tm_h[len(pair) - 1]
                if e < 7:
                    next_w1 = load_w1(e + 1)
            for a, sg in enumerate(pair):
                row0 = e * CAP + sg * 512
                for t in range(4):
                    gt2 = a * 4 + t
                    ob0, ob1 = 3 + (2 * gt2) % 3, 3 + (2 * gt2 + 1) % 3
                    tm = None
                    for hc in range(16):
                        for og, ob in ((0, ob0), (1, ob1)):
                            tm = b.pe(lambda e_, _hc=hc, _t=t, _og=og, _ob=ob, _a=a: e_.matmul(
                                b.psum_f[_ob][:, :],
                                hs_fm2[_a][:, _hc * 512 + _t * 128:_hc * 512 + _t * 128 + 128],
                                w2sb[:, _hc * 1024 + _og * 512:_hc * 1024 + _og * 512 + 512],
                                start=(_hc == 0), stop=False),
                                deps=[t_h[a], tw2, bank_o[ob0], bank_o[ob1]] if (hc == 0 and og == 0) else (),
                                inc=False)
                    oevs = []
                    for og, ob in ((0, ob0), (1, ob1)):
                        tm = b.pe(lambda e_, _og=og, _ob=ob: e_.matmul(
                            b.psum_f[_ob][:, :], st["ones_row_f"][:, :],
                            b2row[:, _og * 512:_og * 512 + 512],
                            start=False, stop=True), deps=[tb2, st["tok_ones"][0]], inc=True)
                        oev = b.act(lambda e_, _og=og, _t=t, _s=t % 2, _ob=ob: e_.activation(
                            og_tm[_s][:, _og * 512:(_og + 1) * 512], b.psum_f[_ob][:, :], AF.Copy),
                            [tm, og_store_prev[t % 2]])
                        oevs.append(oev)
                        bank_o[ob] = oev
                    tst = b.dma("scalar", b.aq[t % 2], lambda e_, _r=row0 + t * 128, _s=t % 2: e_.dma_start(
                        out=osort[_r:_r + 128, :], in_=og_tm[_s][:, :]), deps=oevs)
                    ostore_toks.append(tst)
                    og_store_prev[t % 2] = tst
                    if ip == len(pairs) - 1 and a == len(pair) - 1 and t == 3:
                        w2_free = tm
                        b2_free = tm
                        if e < 7:
                            wload[e + 1] = (next_w1, load_w2(e + 1))

    b.fence()

    res_toks = []
    R = 3
    gdone = [None] * R
    stprev = [None] * R
    for gt in range(NT):
        slot = gt % R
        tg = b.dma("gpsimd", b.gq[3 + slot], lambda e_, _gt=gt, _s=slot: e_.indirect_dma_start(
            out=og_tm[_s][:, :], out_offset=None,
            in_=osort[:, :], in_offset=IndirectOffsetOnAxis(slots[:, _gt:_gt + 1], 0),
            bounds_check=b.moe_shared["bnd"], oob_is_err=False), deps=[gdone[slot]])
        tx = b.dma("sync", b.sq[slot], lambda e_, _gt=gt, _s=slot: e_.dma_start(
            out=resx[_s][:, :], in_=x_in[_gt * 128:(_gt + 1) * 128, :]), deps=[gdone[slot]])
        to = b.dve(lambda e_, _gt=gt, _s=slot: e_.scalar_tensor_tensor(
            outr[_s][:, :], og_tm[_s][:, :], sc_half[:, _gt:_gt + 1], resx[_s][:, :],
            ALU.mult, ALU.add), [tg, tx, stprev[slot]])
        gdone[slot] = to
        tst = b.dma("scalar", b.aq[2 + slot % 2], lambda e_, _gt=gt, _s=slot: e_.dma_start(
            out=x_out[_gt * 128:(_gt + 1) * 128, :], in_=outr[_s][:, :]), deps=[to])
        stprev[slot] = tst
        res_toks.append(tst)
    return res_toks



def build_program(cfg, phases=("moe1",)):
    nc = bass.Bass("TRN2", target_bir_lowering=False)
    TOK = cfg["TOK"]
    b = B(nc, cfg)
    b.make_arenas(cfg.get("BF_COLS", 62464), cfg.get("F_COLS", 9216))

    x_ap = nc.dram_tensor("x", [TOK, D], F32, kind="ExternalInput").ap()
    y_ap = nc.dram_tensor("y", [TOK, D], F32, kind="ExternalOutput").ap()

    W = {}
    wspecs = {
        "L0_attn_W_DKV": [D, 256], "L0_attn_W_UK": [256, D], "L0_attn_W_UV": [256, D],
        "L0_attn_W_UQ": [D, D], "L0_attn_W_O": [D, D],
        "L0_ffn_w_red": [D, 16], "L0_ffn_wg": [8, 16],
        "L0_ffn_W1": [8, D, 2048], "L0_ffn_b1": [8, 2048],
        "L0_ffn_W2": [8, 2048, D], "L0_ffn_b2": [8, D],
        "L1_attn_w_qkv": [D, 3 * D], "L1_attn_b_qkv": [3 * D],
        "L1_attn_w_gate": [D, D], "L1_attn_b_gate": [D],
        "L1_attn_w_proj": [D, D], "L1_attn_b_proj": [D],
        "L1_attn_ln_g": [D], "L1_attn_ln_b": [D],
        "L1_ffn_W1": [L1_H, D], "L1_ffn_b1": [L1_H],
        "L1_ffn_W2": [D, L1_H], "L1_ffn_b2": [D],
        "L1_ffn_scale": [1],
    }
    need = set()
    for p in phases:
        if p.startswith("moe"):
            need.update(k for k in wspecs if k.startswith("L0_ffn"))
        elif p == "mla":
            need.update(k for k in wspecs if k.startswith("L0_attn"))
        elif p.startswith("lff"):
            need.update(k for k in wspecs if k.startswith("L1_ffn"))
        elif p == "lat":
            need.update(k for k in wspecs if k.startswith("L1_attn"))
    for k in sorted(need):
        W[k] = nc.dram_tensor(k, wspecs[k], F32, kind="ExternalInput").ap()

    scratch = {}
    NSROW = 8 * cfg["CAP"]
    if any(p.startswith("moe") for p in phases):
        scratch["xn_sorted"] = nc.dram_tensor("xn_sorted", [NSROW, D], BF16, kind="Internal").ap()
        scratch["osort"] = nc.dram_tensor("osort", [NSROW, D], BF16, kind="Internal").ap()
    if any(p.startswith("lff") for p in phases):
        scratch["w1t"] = nc.dram_tensor("w1t_scr", [D, L1_H], BF16, kind="Internal").ap()
        scratch["w2t"] = nc.dram_tensor("w2t_scr", [L1_H, D], BF16, kind="Internal").ap()
    if "mla" in phases:
        scratch["qscr"] = nc.dram_tensor("qscr", [D, TOK], BF16, kind="Internal").ap()
        scratch["wuq_b"] = nc.dram_tensor("wuq_b", [D, D], BF16, kind="Internal").ap()
        scratch["wo_b"] = nc.dram_tensor("wo_b", [D, D], BF16, kind="Internal").ap()
    if "lat" in phases:
        scratch["wqkv_b"] = nc.dram_tensor("wqkv_b", [D, 3 * D], BF16, kind="Internal").ap()
        scratch["wg_b"] = nc.dram_tensor("wg_b", [D, D], BF16, kind="Internal").ap()
        scratch["wp_b"] = nc.dram_tensor("wp_b", [D, D], BF16, kind="Internal").ap()
    if any(p.startswith("moe") for p in phases):
        scratch["w1e_b"] = nc.dram_tensor("w1e_b", [8, D, 2048], BF16, kind="Internal").ap()
        scratch["w2e_b"] = nc.dram_tensor("w2e_b", [8, 2048, D], BF16, kind="Internal").ap()
    xa = nc.dram_tensor("xbufA", [TOK, D], F32, kind="Internal").ap()
    xb_ = nc.dram_tensor("xbufB", [TOK, D], F32, kind="Internal").ap()

    preamble(b, W)
    b.ts_bf = TStep(b, "tsb", BF16)
    if any(p.startswith("lff") for p in phases):
        prep_l1w(b, W, scratch)
    b.fence()

    wc = {}
    b.cast_sched = []
    def cast_group(gname, names, items, chunk=512):
        sem = Sem(nc, f"cast_{gname}")
        n_dma = 0
        closures = []
        for (src_ap, dst_ap, rows) in items:
            for r0 in range(0, rows, chunk):
                r1 = min(rows, r0 + chunk)
                closures.append(lambda _a=src_ap, _d=dst_ap, _r0=r0, _r1=r1, _s=sem: b.op(
                    "gpsimd", lambda e: e.dma_start(out=_d[_r0:_r1, :], in_=_a[_r0:_r1, :]),
                    deps=(), inc=_s, inc_by=16))
                n_dma += 1
        tok = (sem, 16 * n_dma)
        for nm in names:
            wc[nm] = tok
        b.cast_sched.append(closures)
    if any(p.startswith("moe") for p in phases):
        for e_i in range(8):
            cast_group(f"moe{e_i}", [f"w1_{e_i}", f"w2_{e_i}"], [
                (W["L0_ffn_W1"][e_i], scratch["w1e_b"][e_i], D),
                (W["L0_ffn_W2"][e_i], scratch["w2e_b"][e_i], 2048)])
    if "mla" in phases:
        cast_group("mla", ["wuq", "wo"], [
            (W["L0_attn_W_UQ"], scratch["wuq_b"], D),
            (W["L0_attn_W_O"], scratch["wo_b"], D)])
    if "lat" in phases:
        cast_group("lat", ["wqkv", "wg", "wp"], [
            (W["L1_attn_w_qkv"], scratch["wqkv_b"], D),
            (W["L1_attn_w_gate"], scratch["wg_b"], D),
            (W["L1_attn_w_proj"], scratch["wp_b"], D)])
    b.wcast = wc

    def drain_casts(k=1):
        for _ in range(k):
            if b.cast_sched:
                for cl in b.cast_sched.pop(0):
                    cl()
    b.drain_casts = drain_casts
    if not any(p.startswith("moe") for p in phases):
        b.drain_casts(len(b.cast_sched))

    chain = [x_ap]
    bufs = [xa, xb_]
    for i in range(len(phases) - 1):
        chain.append(bufs[i % 2])
    chain.append(y_ap)

    last_toks = None
    for i, p in enumerate(phases):
        x_in, x_out = chain[i], chain[i + 1]
        if p.startswith("moe"):
            last_toks = emit_moe(b, x_in, x_out, W, scratch, ph=p)
        elif p == "mla":
            last_toks = emit_mla(b, x_in, x_out, W, scratch)
        elif p.startswith("lff"):
            last_toks = emit_lff(b, x_in, x_out, W, scratch, ph=p)
        elif p == "lat":
            last_toks = emit_lat(b, x_in, x_out, W, scratch)
        else:
            raise ValueError(p)
        b.fence()

    def final_wait(e, toks=[t for t in (last_toks or [])]):
        for sem, v in toks:
            e.wait_ge(sem.h, v)
    b.streams["sync"].append(final_wait)

    b.assemble()
    nc.finalize()
    return nc


def make_cfg(TOK, cgrp=CGRP):
    NG = TOK // 512
    return dict(TOK=TOK, NG=NG, NT=TOK // 128, CAP=cgrp * NG, CGRP=cgrp,
                NSEQ=max(1, TOK // 2048), SEQ=min(TOK, 2048))


PHASES_ALL = ("moe1", "mla", "moe2", "lff1", "lat", "lff2")


def kernel(**inputs):
    x = np.ascontiguousarray(np.asarray(inputs["L0_attn_x"]))
    Bt, T, Dm = x.shape
    n_cores = 8
    per = Bt // n_cores
    cfg = make_cfg(per * T, cgrp=96)
    nc = build_program(cfg, phases=PHASES_ALL)
    wnames = [k for k in inputs if k != "L0_attn_x"]
    base = {}
    for k in wnames:
        a = np.ascontiguousarray(np.asarray(inputs[k], dtype=np.float32))
        if a.ndim == 0:
            a = a.reshape(1)
        base[k] = a
    in_maps = []
    for c in range(n_cores):
        m = dict(base)
        m["x"] = np.ascontiguousarray(x[c * per:(c + 1) * per].reshape(per * T, Dm))
        in_maps.append(m)
    res = run_bass_kernel_spmd(nc, in_maps, core_ids=list(range(n_cores)))
    out = np.concatenate([r["y"].reshape(per, T, Dm) for r in res.results], axis=0)
    return out.astype(np.float32)



def prep_l1w(b, W, scratch):
    nc = b.nc
    st = b.st
    b.arena_reset()
    wrow = [b.abf(4096) for i in range(2)]
    wcol = [b.abf(4096) for i in range(2)]
    rdone = [None, None]
    cdone = [None, None]
    bank_done = None
    st_last = [None, None]

    def trans_chunks(src_ap, dst_dram, n_in_chunks, in_cols, dst_rows):
        nonlocal rdone, cdone, bank_done
        dstr = dst_dram.rearrange("(c p) o -> p c o", p=128)
        for rc in range(n_in_chunks):
            sl = rc % 2
            tl = b.dma("gpsimd", b.gq[5], lambda e, _rc=rc, _sl=sl, _s=src_ap, _w=in_cols: e.dma_start(
                out=wrow[_sl][:, 0:_w], in_=_s[_rc * 128:(_rc + 1) * 128, :]),
                deps=[rdone[sl]])
            nblk = in_cols // 1024
            csl = rc % 2
            evs = []
            for blk in range(nblk):
                tt = None
                for c in range(8):
                    cc = blk * 8 + c
                    tt = b.pe(lambda e, _cc=cc, _c=c, _sl=sl: e.transpose(
                        b.psum_b[1][:, _c * 128:(_c + 1) * 128],
                        wrow[_sl][:, _cc * 128:(_cc + 1) * 128], st["id_b"][:, :]),
                        deps=[tl, bank_done] + st["tok_id"], inc=(c == 7))
                tev = b.dve(lambda e, _csl=csl, _blk=blk: e.tensor_copy(
                    wcol[_csl][:, _blk * 1024:(_blk + 1) * 1024].rearrange("p (c t) -> p c t", c=8),
                    b.psum_b[1][:, :].rearrange("p (c t) -> p c t", c=8)),
                    [tt, cdone[csl]])
                bank_done = tev
                evs.append(tev)
            tst = b.dma("scalar", b.aq[4 + rc % 2], lambda e, _rc=rc, _csl=csl, _nb=nblk: e.dma_start(
                out=dstr[:, 0:8 * _nb, _rc * 128:(_rc + 1) * 128],
                in_=wcol[_csl][:, 0:1024 * _nb].rearrange("p (c t) -> p c t", c=8 * _nb)),
                deps=evs)
            st_last[rc % 2] = tst
            cdone[csl] = tst
            rdone[sl] = evs[-1]
    trans_chunks(W["L1_ffn_W1"], scratch["w1t"], 32, 1024, None)
    trans_chunks(W["L1_ffn_W2"], scratch["w2t"], 8, 4096, None)
    b.l1w_tok = [t for t in st_last if t is not None]



def emit_lff(b, x_in, x_out, W, scratch, ph):
    nc = b.nc
    st = b.st
    cfg = b.cfg
    NG = cfg["NG"]
    SEQ = cfg["SEQ"]
    GPS = SEQ // 512
    w1t, w2t = scratch["w1t"], scratch["w2t"]

    b.arena_reset()
    w1win = [b.abf(8 * 1024) for _ in range(2)]
    w2win = [b.abf(8 * 1024) for _ in range(2)]
    hs = [b.abf(8 * 512) for _ in range(2)]
    xnf = [b.abf(8 * 512) for _ in range(2)]
    evo = b.abf(128)
    od3p = b.abf(8 * 4)
    od4p = b.abf(8 * 4)
    m2a = b.abf(4 * 1024)
    m2c = m2a[0:4, :]
    w4a = b.abf(4096)
    w4blk = w4a[0:4, :]
    erow = w4a[32:33, 0:1024]
    erow2 = w4a[64:65, 0:1024]
    acc = b.af32(4 * 1024)
    t1a = b.abf(512)
    t1all = t1a[0:4, :]
    t2a = b.abf(4 * 512)
    t2sb = t2a[0:4, :]
    resx = [b.af32(1024) for _ in range(2)]

    ts = b.ts_bf
    pre = [st["tok_cst"], st["tok_l1ffn_bias"][0], st["tok_l1ffn_bias"][1],
           st["tok_l1ffn_bias"][2], st["tok_ones"][0]] + st["tok_id"]

    xnf_done = [[], []]
    seq_tok = {}
    bank = {"tr": None, 0: None, 1: None, 2: None, 3: None, 4: None, 5: None}
    w1done = [None, None]
    w2done = [None, None]
    hs_done = [None, None]
    acc_done = [None] * 4
    rx_done = [None, None]
    res_toks = []

    for g in range(NG):
        gsl = g % 2
        xnf_g = xnf[gsl]
        xnf_r = xnf_g.rearrange("p (c t) -> p c t", c=8)
        for j in range(4):
            slot = j % 2
            xn, t_xn, xt, tl = ts.tile(x_in, g * 512 + j * 128, slot, extra_x_deps=pre)
            tev = ts.transpose_to(xn, t_xn, xnf_r[:, :, j * 128:(j + 1) * 128],
                                  None, extra=xnf_done[gsl], pb=j % 2)
            ts.xt_done[slot] = t_xn
            ts.xn_done[slot] = tev
            t_fm = tev

        if g % GPS == 0:
            prev_done = res_toks[-1] if res_toks else None
            t_evo = b.dve(lambda e, _x=xnf_r: e.tensor_copy(
                evo[:, :].rearrange("p (c k) -> p c k", c=8), _x[:, :, 1:17]), [t_fm, prev_done])
            t_o3 = b.dve(lambda e: e.tensor_copy(
                od3p[:, :].rearrange("p (c k) -> p c k", c=8),
                evo[:, :].rearrange("p (c k) -> p c k", c=8)[:, :, 1:9:2]), [t_evo])
            t_o4 = b.dve(lambda e: e.tensor_copy(
                od4p[:, :].rearrange("p (c k) -> p c k", c=8),
                evo[:, :].rearrange("p (c k) -> p c k", c=8)[:, :, 9:16:2]), [t_evo])
            t_mz = b.gp(lambda e: e.memset(w4a[:, :], 0.0), [prev_done])
            t_mz2 = b.gp(lambda e: e.memset(m2a[:, :], 0.0), [prev_done])
            t_ev_prev = None
            w4tok = None
            m2tok = None
            for e_i in range(4):
                for colk, dst_is_w4 in [(2 * e_i, True), (8 + 2 * e_i, False)]:
                    tt = None
                    for c in range(8):
                        tt = b.pe(lambda e, _c=c, _k=colk: e.transpose(
                            b.psum_b[1][0:1, _c * 128:(_c + 1) * 128],
                            evo[:, :].rearrange("p (c k) -> p c k", c=8)[:, _c, _k:_k + 1],
                            st["id_b"][:, :]), deps=[t_evo, t_ev_prev, b.pbd[1]], inc=(c == 7))
                    t_er = b.dve(lambda e: e.tensor_single_scalar(
                        erow[:, :], b.psum_b[1][0:1, 0:1024], st["scale_t"][0:1, 0:1], ALU.mult),
                        [tt, t_ev_prev, t_mz])
                    b.pbd[1] = t_er
                    t_ev_prev = t_er
                    if dst_is_w4:
                        w4tok = b.dma("gpsimd", b.gq[5], lambda e, _e=e_i: e.dma_start(
                            out=w4blk[_e:_e + 1, _e * 1024:(_e + 1) * 1024], in_=erow[:, :]),
                            deps=[t_er, t_mz])
                        t_ev_prev = w4tok
                    else:
                        t_es = b.dve(lambda e: e.tensor_copy(
                            erow2[:, :],
                            erow[:, :].rearrange("p (k c) -> p c k", c=4)), [t_er, t_mz2])
                        for cc in range(4):
                            m2tok = b.dma("gpsimd", b.gq[5], lambda e, _e=e_i, _cc=cc: e.dma_start(
                                out=m2c[_e:_e + 1, _cc * 1024 + _e * 256:_cc * 1024 + (_e + 1) * 256],
                                in_=erow2[:, _cc * 256:(_cc + 1) * 256]), deps=[t_es, t_mz2])
                        t_ev_prev = m2tok
            seq_tok = {"w4": w4tok, "m2": m2tok, "o3": t_o3, "o4": t_o4}

        tm = None
        for dc in range(8):
            tm = b.pe(lambda e, _dc=dc, _x=xnf_g: e.matmul(
                b.psum_f[2][0:4, :], od3p[:, _dc * 4:(_dc + 1) * 4], _x[:, _dc * 512:(_dc + 1) * 512],
                start=(_dc == 0), stop=(_dc == 7)),
                deps=[t_fm, seq_tok["o3"], bank[2]], inc=(dc == 7))
        t_t1 = b.dve(lambda e: e.tensor_copy(t1all[:, :], b.psum_f[2][0:4, :]), [tm])
        bank[2] = t_t1

        for blk in range(4):
            wsl = blk % 2
            tw1 = b.dma("sync", b.sq[4 + wsl], lambda e, _blk=blk, _w=wsl: e.dma_start(
                out=w1win[_w][:, :].rearrange("p (c h) -> p c h", c=8),
                in_=w1t.rearrange("(c p) h -> p c h", p=128)[:, :, _blk * 1024:(_blk + 1) * 1024]),
                deps=[w1done[wsl]] + b.l1w_tok)
            t_h = None
            hsb = hs[wsl]
            for hl in range(8):
                hc = blk * 8 + hl
                pb = hl % 2
                for dc in range(8):
                    b.pe(lambda e, _dc=dc, _hl=hl, _pb=pb, _x=xnf_g, _w=wsl: e.matmul(
                        b.psum_f[_pb][:, :],
                        w1win[_w][:, _dc * 1024 + _hl * 128:_dc * 1024 + _hl * 128 + 128],
                        _x[:, _dc * 512:(_dc + 1) * 512],
                        start=(_dc == 0), stop=False),
                        deps=[t_fm, tw1, bank[pb]] if dc == 0 else (), inc=False)
                tm = b.pe(lambda e, _hc=hc, _pb=pb: e.matmul(
                    b.psum_f[_pb][:, :], w4blk[:, _hc * 128:(_hc + 1) * 128], t1all[:, :],
                    start=False, stop=True), deps=[seq_tok["w4"], t_t1], inc=True)
                if hl == 7:
                    w1done[wsl] = tm
                t_h = b.act(lambda e, _hc=hc, _hl=hl, _pb=pb, _hs=hsb: e.activation(
                    _hs[:, _hl * 512:(_hl + 1) * 512], b.psum_f[_pb][:, :], AF.Gelu,
                    bias=st["l1b1_col"][:, _hc:_hc + 1]), [tm, hs_done[wsl]])
                bank[pb] = t_h

            tm = None
            for jc in range(8):
                tm = b.pe(lambda e, _jc=jc, _hs=hsb: e.matmul(
                    b.psum_f[2][0:4, :], od4p[:, _jc * 4:(_jc + 1) * 4],
                    _hs[:, _jc * 512:(_jc + 1) * 512],
                    start=(_jc == 0), stop=(_jc == 7)),
                    deps=[t_h, seq_tok["o4"], bank[2]], inc=(jc == 7))
            tt2 = b.dve(lambda e, _blk=blk: e.tensor_copy(
                t2sb[0:4, _blk * 512:(_blk + 1) * 512], b.psum_f[2][0:4, :]), [tm])
            bank[2] = tt2
            if blk == 3:
                t_t2 = tt2

            tw2 = b.dma("sync", b.sq[6], lambda e, _blk=blk, _w=wsl: e.dma_start(
                out=w2win[_w][:, :].rearrange("p (c o) -> p c o", c=8),
                in_=w2t.rearrange("(c p) o -> p c o", p=128)[:, _blk * 8:(_blk + 1) * 8, :]),
                deps=[w2done[wsl]] + b.l1w_tok)
            w2rd = None
            hs_rd = None
            for t in range(4):
                tsl = t % 2
                if blk == 3:
                    trx = b.dma("sync", b.sq[tsl], lambda e, _g=g, _t=t, _s=tsl: e.dma_start(
                        out=resx[_s][:, :], in_=x_in[_g * 512 + _t * 128:_g * 512 + (_t + 1) * 128, :]),
                        deps=[rx_done[tsl]])
                tms = {}
                ob0 = 3 + (2 * t) % 3
                ob1 = 3 + (2 * t + 1) % 3
                for hl in range(8):
                    for og in range(2):
                        ob = ob0 if og == 0 else ob1
                        tms[og] = b.pe(lambda e, _hl=hl, _t=t, _og=og, _w=wsl, _ob=ob, _hs=hsb,
                                       _st=(blk < 3 and hl == 7): e.matmul(
                            b.psum_f[_ob][:, :],
                            _hs[:, _hl * 512 + _t * 128:_hl * 512 + _t * 128 + 128],
                            w2win[_w][:, _hl * 1024 + _og * 512:_hl * 1024 + _og * 512 + 512],
                            start=(_hl == 0), stop=_st),
                            deps=[t_h, tw2, bank[ob0], bank[ob1]] if (hl == 0 and og == 0) else (),
                            inc=(blk < 3 and hl == 7))
                for og in range(2):
                    ob = ob0 if og == 0 else ob1
                    tm = tms[og]
                    w2rd = tm
                    hs_rd = tm
                    if blk == 3:
                        for cq in range(4):
                            b.pe(lambda e, _cq=cq, _t=t, _og=og, _ob=ob: e.matmul(
                                b.psum_f[_ob][:, :], t2sb[0:4, _cq * 512 + _t * 128:_cq * 512 + _t * 128 + 128],
                                m2c[:, _cq * 1024 + _og * 512:_cq * 1024 + _og * 512 + 512],
                                start=False, stop=False), deps=[t_t2, seq_tok["m2"]] if cq == 0 else (),
                                inc=False)
                        tm = b.pe(lambda e, _og=og, _ob=ob: e.matmul(
                            b.psum_f[_ob][:, :], st["ones_row_f"][:, :],
                            st["l1b2_row"][:, _og * 512:(_og + 1) * 512],
                            start=False, stop=True), deps=[], inc=True)
                        ta = b.dve(lambda e, _t=t, _og=og, _ob=ob: e.tensor_add(
                            acc[:, _t * 1024 + _og * 512:_t * 1024 + (_og + 1) * 512],
                            acc[:, _t * 1024 + _og * 512:_t * 1024 + (_og + 1) * 512],
                            b.psum_f[_ob][:, :]), [tm])
                        to = b.dve(lambda e, _t=t, _og=og, _s=tsl: e.scalar_tensor_tensor(
                            resx[_s][:, _og * 512:(_og + 1) * 512],
                            acc[:, _t * 1024 + _og * 512:_t * 1024 + (_og + 1) * 512], 0.5,
                            resx[_s][:, _og * 512:(_og + 1) * 512],
                            ALU.mult, ALU.add), [ta, trx])
                        bank[ob] = to
                        acc_done[t] = to
                        if og == 1:
                            tst = b.dma("scalar", b.aq[tsl], lambda e, _g=g, _t=t, _s=tsl: e.dma_start(
                                out=x_out[_g * 512 + _t * 128:_g * 512 + (_t + 1) * 128, :],
                                in_=resx[_s][:, :]), deps=[to])
                            rx_done[tsl] = tst
                            res_toks.append(tst)
                    else:
                        if blk == 0:
                            ta = b.dve(lambda e, _t=t, _og=og, _ob=ob: e.tensor_copy(
                                acc[:, _t * 1024 + _og * 512:_t * 1024 + (_og + 1) * 512],
                                b.psum_f[_ob][:, :]), [tm, acc_done[t]])
                        else:
                            ta = b.dve(lambda e, _t=t, _og=og, _ob=ob: e.tensor_add(
                                acc[:, _t * 1024 + _og * 512:_t * 1024 + (_og + 1) * 512],
                                acc[:, _t * 1024 + _og * 512:_t * 1024 + (_og + 1) * 512],
                                b.psum_f[_ob][:, :]), [tm])
                        bank[ob] = ta
            w2done[wsl] = w2rd
            hs_done[wsl] = hs_rd
        xnf_done[gsl] = [t_h, t_t1]
    return res_toks







def emit_lat(b, x_in, x_out, W, scratch):
    nc = b.nc
    st = b.st
    cfg = b.cfg
    NG = cfg["NG"]
    GPS = cfg["SEQ"] // 512

    b.arena_reset()
    xnf = [b.abf(8 * 512) for _ in range(2)]
    q_fm = b.abf(8 * 512)
    k_fm = b.abf(8 * 512)
    v_ext = b.abf(4 * 1032)
    av = b.abf(4 * 512)
    ktm = b.abf(4 * 1024)
    s_ext = b.abf(8 * 1025)
    wqk = [b.abf(8 * 128) for _ in range(2)]
    wv = [b.abf(8 * 512) for _ in range(2)]
    wp = b.abf(8 * 1024)
    nrm = [b.abf(1024) for _ in range(2)]
    nrmf = b.abf(1024)
    tmp1s = [b.abf(512) for _ in range(2)]
    tmp2s = [b.abf(512) for _ in range(2)]
    tmp3s = [b.abf(512) for _ in range(2)]
    gates = [b.abf(512) for _ in range(2)]
    S32 = b.af32(8 * 1024)
    rx = b.af32(1024)
    sc = nc.alloc_sbuf_tensor("lat_sc", [128, 16], F32)
    kcs = nc.alloc_sbuf_tensor("lat_kcs", [128, 8], F32)

    ts = b.ts_bf
    pre = [st["tok_cst"], st["tok_ones"][0], st["tok_ones"][1], st["tok_bvb"]] \
        + st["tok_id"] + st["tok_mk"] + st["tok_l1attn_bias"]
    wq_ap = scratch["wqkv_b"]
    wg_ap = scratch["wg_b"]
    wp_ap = scratch["wp_b"]

    bank = {"tr": None, 0: None, 1: None, 2: None, 3: None, 4: None, 5: None, "b1": None}
    xnf_done = [[], []]
    wqk_done = [None, None]
    wv_done = [None, None]
    wp_done = None
    vext_done = None
    av_done = None
    q_used = None
    k_used = None
    S_done = []
    sext_rd = None
    nrm_done = [None, None]
    nrmf_done = None
    rx_done = None
    tmp_done = {}
    res_toks = []
    wcnt = [0]

    t_vones = b.gp(lambda e: e.memset(
        v_ext[:, :].rearrange("p (t c) -> p t c", t=4)[:, :, 1024:1025], 1.0), pre)
    vones_tok = t_vones

    for c in range(NG):
        cin = c % GPS
        csl = c % 2
        xnf_g = xnf[csl]
        xnf_r = xnf_g.rearrange("p (c t) -> p c t", c=8)
        for j in range(4):
            slot = j % 2
            xn, t_xn, xt, tl = ts.tile(x_in, c * 512 + j * 128, slot, extra_x_deps=pre, ln2=True)
            tev = ts.transpose_to(xn, t_xn, xnf_r[:, :, j * 128:(j + 1) * 128],
                                  None, extra=xnf_done[csl],
                                  gain=st["lng_col"], beta=st["lnb_col"], pb=j % 2)
            ts.xt_done[slot] = t_xn
            ts.xn_done[slot] = tev
            t_fm = tev

        if cin == 0:
            t_s0 = b.dve(lambda e: e.memset(S32[:, :], 0.0), S_done)
            t_k0 = b.dve(lambda e: e.memset(kcs[:, :], 0.0), S_done)
            S_done = [t_s0, t_k0]

        def win_load(src_ap, col0, extra=()):
            wsl = wcnt[0] % 2
            wcnt[0] += 1
            q = ("sync", b.sq[3]) if wsl == 0 else ("scalar", b.aq[5])
            tw = b.dma(q[0], q[1], lambda e, _c0=col0, _w=wsl, _s=src_ap: e.dma_start(
                out=wqk[_w][:, :].rearrange("p (d f) -> p d f", d=8),
                in_=_s.rearrange("(d p) f -> p d f", p=128)[:, :, _c0:_c0 + 128]),
                deps=[wqk_done[wsl], b.wcast.get("wqkv"), b.wcast.get("wg")] + list(extra))
            return tw, wsl

        def fproj(src_ap, col0, pb, extra_mm=()):
            tw, wsl = win_load(src_ap, col0)
            tm = None
            for dc in range(8):
                tm = b.pe(lambda e, _dc=dc, _pb=pb, _w=wsl, _x=xnf_g: e.matmul(
                    b.psum_f[_pb][:, :], wqk[_w][:, _dc * 128:(_dc + 1) * 128],
                    _x[:, _dc * 512:(_dc + 1) * 512],
                    start=(_dc == 0), stop=(_dc == 7)),
                    deps=[t_fm, tw, bank[pb]] + list(extra_mm), inc=(dc == 7))
            wqk_done[wsl] = tm
            return tm

        for fc in range(8):
            fsl = fc % 2
            tmp1, tmp2, tmp3, gate = tmp1s[fsl], tmp2s[fsl], tmp3s[fsl], gates[fsl]
            pb_g = fsl
            pb_k = 2 + fsl
            pb_q = fsl
            tm = fproj(wg_ap, fc * 128, pb_g)
            t_g = b.act(lambda e, _fc=fc, _pb=pb_g, _o=gate: e.activation(
                _o[:, :], b.psum_f[_pb][:, :], AF.Sigmoid,
                bias=st["bg_col"][:, _fc:_fc + 1]), [tm, tmp_done.get(("gate", fsl))])
            bank[pb_g] = t_g
            tm = fproj(wq_ap, 1024 + fc * 128, pb_k)
            t_kg = b.dve(lambda e, _fc=fc, _pb=pb_k, _t1=tmp1, _g=gate: e.scalar_tensor_tensor(
                _t1[:, :], b.psum_f[_pb][:, :], st["bk_col"][:, _fc:_fc + 1], _g[:, :],
                ALU.add, ALU.mult), [tm, t_g, tmp_done.get(("t1", fsl))])
            tmp_done[("gate", fsl)] = t_kg
            t_r1 = b.act(lambda e, _t1=tmp1, _t2=tmp2: e.activation(_t2[:, :], _t1[:, :], AF.Relu,
                                                                    scale=-1.0),
                         [t_kg, tmp_done.get(("t2", fsl))])
            t_e = b.act(lambda e, _t2=tmp2, _t3=tmp3: e.activation(_t3[:, :], _t2[:, :], AF.Exp,
                                                                   scale=-1.0),
                        [t_r1, tmp_done.get(("t3", fsl))])
            t_r2 = b.act(lambda e, _t1=tmp1, _t2=tmp2: e.activation(_t2[:, :], _t1[:, :], AF.Relu),
                         [t_e])
            t_k = b.dve(lambda e, _fc=fc, _t2=tmp2, _t3=tmp3: e.tensor_add(
                k_fm[:, _fc * 512:(_fc + 1) * 512], _t2[:, :], _t3[:, :]), [t_r2, t_e, k_used])
            bank[pb_k] = t_kg
            tm = fproj(wq_ap, fc * 128, pb_q)
            t_r3 = b.act(lambda e, _fc=fc, _pb=pb_q, _t1=tmp1: e.activation(
                _t1[:, :], b.psum_f[_pb][:, :], AF.Relu, scale=-1.0,
                bias=st["nbq_col"][:, _fc:_fc + 1]), [tm, t_k])
            t_e2 = b.act(lambda e, _t1=tmp1, _t3=tmp3: e.activation(_t3[:, :], _t1[:, :], AF.Exp,
                                                                    scale=-1.0),
                         [t_r3, t_k])
            t_r4 = b.act(lambda e, _fc=fc, _pb=pb_q, _t1=tmp1: e.activation(
                _t1[:, :], b.psum_f[_pb][:, :], AF.Relu,
                bias=st["bq_col"][:, _fc:_fc + 1]), [t_e2])
            t_q = b.dve(lambda e, _fc=fc, _t1=tmp1, _t3=tmp3: e.tensor_add(
                q_fm[:, _fc * 512:(_fc + 1) * 512], _t1[:, :], _t3[:, :]), [t_r4, t_e2, q_used])
            bank[pb_q] = t_r4
            tmp_done[("t1", fsl)] = t_q
            tmp_done[("t2", fsl)] = t_k
            tmp_done[("t3", fsl)] = t_q
        t_qk = t_q

        for og in range(2):
            wsl = og
            tw_v = b.dma("scalar", b.aq[4], lambda e, _og=og, _w=wsl: e.dma_start(
                out=wv[_w][:, :].rearrange("p (d f) -> p d f", d=8),
                in_=wq_ap.rearrange("(d p) f -> p d f", p=128)[:, :, 2048 + _og * 512:2048 + (_og + 1) * 512]),
                deps=[wv_done[wsl], b.wcast.get("wqkv")])
            for tt in range(4):
                pb = 2 + tt % 2
                tm = None
                for dc in range(8):
                    tm = b.pe(lambda e, _dc=dc, _tt=tt, _pb=pb, _w=wsl, _x=xnf_g: e.matmul(
                        b.psum_f[_pb][:, :], _x[:, _dc * 512 + _tt * 128:_dc * 512 + _tt * 128 + 128],
                        wv[_w][:, _dc * 512:(_dc + 1) * 512],
                        start=(_dc == 0), stop=False),
                        deps=[t_fm, tw_v, bank[pb]], inc=False)
                tm = b.pe(lambda e, _og=og, _pb=pb: e.matmul(
                    b.psum_f[_pb][:, :], st["ones_row_b"][:, :],
                    st["bv_row_b"][:, _og * 512:(_og + 1) * 512],
                    start=False, stop=True), deps=[], inc=True)
                if tt == 3:
                    wv_done[wsl] = tm
                t_ve = b.act(lambda e, _tt=tt, _og=og, _pb=pb: e.activation(
                    v_ext[:, _tt * 1032 + _og * 512:_tt * 1032 + _og * 512 + 512],
                    b.psum_f[_pb][:, :], AF.Copy), [tm, vext_done, vones_tok])
                bank[pb] = t_ve
        t_v = t_ve

        t_av = []
        for stl in range(4):
            pb = stl % 2
            tm = None
            for dc in range(8):
                tm = b.pe(lambda e, _dc=dc, _stl=stl, _pb=pb: e.matmul(
                    b.psum_f[_pb][:, :], k_fm[:, _dc * 512 + _stl * 128:_dc * 512 + _stl * 128 + 128],
                    q_fm[:, _dc * 512:(_dc + 1) * 512],
                    start=(_dc == 0), stop=(_dc == 7)),
                    deps=[t_qk, bank[pb]], inc=(dc == 7))
            t_c2 = b.dve(lambda e, _stl=stl, _pb=pb: e.tensor_mul(
                av[:, _stl * 512:(_stl + 1) * 512], b.psum_f[_pb][:, :],
                st["mk_b"][:, _stl * 512:(_stl + 1) * 512]), [tm, av_done])
            bank[pb] = t_c2
            t_av.append(t_c2)

        tw_p = None
        for dc in range(0, 8, 4):
            tw_p = b.dma("sync", b.sq[6], lambda e, _dc=dc: e.dma_start(
                out=wp[:, _dc * 1024:(_dc + 4) * 1024].rearrange("p (c k) -> p c k", c=4),
                in_=wp_ap.rearrange("(c p) f -> p c f", p=128)[:, _dc:_dc + 4, :]),
                deps=[wp_done, b.wcast.get("wp")])

        out_mm_toks = []
        for tt in range(4):
            nsl = tt % 2
            groups = []
            tm4 = tm5 = tmd = None
            first = True
            for stl in range(tt + 1):
                lastin = (cin == 0 and stl == tt)
                tm4 = b.pe(lambda e, _stl=stl, _tt=tt, _f=first, _l=lastin: e.matmul(
                    b.psum_f[4][:, :], av[:, _stl * 512 + _tt * 128:_stl * 512 + _tt * 128 + 128],
                    v_ext[:, _stl * 1032:_stl * 1032 + 512],
                    start=_f, stop=_l), deps=[t_av[stl], t_v, bank[4]], inc=lastin)
                tm5 = b.pe(lambda e, _stl=stl, _tt=tt, _f=first, _l=lastin: e.matmul(
                    b.psum_f[5][:, :], av[:, _stl * 512 + _tt * 128:_stl * 512 + _tt * 128 + 128],
                    v_ext[:, _stl * 1032 + 512:_stl * 1032 + 1024],
                    start=_f, stop=_l), deps=[bank[5]], inc=lastin)
                tmd = b.pe(lambda e, _stl=stl, _tt=tt, _f=first, _l=lastin: e.matmul(
                    b.psum_f[3][:, 0:1], av[:, _stl * 512 + _tt * 128:_stl * 512 + _tt * 128 + 128],
                    v_ext[:, _stl * 1032 + 1024:_stl * 1032 + 1025],
                    start=_f, stop=_l), deps=[bank[3]], inc=lastin)
                first = False
            if cin > 0:
                for dc in range(8):
                    last = dc == 7
                    tm4 = b.pe(lambda e, _dc=dc, _tt=tt, _l=last: e.matmul(
                        b.psum_f[4][:, :], q_fm[:, _dc * 512 + _tt * 128:_dc * 512 + _tt * 128 + 128],
                        s_ext[:, _dc * 1025:_dc * 1025 + 512],
                        start=False, stop=_l), deps=[t_qk] + S_done, inc=last)
                    tm5 = b.pe(lambda e, _dc=dc, _tt=tt, _l=last: e.matmul(
                        b.psum_f[5][:, :], q_fm[:, _dc * 512 + _tt * 128:_dc * 512 + _tt * 128 + 128],
                        s_ext[:, _dc * 1025 + 512:_dc * 1025 + 1024],
                        start=False, stop=_l), deps=[], inc=last)
                    tmd = b.pe(lambda e, _dc=dc, _tt=tt, _l=last: e.matmul(
                        b.psum_f[3][:, 0:1], q_fm[:, _dc * 512 + _tt * 128:_dc * 512 + _tt * 128 + 128],
                        s_ext[:, _dc * 1025 + 1024:_dc * 1025 + 1025],
                        start=False, stop=_l), deps=[], inc=last)
            out_mm_toks += [tm4, tm5, tmd]
            t_d1 = b.dve(lambda e: e.tensor_single_scalar(sc[:, 0:1], b.psum_f[3][:, 0:1],
                                                          DEN_EPS, ALU.add), [tmd])
            t_rc = b.dve(lambda e: e.reciprocal(sc[:, 1:2], sc[:, 0:1]), [t_d1])
            t_n0 = b.dve(lambda e, _n=nsl: e.tensor_scalar(
                nrm[_n][:, 0:512], b.psum_f[4][:, :], sc[:, 1:2], None, ALU.mult),
                [t_rc, tm4, nrm_done[nsl]])
            t_n1 = b.dve(lambda e, _n=nsl: e.tensor_scalar(
                nrm[_n][:, 512:1024], b.psum_f[5][:, :], sc[:, 1:2], None, ALU.mult),
                [t_rc, tm5])
            bank[4] = t_n0
            bank[5] = t_n1
            bank[3] = t_d1
            tt_tr = None
            for ch in range(8):
                tt_tr = b.pe(lambda e, _ch=ch, _n=nsl: e.transpose(
                    b.psum_b[1][:, _ch * 128:(_ch + 1) * 128],
                    nrm[_n][:, _ch * 128:(_ch + 1) * 128], st["id_b"][:, :]),
                    deps=[t_n0, t_n1, b.pbd[1]], inc=(ch == 7))
            t_nf = b.dve(lambda e: e.tensor_copy(
                nrmf[:, :].rearrange("p (c t) -> p c t", c=8),
                b.psum_b[1][:, :].rearrange("p (c t) -> p c t", c=8)), [tt_tr, nrmf_done])
            b.pbd[1] = t_nf
            nrm_done[nsl] = t_nf
            trx = b.dma("sync", b.sq[2], lambda e, _c=c, _tt=tt: e.dma_start(
                out=rx[:, :], in_=x_in[_c * 512 + _tt * 128:_c * 512 + (_tt + 1) * 128, :]),
                deps=[rx_done])
            for og in range(2):
                pb = 4 if og == 0 else 5
                tm = None
                for dc in range(8):
                    tm = b.pe(lambda e, _dc=dc, _og=og, _pb=pb: e.matmul(
                        b.psum_f[_pb][:, :], nrmf[:, _dc * 128:(_dc + 1) * 128],
                        wp[:, _dc * 1024 + _og * 512:_dc * 1024 + _og * 512 + 512],
                        start=(_dc == 0), stop=False),
                        deps=[t_nf, tw_p, bank[pb]], inc=False)
                tm = b.pe(lambda e, _og=og, _pb=pb: e.matmul(
                    b.psum_f[_pb][:, :], st["ones_row_f"][:, :],
                    st["b_proj_row"][:, _og * 512:(_og + 1) * 512],
                    start=False, stop=True), deps=[], inc=True)
                wp_done = tm
                t_o = b.dve(lambda e, _og=og, _pb=pb: e.tensor_add(
                    rx[:, _og * 512:(_og + 1) * 512], b.psum_f[_pb][:, :],
                    rx[:, _og * 512:(_og + 1) * 512]), [tm, trx])
                bank[pb] = t_o
            tst = b.dma("scalar", b.aq[tt % 2], lambda e, _c=c, _tt=tt: e.dma_start(
                out=x_out[_c * 512 + _tt * 128:_c * 512 + (_tt + 1) * 128, :], in_=rx[:, :]),
                deps=[bank[4], bank[5]])
            rx_done = tst
            res_toks.append(tst)
            nrmf_done = tm

        if cin < GPS - 1:
            ktm_ev = []
            for stl in range(4):
                tt_tr = None
                for ch in range(8):
                    tt_tr = b.pe(lambda e, _ch=ch, _stl=stl: e.transpose(
                        b.psum_b[1][:, _ch * 128:(_ch + 1) * 128],
                        k_fm[:, _ch * 512 + _stl * 128:_ch * 512 + _stl * 128 + 128],
                        st["id_b"][:, :]), deps=[t_qk, b.pbd[1]], inc=(ch == 7))
                t_ke = b.dve(lambda e, _stl=stl: e.tensor_copy(
                    ktm[:, _stl * 1024:(_stl + 1) * 1024].rearrange("p (c t) -> p c t", c=8),
                    b.psum_b[1][:, :].rearrange("p (c t) -> p c t", c=8)), [tt_tr])
                b.pbd[1] = t_ke
                ktm_ev.append(t_ke)
            new_S_done = []
            for d1c in range(8):
                for og in range(2):
                    pb = 2 + (d1c * 2 + og) % 2
                    tm = None
                    for stl in range(4):
                        tm = b.pe(lambda e, _stl=stl, _d=d1c, _og=og, _pb=pb: e.matmul(
                            b.psum_f[_pb][:, :], ktm[:, _stl * 1024 + _d * 128:_stl * 1024 + _d * 128 + 128],
                            v_ext[:, _stl * 1032 + _og * 512:_stl * 1032 + _og * 512 + 512],
                            start=(_stl == 0), stop=(_stl == 3)),
                            deps=[ktm_ev[stl], t_v, bank[pb]] + out_mm_toks[-3:], inc=(stl == 3))
                    t_sa = b.dve(lambda e, _d=d1c, _og=og, _pb=pb: e.tensor_add(
                        S32[:, _d * 1024 + _og * 512:_d * 1024 + _og * 512 + 512],
                        S32[:, _d * 1024 + _og * 512:_d * 1024 + _og * 512 + 512],
                        b.psum_f[_pb][:, :]), [tm] + S_done)
                    t_sc = b.dve(lambda e, _d=d1c, _og=og: e.tensor_copy(
                        s_ext[:, _d * 1025 + _og * 512:_d * 1025 + _og * 512 + 512],
                        S32[:, _d * 1024 + _og * 512:_d * 1024 + _og * 512 + 512]), [t_sa])
                    bank[pb] = t_sa
                    new_S_done.append(t_sc)
            for dc in range(8):
                t_rs = b.dve(lambda e, _dc=dc: e.reduce_sum(
                    sc[:, 2:3], k_fm[:, _dc * 512:(_dc + 1) * 512], axis=AX.X), [t_qk])
                t_ka = b.dve(lambda e, _dc=dc: e.tensor_add(
                    kcs[:, _dc:_dc + 1], kcs[:, _dc:_dc + 1], sc[:, 2:3]), [t_rs] + S_done)
                t_kc = b.dve(lambda e, _dc=dc: e.tensor_copy(
                    s_ext[:, _dc * 1025 + 1024:_dc * 1025 + 1025], kcs[:, _dc:_dc + 1]), [t_ka])
                new_S_done.append(t_kc)
            S_done = new_S_done
            vext_done = tm
            k_used = tm
        else:
            vext_done = out_mm_toks[-3]
            k_used = t_av[-1]
        av_done = out_mm_toks[-3]
        q_used = out_mm_toks[-3]
        xnf_done[csl] = [t_v, t_qk]
    return res_toks



def emit_mla(b, x_in, x_out, W, scratch):
    nc = b.nc
    st = b.st
    cfg = b.cfg
    NG = cfg["NG"]
    SEQ = cfg["SEQ"]
    GPS = SEQ // 512
    NSEQ = cfg["TOK"] // SEQ
    NTILE_SEQ = SEQ // 128
    qscr = scratch["qscr"]

    b.arena_reset()
    wdkv = b.abf(8 * 256)
    wuk = b.abf(2 * 1024)
    wuv = b.abf(2 * 1024)
    wuq = [b.abf(8 * 128) for _ in range(2)]
    wo = b.abf(8 * 512)
    xnf = b.abf(8 * 512)
    ckv_g = b.abf(2 * 512)
    k_fm = b.abf(8 * SEQ)
    v_ext = b.abf(NTILE_SEQ * 1040)
    qld = b.abf(8 * 512)
    qst = [b.abf(512) for _ in range(2)]
    pT = [b.abf(512) for _ in range(3)]
    ctxn = b.abf(4 * 1024)
    ctxf = b.abf(1024)
    rx = [b.af32(1024) for _ in range(2)]
    sc = nc.alloc_sbuf_tensor("mla_sc", [128, 8], F32)

    ts = b.ts_bf
    pre = [st["tok_cst"], st["tok_ones"][1], st["tok_ones"][3]] + st["tok_id"] + st["tok_mk"]

    tws = []
    for dc in range(8):
        tws.append(b.dma("gpsimd", b.gq[6], lambda e, _dc=dc: e.dma_start(
            out=wdkv[:, _dc * 256:(_dc + 1) * 256],
            in_=W["L0_attn_W_DKV"][_dc * 128:(_dc + 1) * 128, :])))
    for lc in range(2):
        tws.append(b.dma("gpsimd", b.gq[6], lambda e, _lc=lc: e.dma_start(
            out=wuk[:, _lc * 1024:(_lc + 1) * 1024],
            in_=W["L0_attn_W_UK"][_lc * 128:(_lc + 1) * 128, :])))
        tws.append(b.dma("gpsimd", b.gq[6], lambda e, _lc=lc: e.dma_start(
            out=wuv[:, _lc * 1024:(_lc + 1) * 1024],
            in_=W["L0_attn_W_UV"][_lc * 128:(_lc + 1) * 128, :])))
    tw_res = tws[-1]

    bank = {"tr": None, 0: None, 1: None, 2: None, 3: None, 4: None, 5: None, "b1": None}
    xnf_done = []
    ckv_done = []
    kfm_done = []
    vext_done = []
    qst_done = [None, None]
    wuq_done = [None, None]
    qld_done = None
    pT_done = [None, None, None]
    ctxn_done = [None] * 4
    ctxf_done = None
    wo_done = None
    rx_done = [None, None]
    res_toks = []
    t_vones = b.gp(lambda e: e.memset(
        v_ext[:, :].rearrange("p (t h c) -> p t h c", t=NTILE_SEQ, h=16)[:, :, :, 64:65], 1.0), pre)

    for s in range(NSEQ):
        s0 = s * SEQ
        new_kfm_done = []
        new_vext_done = []
        new_ckv_done = []
        for g in range(GPS):
            t0 = s0 + g * 512
            xnf_r = xnf.rearrange("p (c t) -> p c t", c=8)
            for j in range(4):
                slot = j % 2
                xn, t_xn, xt, tl = ts.tile(x_in, t0 + j * 128, slot, extra_x_deps=pre)
                tev = ts.transpose_to(xn, t_xn, xnf_r[:, :, j * 128:(j + 1) * 128],
                                      None, extra=xnf_done, pb=j % 2)
                ts.xt_done[slot] = t_xn
                ts.xn_done[slot] = tev
                t_fm = tev
            t_ckv = None
            for lc in range(2):
                pb = lc
                tm = None
                for dc in range(8):
                    tm = b.pe(lambda e, _dc=dc, _lc=lc, _pb=pb: e.matmul(
                        b.psum_f[_pb][:, :], wdkv[:, _dc * 256 + _lc * 128:_dc * 256 + _lc * 128 + 128],
                        xnf[:, _dc * 512:(_dc + 1) * 512],
                        start=(_dc == 0), stop=(_dc == 7)),
                        deps=[t_fm, tw_res, bank[pb]] + ckv_done, inc=(dc == 7))
                t_ckv = b.dve(lambda e, _lc=lc, _pb=pb: e.tensor_copy(
                    ckv_g[:, _lc * 512:(_lc + 1) * 512], b.psum_f[_pb][:, :]), [tm])
                bank[pb] = t_ckv
            new_ckv_done = [t_ckv]
            for fc in range(8):
                pb = fc % 2
                tm = None
                for lc in range(2):
                    tm = b.pe(lambda e, _fc=fc, _lc=lc, _pb=pb: e.matmul(
                        b.psum_f[_pb][:, :], wuk[:, _lc * 1024 + _fc * 128:_lc * 1024 + _fc * 128 + 128],
                        ckv_g[:, _lc * 512:(_lc + 1) * 512],
                        start=(_lc == 0), stop=(_lc == 1)),
                        deps=[t_ckv, bank[pb]] + kfm_done, inc=(lc == 1))
                t_k = b.dve(lambda e, _fc=fc, _g=g, _pb=pb: e.tensor_copy(
                    k_fm[:, _fc * SEQ + _g * 512:_fc * SEQ + (_g + 1) * 512], b.psum_f[_pb][:, :]), [tm])
                bank[pb] = t_k
                new_kfm_done.append(t_k)
            for fc in range(8):
                wsl = fc % 2
                pb = fc % 2
                tw = b.dma("sync", b.sq[3], lambda e, _fc=fc, _w=wsl: e.dma_start(
                    out=wuq[_w][:, :].rearrange("p (d f) -> p d f", d=8),
                    in_=scratch["wuq_b"].rearrange("(d p) f -> p d f", p=128)[:, :, _fc * 128:(_fc + 1) * 128]),
                    deps=[wuq_done[wsl], b.wcast.get("wuq")])
                tm = None
                for dc in range(8):
                    tm = b.pe(lambda e, _dc=dc, _pb=pb, _w=wsl: e.matmul(
                        b.psum_f[_pb][:, :], wuq[_w][:, _dc * 128:(_dc + 1) * 128],
                        xnf[:, _dc * 512:(_dc + 1) * 512],
                        start=(_dc == 0), stop=(_dc == 7)),
                        deps=[t_fm, tw, bank[pb]], inc=(dc == 7))
                wuq_done[wsl] = tm
                t_q = b.act(lambda e, _pb=pb, _w=wsl: e.activation(
                    qst[_w][:, :], b.psum_f[_pb][:, :], AF.Copy), [tm, qst_done[wsl]])
                bank[pb] = t_q
                qst_done[wsl] = b.dma("sync", b.sq[4 + wsl], lambda e, _fc=fc, _g=g, _s=s, _w=wsl: e.dma_start(
                    out=qscr[_fc * 128:(_fc + 1) * 128, _s * SEQ + _g * 512:_s * SEQ + (_g + 1) * 512],
                    in_=qst[_w][:, :]), deps=[t_q])
            for tt in range(4):
                gt = g * 4 + tt
                pb = tt % 2
                for og in range(2):
                    tm = None
                    for lc in range(2):
                        tm = b.pe(lambda e, _tt=tt, _lc=lc, _og=og, _pb=pb: e.matmul(
                            b.psum_f[_pb][:, :], ckv_g[:, _lc * 512 + _tt * 128:_lc * 512 + _tt * 128 + 128],
                            wuv[:, _lc * 1024 + _og * 512:_lc * 1024 + _og * 512 + 512],
                            start=(_lc == 0), stop=(_lc == 1)),
                            deps=[t_ckv, bank[pb]] + vext_done, inc=(lc == 1))
                    t_v = b.act(lambda e, _gt=gt, _og=og, _pb=pb: e.activation(
                        v_ext[:, :].rearrange("p (t h c) -> p t h c", t=NTILE_SEQ, h=16)
                        [:, _gt, _og * 8:(_og + 1) * 8, 0:64],
                        b.psum_f[_pb][:, :].rearrange("p (h c) -> p h c", h=8), AF.Copy), [tm, t_vones])
                    bank[pb] = t_v
                    new_vext_done.append(t_v)
        xnf_done = [t_q, t_ckv]
        ckv_done = new_ckv_done
        kfm_done_prev = kfm_done
        kfm_done = new_kfm_done
        vext_done = new_vext_done
        t_kv_ready = [kfm_done[-1], new_vext_done[-1], t_vones]

        for qtg in range(GPS):
            tq = b.dma("sync", b.sq[6], lambda e, _s=s, _qtg=qtg: e.dma_start(
                out=qld[:, :].rearrange("p (c t) -> p c t", c=8),
                in_=qscr.rearrange("(c p) t -> p c t", p=128)[:, :, _s * SEQ + _qtg * 512:_s * SEQ + (_qtg + 1) * 512]),
                deps=[qld_done, qst_done[0], qst_done[1]])
            jmax = 4 * qtg + 3
            ctx_last = None
            ctx_banks_done = [bank[0], bank[1], bank[2], bank[3]]
            for h in range(16):
                hb = (h % 2) * 64
                fcq = h // 2
                pend = []

                def emit_ctx(pend):
                    nonlocal ctx_last
                    j, psl, t_e = pend
                    for tt in range(4):
                        if j > 4 * qtg + tt:
                            continue
                        first = (j == 0)
                        lastj = (j == 4 * qtg + tt)
                        tm_c = b.pe(lambda e, _h=h, _j=j, _tt=tt, _sl=psl, _f=first, _l=lastj: e.matmul(
                            b.psum_f[_tt][:, 0:65],
                            pT[_sl][:, _tt * 128:(_tt + 1) * 128],
                            v_ext[:, _j * 1040 + _h * 65:_j * 1040 + (_h + 1) * 65],
                            start=_f, stop=_l),
                            deps=[t_e] + ([ctx_banks_done[tt]] if first else []),
                            inc=lastj)
                        if lastj:
                            ctx_last = tm_c
                        pT_done[psl] = tm_c

                for j in range(jmax + 1):
                    skey = 4 + j % 2
                    tm_s = b.pe(lambda e, _h=h, _j=j, _hb=hb, _fcq=fcq, _pb=skey: e.matmul(
                        b.psum_f[_pb][:, :],
                        k_fm[_hb:_hb + 64, _fcq * SEQ + _j * 128:_fcq * SEQ + (_j + 1) * 128],
                        qld[_hb:_hb + 64, _fcq * 512:(_fcq + 1) * 512],
                        start=True, stop=True),
                        deps=[tq, bank[skey]] + t_kv_ready, inc=True)
                    psl = j % 3
                    t_e = b.act(lambda e, _pb=skey, _sl=psl: e.activation(
                        pT[_sl][:, :], b.psum_f[_pb][:, :], AF.Exp, scale=0.125),
                        [tm_s, pT_done[psl]])
                    bank[skey] = t_e
                    a = j - 4 * qtg
                    if 0 <= a <= 3:
                        t_e = b.dve(lambda e, _sl=psl, _a=a: e.tensor_mul(
                            pT[_sl][:, :], pT[_sl][:, :],
                            st["mk_b"][:, _a * 512:(_a + 1) * 512]), [t_e])
                    pend.append((j, psl, t_e))
                    if len(pend) > 2:
                        emit_ctx(pend.pop(0))
                for p_ in pend:
                    emit_ctx(p_)
                pend = []
                for tt in range(4):
                    t_d = b.dve(lambda e, _tt=tt: e.reciprocal(sc[:, _tt:_tt + 1],
                                                               b.psum_f[_tt][:, 64:65]), [ctx_last])
                    t_n = b.dve(lambda e, _h=h, _tt=tt: e.tensor_scalar(
                        ctxn[:, _tt * 1024 + _h * 64:_tt * 1024 + (_h + 1) * 64],
                        b.psum_f[_tt][:, 0:64], sc[:, _tt:_tt + 1], None, ALU.mult),
                        [t_d, ctxn_done[tt]])
                    ctx_banks_done[tt] = t_n
                    bank[tt] = t_n
            for tt in range(4):
                ttg = qtg * 4 + tt
                trc = None
                for ch in range(8):
                    trc = b.pe(lambda e, _ch=ch, _tt=tt: e.transpose(
                        b.psum_b[1][:, _ch * 128:(_ch + 1) * 128],
                        ctxn[:, _tt * 1024 + _ch * 128:_tt * 1024 + _ch * 128 + 128],
                        st["id_b"][:, :]), deps=[bank[tt], b.pbd[1]], inc=(ch == 7))
                t_cf = b.dve(lambda e: e.tensor_copy(
                    ctxf[:, :].rearrange("p (c t) -> p c t", c=8),
                    b.psum_b[1][:, :].rearrange("p (c t) -> p c t", c=8)), [trc, ctxf_done])
                b.pbd[1] = t_cf
                ctxn_done[tt] = t_cf
                rsl = tt % 2
                trx = b.dma("sync", b.sq[rsl], lambda e, _s=s, _ttg=ttg, _r=rsl: e.dma_start(
                    out=rx[_r][:, :], in_=x_in[_s * SEQ + _ttg * 128:_s * SEQ + (_ttg + 1) * 128, :]),
                    deps=[rx_done[rsl]])
                for og in range(2):
                    pb = 4 + og
                    tw_o = b.dma("scalar", b.aq[4], lambda e, _og=og: e.dma_start(
                        out=wo[:, :].rearrange("p (d f) -> p d f", d=8),
                        in_=scratch["wo_b"].rearrange("(d p) f -> p d f", p=128)[:, :, _og * 512:(_og + 1) * 512]),
                        deps=[wo_done, b.wcast.get("wo")])
                    tm = None
                    for dc in range(8):
                        tm = b.pe(lambda e, _dc=dc, _og=og, _pb=pb: e.matmul(
                            b.psum_f[_pb][:, :], ctxf[:, _dc * 128:(_dc + 1) * 128],
                            wo[:, _dc * 512:(_dc + 1) * 512],
                            start=(_dc == 0), stop=(_dc == 7)),
                            deps=[t_cf, tw_o, bank[pb]], inc=(dc == 7))
                    wo_done = tm
                    t_o = b.dve(lambda e, _og=og, _pb=pb, _r=rsl: e.tensor_add(
                        rx[_r][:, _og * 512:(_og + 1) * 512], b.psum_f[_pb][:, :],
                        rx[_r][:, _og * 512:(_og + 1) * 512]), [tm, trx])
                    bank[pb] = t_o
                tst = b.dma("scalar", b.aq[rsl], lambda e, _s=s, _ttg=ttg, _r=rsl: e.dma_start(
                    out=x_out[_s * SEQ + _ttg * 128:_s * SEQ + (_ttg + 1) * 128, :], in_=rx[_r][:, :]),
                    deps=[bank[4], bank[5]])
                rx_done[rsl] = tst
                ctxf_done = tm
                res_toks.append(tst)
            qld_done = ctx_last
    return res_toks


# version 17
# speedup vs baseline: 1.0014x; 1.0014x over previous
import numpy as np
import ml_dtypes
import concourse.bass as bass
import concourse.mybir as mybir
from concourse.bass_utils import run_bass_kernel_spmd
from concourse.bass_core_rust import IndirectOffsetOnAxis

F32 = mybir.dt.float32
BF16 = mybir.dt.bfloat16
I32 = mybir.dt.int32
U32 = mybir.dt.uint32
AF = mybir.ActivationFunctionType
ALU = mybir.AluOpType
AX = mybir.AxisListType

D = 1024
CGRP = 160
LN_EPS = 1e-5
DEN_EPS = 1e-6
L1_H = 4096

ENGINES = ("sync", "tensor", "vector", "scalar", "gpsimd")


class Sem:
    def __init__(self, nc, name):
        self.h = nc.alloc_semaphore(name)
        self.n = 0
        self.name = name
        self.last = None

    def bump(self, k):
        self.n += k
        return self.n

    def tok(self):
        return (self, self.n)


class B:

    def __init__(self, nc, cfg):
        self.nc = nc
        self.cfg = cfg
        self.streams = {e: [] for e in ENGINES}
        self.maxwait = {e: {} for e in ENGINES}
        self.eP = Sem(nc, "eP")
        self.eV = Sem(nc, "eV")
        self.eA = Sem(nc, "eA")
        self.eG = Sem(nc, "eG")
        self.sq = [Sem(nc, f"sq{i}") for i in range(8)]
        self.gq = [Sem(nc, f"gq{i}") for i in range(8)]
        self.aq = [Sem(nc, f"aq{i}") for i in range(6)]
        self.psum_f = [nc.alloc_psum_tensor(f"psf{i}", [128, 512], F32) for i in range(6)]
        self.psum_b = [nc.alloc_psum_tensor(f"psb{i}", [128, 1024], BF16) for i in range(2)]
        self.pbd = {0: None, 1: None}

    def make_arenas(self, bf_cols, f32_cols):
        self.ar_bf = self.nc.alloc_sbuf_tensor("ar_bf", [128, bf_cols], BF16)
        self.ar_f = self.nc.alloc_sbuf_tensor("ar_f", [128, f32_cols], F32)
        self.ar_off = {"bf": 0, "f": 0}

    def arena_reset(self):
        self.ar_off = {"bf": 0, "f": 0}

    def abf(self, cols):
        o = self.ar_off["bf"]
        assert o + cols <= self.ar_bf.shape[1], f"bf arena overflow {o}+{cols}"
        self.ar_off["bf"] = o + cols
        return self.ar_bf[:, o:o + cols]

    def af32(self, cols):
        o = self.ar_off["f"]
        assert o + cols <= self.ar_f.shape[1], f"f32 arena overflow {o}+{cols}"
        self.ar_off["f"] = o + cols
        return self.ar_f[:, o:o + cols]

    def _waits(self, eng, deps):
        out = []
        mw = self.maxwait[eng]
        for d in deps:
            if d is None:
                continue
            sem, val = d
            if val <= 0:
                continue
            if mw.get(sem.name, -1) >= val:
                continue
            mw[sem.name] = val
            out.append((sem.h, val))
        return out

    def op(self, eng, f, deps=(), inc=None, inc_by=1):
        waits = self._waits(eng, deps)
        tok = None
        if inc is not None:
            tok = (inc, inc.bump(inc_by))

        def run(e, waits=waits, f=f, inc=inc, inc_by=inc_by):
            for (h, v) in waits:
                e.wait_ge(h, v)
            ins = f(e)
            if inc is not None and ins is not None:
                ins.then_inc(inc.h, inc_by)
        self.streams[eng].append(run)
        return tok

    def pe(self, f, deps=(), inc=True):
        return self.op("tensor", f, deps, self.eP if inc else None)

    def dve(self, f, deps=()):
        return self.op("vector", f, deps, self.eV)

    def act(self, f, deps=()):
        return self.op("scalar", f, deps, self.eA)

    def gp(self, f, deps=()):
        return self.op("gpsimd", f, deps, self.eG)

    def dma(self, queue, slot, f, deps=()):
        deps = list(deps)
        if slot.last is not None:
            deps.append(slot.last)
        tok = self.op(queue, f, deps, slot, inc_by=16)
        slot.last = tok
        return tok

    def fence(self):
        toks = [self.eP.tok(), self.eV.tok(), self.eA.tok(), self.eG.tok()]
        toks += [s.tok() for s in self.sq + self.gq + self.aq]
        for eng in ENGINES:
            waits = self._waits(eng, toks)
            if waits:
                def run(e, waits=waits):
                    for (h, v) in waits:
                        e.wait_ge(h, v)
                self.streams[eng].append(run)

    def assemble(self):
        block = self.nc.Block()
        with block:
            for eng in ENGINES:
                ops = self.streams[eng]

                def make(ops):
                    def body(e):
                        for f in ops:
                            f(e)
                    return body
                getattr(block, eng)(make(ops))



def preamble(b, W):
    nc = b.nc
    cfg = b.cfg
    st = {}

    consts = [LN_EPS, -0.5, 0.5, 1.0 / 1024.0, DEN_EPS, 1e9, 1e4,
              float(cfg["CAP"]), 0.125, 1e-10, 1.0 + LN_EPS]
    consts = sorted(set(consts))
    cst = nc.alloc_sbuf_tensor("cst", [128, len(consts)], F32)
    tok_cst = []
    for i, v in enumerate(consts):
        nc.const_aps.register(F32, v, cst[:, i:i + 1])
        tok_cst.append(b.gp(lambda e, _i=i, _v=v: e.memset(cst[:, _i:_i + 1], _v)))
    st["tok_cst"] = tok_cst[-1]

    b.arena_reset()
    iv = b.af32(128)
    st["id_f"] = nc.alloc_sbuf_tensor("id_f", [128, 128], F32)
    st["id_b"] = nc.alloc_sbuf_tensor("id_b", [128, 128], BF16)
    st["u01"] = nc.alloc_sbuf_tensor("u01", [128, 128], F32)
    t_iv = b.gp(lambda e: e.iota(iv[:, :], pattern=[[-1, 128]], base=0,
                                 channel_multiplier=1, allow_small_or_imprecise_dtypes=True))
    t1 = b.dve(lambda e: e.tensor_single_scalar(st["id_f"][:, :], iv[:, :], 0.0, ALU.is_equal), [t_iv, st["tok_cst"]])
    t2 = b.dve(lambda e: e.tensor_single_scalar(st["id_b"][:, :], iv[:, :], 0.0, ALU.is_equal), [t_iv])
    t3 = b.dve(lambda e: e.tensor_single_scalar(st["u01"][:, :], iv[:, :], 0.0, ALU.is_lt), [t_iv])
    st["tok_id"] = [t1, t2, t3]

    miv = b.af32(512)
    st["mk_b"] = nc.alloc_sbuf_tensor("mk_b", [128, 4 * 512], BF16)
    tmks = []
    for a in range(4):
        ti = b.gp(lambda e, _a=a: e.iota(miv[:, :], pattern=[[1, 512]], base=-128 * _a,
                                         channel_multiplier=-1, allow_small_or_imprecise_dtypes=True),
                  deps=tmks[-1:])
        tmks.append(b.dve(lambda e, _a=a: e.tensor_single_scalar(
            st["mk_b"][:, _a * 512:(_a + 1) * 512], miv[:, :], 0.0, ALU.is_ge), [ti]))
    st["tok_mk"] = tmks[-1:]

    st["iota8"] = nc.alloc_sbuf_tensor("iota8", [128, 8], F32)
    st["tok_iota8"] = b.gp(lambda e: e.iota(st["iota8"][:, :], pattern=[[1, 8]], base=0,
                                            channel_multiplier=0, allow_small_or_imprecise_dtypes=True))

    st["ones128_f"] = nc.alloc_sbuf_tensor("ones128_f", [128, 128], F32)
    st["tok_ones128"] = b.gp(lambda e: e.memset(st["ones128_f"][:, :], 1.0))
    st["ones_row_f"] = nc.alloc_sbuf_tensor("ones_row_f", [1, 128], F32)
    st["ones_row_b"] = nc.alloc_sbuf_tensor("ones_row_b", [1, 128], BF16)
    st["ones_col_f"] = nc.alloc_sbuf_tensor("ones_col_f", [128, 1], F32)
    st["ones_col_b"] = nc.alloc_sbuf_tensor("ones_col_b", [128, 1], BF16)
    st["tok_ones"] = [
        b.gp(lambda e: e.memset(st["ones_row_f"][:, :], 1.0)),
        b.gp(lambda e: e.memset(st["ones_row_b"][:, :], 1.0)),
        b.gp(lambda e: e.memset(st["ones_col_f"][:, :], 1.0)),
        b.gp(lambda e: e.memset(st["ones_col_b"][:, :], 1.0)),
    ]

    ps_t = b.psum_f[0]
    brow = b.af32(1024)[0:1, :]
    brow_state = {"tok": None}

    def bias_cols(name, row_ap, F):
        cols = nc.alloc_sbuf_tensor(f"{name}_col", [128, F // 128], F32)
        tev = None
        for c0 in range(0, F, 1024):
            w = min(1024, F - c0)
            tl = b.dma("sync", b.sq[7], lambda e, _c0=c0, _w=w: e.dma_start(
                out=brow[:, 0:_w], in_=row_ap[:, _c0:_c0 + _w]), deps=[brow_state["tok"]])
            for c in range(w // 128):
                cc = c0 // 128 + c
                tt = b.pe(lambda e, _c=c, _cc=cc: e.transpose(
                    ps_t[:, (_cc % 4) * 128:(_cc % 4) * 128 + 1],
                    brow[:, _c * 128:(_c + 1) * 128],
                    st["id_f"][0:1, 0:1]),
                    deps=[tl, t1, tev])
                tev = b.dve(lambda e, _cc=cc: e.tensor_copy(
                    cols[:, _cc:_cc + 1], ps_t[:, (_cc % 4) * 128:(_cc % 4) * 128 + 1]), [tt])
            brow_state["tok"] = tev
        return cols, tev

    if "L1_attn_b_qkv" in W:
        st["bv_row"] = nc.alloc_sbuf_tensor("bv_row", [1, 1024], F32)
        tr0 = b.dma("sync", b.sq[7], lambda e: e.dma_start(
            out=st["bv_row"][:, :], in_=W["L1_attn_b_qkv"][None, 2048:3072]))
        st["bv_row_b"] = nc.alloc_sbuf_tensor("bv_row_b", [1, 1024], BF16)
        st["tok_bvb"] = b.dve(lambda e: e.tensor_copy(st["bv_row_b"][:, :], st["bv_row"][:, :]), [tr0])
        st["bq_col"], tq = bias_cols("bq", W["L1_attn_b_qkv"][None, 0:1024], 1024)
        st["bk_col"], tk = bias_cols("bk", W["L1_attn_b_qkv"][None, 1024:2048], 1024)
        st["bg_col"], tg = bias_cols("bg", W["L1_attn_b_gate"][None, :], 1024)
        st["b_proj_row"] = nc.alloc_sbuf_tensor("b_proj_row", [1, 1024], F32)
        tr1 = b.dma("sync", b.sq[7], lambda e: e.dma_start(
            out=st["b_proj_row"][:, :], in_=W["L1_attn_b_proj"][None, :]))
        st["lng_col"], tlg = bias_cols("lng", W["L1_attn_ln_g"][None, :], 1024)
        st["lnb_col"], tlb = bias_cols("lnb", W["L1_attn_ln_b"][None, :], 1024)
        st["nbq_col"] = nc.alloc_sbuf_tensor("nbq_col", [128, 8], F32)
        tnq = b.dve(lambda e: e.tensor_scalar_mul(st["nbq_col"][:, :], st["bq_col"][:, :], -1.0), [tq])
        st["tok_l1attn_bias"] = [tq, tk, tg, tlg, tlb, tr0, tr1, tnq]
    if "L1_ffn_b1" in W:
        st["l1b1_col"], tb1 = bias_cols("l1b1", W["L1_ffn_b1"][None, :], 4096)
        st["l1b2_row"] = nc.alloc_sbuf_tensor("l1b2_row", [1, 1024], F32)
        tr2 = b.dma("sync", b.sq[7], lambda e: e.dma_start(
            out=st["l1b2_row"][:, :], in_=W["L1_ffn_b2"][None, :]))
        st["scale_t"] = nc.alloc_sbuf_tensor("scale_t", [1, 1], F32)
        tr3 = b.dma("sync", b.sq[7], lambda e: e.dma_start(
            out=st["scale_t"][:, :], in_=W["L1_ffn_scale"][None, :]))
        st["tok_l1ffn_bias"] = [tb1, tr2, tr3]
    if "L0_ffn_b1" in W:
        st["l0b1_col"], tb = bias_cols("l0b1", W["L0_ffn_b1"].rearrange("e h -> (e h)")[None, :], 8 * 2048)
        st["tok_l0ffn_bias"] = [tb]

    if "L0_ffn_wg" in W:
        wgt = nc.alloc_sbuf_tensor("wgt", [8, 16], F32)
        wg_ss = nc.alloc_sbuf_tensor("wg_ss", [8, 1], F32)
        wg_sq = nc.alloc_sbuf_tensor("wg_sq", [8, 16], F32)
        st["wg_nT"] = nc.alloc_sbuf_tensor("wg_nT", [16, 8], F32)
        st["w_red"] = nc.alloc_sbuf_tensor("w_red", [128, 8 * 16], F32)
        tw = b.dma("sync", b.sq[7], lambda e: e.dma_start(out=wgt[:, :], in_=W["L0_ffn_wg"][:, :]))
        twr = None
        for c in range(8):
            twr = b.dma("sync", b.sq[7], lambda e, _c=c: e.dma_start(
                out=st["w_red"][:, _c * 16:(_c + 1) * 16],
                in_=W["L0_ffn_w_red"][_c * 128:(_c + 1) * 128, :]))
        t_sq = b.act(lambda e: e.activation(wg_sq[:, :], wgt[:, :], AF.Square,
                                            accum_out=wg_ss[:, :]), [tw, st["tok_cst"]])
        t_ln = b.act(lambda e: e.activation(wg_ss[:, :], wg_ss[:, :], AF.Ln), [t_sq])
        t_ex = b.act(lambda e: e.activation(wg_ss[:, :], wg_ss[:, :], AF.Exp, scale=-0.5), [t_ln])
        t_cl = b.dve(lambda e: e.tensor_scalar_min(wg_ss[:, :], wg_ss[:, :], 1e4), [t_ex])
        t_mu = b.dve(lambda e: e.tensor_scalar(wgt[:, :], wgt[:, :], wg_ss[:, :], None, ALU.mult), [t_cl])
        t_tr = b.pe(lambda e: e.transpose(ps_t[0:16, 0:8], wgt[:, :], st["id_f"][0:8, 0:8]),
                    [t_mu, t1])
        tok_wg = b.dve(lambda e: e.tensor_copy(st["wg_nT"][:, :], ps_t[0:16, 0:8]), [t_tr])
        st["wg_nT_b"] = nc.alloc_sbuf_tensor("wg_nT_b", [16, 8], BF16)
        st["w_red_b"] = nc.alloc_sbuf_tensor("w_red_b", [128, 8 * 16], BF16)
        st["tok_wg"] = b.dve(lambda e: e.tensor_copy(st["wg_nT_b"][:, :], st["wg_nT"][:, :]), [tok_wg])
        st["tok_wred"] = b.dve(lambda e: e.tensor_copy(st["w_red_b"][:, :], st["w_red"][:, :]), [twr])

    b.st = st
    return st



class TStep:

    def __init__(self, b, name, dtype, share=None):
        nc = b.nc
        self.b = b
        self.dtype = dtype
        if share is None:
            self.xt = [nc.alloc_sbuf_tensor(f"{name}_xt{i}", [128, 1024], F32) for i in range(2)]
            self.sqd = nc.alloc_sbuf_tensor(f"{name}_sqd", [128, 1024], F32)
        else:
            self.xt = share.xt
            self.sqd = share.sqd
        self.xn = [nc.alloc_sbuf_tensor(f"{name}_xn{i}", [128, 1024], dtype) for i in range(2)]
        self.stats = [nc.alloc_sbuf_tensor(f"{name}_st{i}", [128, 8], F32) for i in range(2)]
        self.xt_done = [None, None]
        self.xn_done = [None, None]
        self.st_done = [(), ()]

    def tile(self, x_dram, tok0, slot, extra_x_deps=(), ln2=False):
        b = self.b
        xt = self.xt[slot]
        xn = self.xn[slot]
        s = self.stats[slot]
        prev = list(self.st_done[slot])
        tl = b.dma("sync", b.sq[slot], lambda e: e.dma_start(
            out=xt[:, :], in_=x_dram[tok0:tok0 + 128, :]),
            deps=list(extra_x_deps) + [self.xt_done[slot]])
        t_sq = b.act(lambda e: e.activation(self.sqd[:, :], xt[:, :], AF.Square,
                                            accum_out=s[:, 0:1]), [tl] + prev)
        t_s = b.dve(lambda e: e.reduce_sum(s[:, 1:2], xt[:, :], axis=AX.X), [tl] + prev)
        t_m = b.dve(lambda e: e.tensor_scalar_mul(s[:, 2:3], s[:, 1:2], 1.0 / 1024.0), [t_s])
        t_e = b.dve(lambda e: e.tensor_scalar_mul(s[:, 3:4], s[:, 0:1], 1.0 / 1024.0), [t_sq])
        t_ms = b.dve(lambda e: e.tensor_mul(s[:, 4:5], s[:, 2:3], s[:, 2:3]), [t_m])
        t_v = b.dve(lambda e: e.tensor_sub(s[:, 5:6], s[:, 3:4], s[:, 4:5]), [t_e, t_ms])
        if ln2:
            t_ln = b.act(lambda e: e.activation(s[:, 6:7], s[:, 5:6], AF.Ln,
                                                bias=1e-10, scale=1.0 + LN_EPS), [t_v])
        else:
            t_ln = b.act(lambda e: e.activation(s[:, 6:7], s[:, 5:6], AF.Ln, bias=LN_EPS), [t_v])
        t_rs = b.act(lambda e: e.activation(s[:, 7:8], s[:, 6:7], AF.Exp, scale=-0.5), [t_ln])
        t_nmr = b.dve(lambda e: e.tensor_scalar(s[:, 4:5], s[:, 2:3], s[:, 7:8], -1.0,
                                                ALU.mult, ALU.mult), [t_rs, t_m])
        t_xn = b.act(lambda e: e.activation(xn[:, :], xt[:, :], AF.Identity,
                                            scale=s[:, 7:8], bias=s[:, 4:5]),
                     [t_nmr, tl, self.xn_done[slot]])
        self.st_done[slot] = (t_xn, t_nmr)
        return xn, t_xn, xt, tl

    def transpose_to(self, xn_ap, t_xn, dst_view, bank_done, extra=(), gain=None, beta=None, pb=0):
        b = self.b
        st = b.st
        if self.dtype == BF16:
            bank = b.psum_b[pb]
            ident = st["id_b"]
            tt = None
            for c in range(8):
                tt = b.pe(lambda e, _c=c: e.transpose(
                    bank[:, _c * 128:(_c + 1) * 128], xn_ap[:, _c * 128:(_c + 1) * 128],
                    ident[:, :]), deps=[t_xn, bank_done, b.pbd[pb]] + list(extra), inc=(c == 7))
            if gain is None:
                tev = b.dve(lambda e: e.tensor_copy(
                    dst_view, bank[:, :].rearrange("p (c t) -> p c t", c=8)), [tt])
            else:
                tev = None
                for c in range(8):
                    tev = b.dve(lambda e, _c=c: e.tensor_scalar(
                        dst_view[:, _c, :], bank[:, _c * 128:(_c + 1) * 128],
                        gain[:, _c:_c + 1], beta[:, _c:_c + 1], ALU.mult, ALU.add),
                        [tt])
            b.pbd[pb] = tev
            return tev
        else:
            ident = st["id_f"]
            banks = (b.psum_f[4], b.psum_f[5])
            tev = None
            tts = []
            for half in range(2):
                bank = banks[half]
                tt = None
                for c in range(4):
                    cc = half * 4 + c
                    tt = b.pe(lambda e, _cc=cc, _c=c, _bank=bank: e.transpose(
                        _bank[:, _c * 128:(_c + 1) * 128], xn_ap[:, _cc * 128:(_cc + 1) * 128],
                        ident[:, :]), deps=[t_xn, bank_done] + list(extra), inc=(c == 3))
                tts.append(tt)
            tev = b.dve(lambda e: e.tensor_copy(
                dst_view[:, 0:4, :], banks[0][:, :].rearrange("p (c t) -> p c t", c=4)), [tts[0]])
            tev2 = b.dve(lambda e: e.tensor_copy(
                dst_view[:, 4:8, :], banks[1][:, :].rearrange("p (c t) -> p c t", c=4)), [tts[1]])
            return tev2



def emit_moe(b, x_in, x_out, W, scratch, ph):
    nc = b.nc
    st = b.st
    cfg = b.cfg
    NG, NT, CAP = cfg["NG"], cfg["NT"], cfg["CAP"]
    CG = cfg["CGRP"]
    NS = 8 * CAP
    xn_sorted = scratch["xn_sorted"]
    osort = scratch["osort"]
    if not hasattr(b, "moe_shared"):
        b.moe_shared = {
            "mi8": nc.alloc_sbuf_tensor("mi8_sh", [128, 16], U32),
            "slots": nc.alloc_sbuf_tensor("slots_sh", [128, NT], I32),
            "bnd": nc.gpsimd.alloc_register("moebnd"),
        }
        b.gp(lambda e: e.reg_mov(b.moe_shared["bnd"], NS - 1))

    b.arena_reset()
    xnf2 = [b.abf(8 * 512) for _ in range(2)]
    xredT = b.abf(512)
    xnbf = [b.abf(1024) for _ in range(4)]
    oh_sb = b.af32(4 * 8)
    sc_half = b.af32(NT)
    esb2 = [b.af32(8) for _ in range(2)]
    mx82 = [b.af32(8) for _ in range(2)]
    tmp82 = [b.af32(8) for _ in range(2)]
    scol2 = [b.af32(8) for _ in range(2)]
    b2row = b.af32(1024)[0:1, :]
    mi8 = b.moe_shared["mi8"]
    slots = b.moe_shared["slots"]
    base_sb = b.af32(8)[0:1, :]
    zero_sb = b.af32(8)[0:1, :]
    t_z = b.gp(lambda e: e.memset(zero_sb[:, :], 0.0))

    ts = b.ts_bf
    resx = [ts.xt[0], ts.xt[1], b.af32(1024)]
    outr = [b.af32(1024) for _ in range(3)]
    pre_deps = [st["tok_wg"], st["tok_wred"], st["tok_iota8"], st["tok_cst"],
                st["tok_ones"][-1], st["tok_l0ffn_bias"][-1]] + st["tok_id"]

    bank_g = {0: [], (1, 0): [], (1, 1): [], (2, 0): [], (2, 1): []}
    xnf_done = [None, None]
    scatter_toks = []
    scat_prev = [None] * 4
    cnt_prev = None
    cnt_last = None
    base_rd = None
    mi_done = [None, None]

    for g in range(NG):
        gsl = g % 2
        xnf_fm = xnf2[gsl]
        xnf_r = xnf_fm.rearrange("p (c t) -> p c t", c=8)
        tile_toks = []
        for j in range(4):
            gt = g * 4 + j
            slot = gt % 2
            xn, t_xn, xt, tl = ts.tile(x_in, g * 512 + j * 128, slot, extra_x_deps=pre_deps)
            t_bf = b.dve(lambda e, _j=j, _xn=xn: e.tensor_copy(xnbf[_j][:, :], _xn[:, :]),
                         [t_xn, scat_prev[j]])
            tev = ts.transpose_to(xn, t_xn, xnf_r[:, :, j * 128:(j + 1) * 128],
                                  None, extra=[xnf_done[gsl]], pb=j % 2)
            ts.xt_done[slot] = t_xn
            ts.xn_done[slot] = tev
            tile_toks.append((gt, slot, t_bf, tev))
        t_fm = tile_toks[-1][3]

        pxr = b.psum_f[0]
        tm = None
        for c in range(8):
            tm = b.pe(lambda e, _c=c, _x=xnf_fm: e.matmul(pxr[0:16, :], st["w_red_b"][:, _c * 16:(_c + 1) * 16],
                                                          _x[:, _c * 512:(_c + 1) * 512],
                                                          start=(_c == 0), stop=(_c == 7)),
                      deps=[t_fm, st["tok_wred"], st["tok_wg"], st["tok_ones128"]] + bank_g[0],
                      inc=(c == 7))
        t_xr = b.dve(lambda e: e.tensor_copy(xredT[0:16, :], pxr[0:16, :]), [tm])
        bank_g[0] = [t_xr]
        xnf_done[gsl] = t_xr

        if g == 0:
            t_base = None
        else:
            t_base = b.dve(lambda e: e.tensor_copy(base_sb[0:1, :], b.psum_f[3][0:1, 0:8]),
                           [cnt_last, base_rd])
        for (gt, slot, t_bf, tev) in tile_toks:
            j = gt % 4
            jp = j % 2
            esb, mx8, tmp8, scol = esb2[jp], mx82[jp], tmp82[jp], scol2[jp]
            plg = b.psum_f[1] if jp == 0 else b.psum_f[4]
            t_lg = b.pe(lambda e, _j=j, _p=plg: e.matmul(_p[:, 0:8], xredT[0:16, _j * 128:(_j + 1) * 128],
                                                 st["wg_nT_b"][:, :], start=True, stop=True),
                        deps=[t_xr] + bank_g[(1, jp)])
            t_nm = b.dve(lambda e, _p=plg, _s=scol: e.reduce_max(_s[:, 0:1], _p[:, 0:8], axis=AX.X, negate=True), [t_lg])
            t_ex = b.act(lambda e, _p=plg, _s=scol, _e=esb: e.activation(_e[:, :], _p[:, 0:8], AF.Exp,
                                                bias=_s[:, 0:1], accum_out=_s[:, 1:2]), [t_nm, t_lg])
            t_rc = b.dve(lambda e, _s=scol: e.reciprocal(_s[:, 2:3], _s[:, 1:2]), [t_ex])
            t_sh = b.dve(lambda e, _gt=gt, _s=scol: e.tensor_scalar_mul(sc_half[:, _gt:_gt + 1], _s[:, 2:3], 0.5), [t_rc])
            t_mx = b.dve(lambda e, _p=plg, _m=mx8: e.max(_m[:, :], _p[:, 0:8]), [t_lg, mi_done[jp]])
            t_mi = b.dve(lambda e, _p=plg, _m=mx8, _jp=jp: e.max_index(
                mi8[:, _jp * 8:(_jp + 1) * 8], _m[:, :], _p[:, 0:8]), [t_mx])
            t_if = b.dve(lambda e, _s=scol, _jp=jp: e.tensor_copy(_s[:, 3:4], mi8[:, _jp * 8:_jp * 8 + 1]), [t_mi])
            mi_done[jp] = t_if
            bank_g[(1, jp)] = [t_if, t_ex]
            t_oh = b.dve(lambda e, _j=j, _s=scol: e.tensor_scalar(oh_sb[:, _j * 8:(_j + 1) * 8], st["iota8"][:, :],
                                                         _s[:, 3:4], None, ALU.is_equal),
                         [t_if, cnt_prev])
            prk = b.psum_f[2] if jp == 0 else b.psum_f[5]
            tr = None
            for i in range(j + 1):
                lhs = st["ones128_f"] if i < j else st["u01"]
                tr = b.pe(lambda e, _i=i, _j=j, _lhs=lhs, _p=prk, _st=(i == j and g == 0): e.matmul(
                    _p[:, 0:8], _lhs[:, :], oh_sb[:, _i * 8:(_i + 1) * 8],
                    start=(_i == 0), stop=_st),
                    deps=[t_oh] + bank_g[(2, jp)], inc=(i == j and g == 0))
            if g > 0:
                tr = b.pe(lambda e, _p=prk: e.matmul(
                    _p[:, 0:8], st["ones_row_f"][:, :], base_sb[0:1, :],
                    start=False, stop=True), deps=[t_base], inc=True)
            t_rm = b.dve(lambda e, _j=j, _t=tmp8, _p=prk: e.tensor_mul(_t[:, :], _p[:, 0:8], oh_sb[:, _j * 8:(_j + 1) * 8]), [tr])
            t_ra = b.dve(lambda e, _s=scol, _t=tmp8: e.reduce_sum(_s[:, 4:5], _t[:, :], axis=AX.X), [t_rm])
            bank_g[(2, jp)] = [t_rm]
            t_s1 = b.dve(lambda e, _s=scol: e.scalar_tensor_tensor(_s[:, 5:6], _s[:, 3:4], float(CAP),
                                                          _s[:, 4:5], ALU.mult, ALU.add), [t_ra, t_if])
            t_ov = b.dve(lambda e, _s=scol: e.tensor_single_scalar(_s[:, 6:7], _s[:, 4:5], float(CAP), ALU.is_ge), [t_ra])
            t_s2 = b.dve(lambda e, _s=scol: e.scalar_tensor_tensor(_s[:, 7:8], _s[:, 6:7], 1e9,
                                                          _s[:, 5:6], ALU.mult, ALU.add), [t_ov, t_s1])
            t_si = b.dve(lambda e, _gt=gt, _s=scol: e.tensor_copy(slots[:, _gt:_gt + 1], _s[:, 7:8]), [t_s2])
            t_sc = b.op("gpsimd", lambda e, _gt=gt, _j=j: e.indirect_dma_start(
                out=xn_sorted[:, :], out_offset=IndirectOffsetOnAxis(slots[:, _gt:_gt + 1], 0),
                in_=xnbf[_j][:, :], in_offset=None,
                bounds_check=b.moe_shared["bnd"], oob_is_err=False),
                deps=[t_si, t_bf], inc=b.gq[0], inc_by=16)
            scatter_toks.append(t_sc)
        b.drain_casts(1)
        for j in range(4):
            scat_prev[j] = b.gq[0].tok()
        if g < NG - 1:
            for j in range(4):
                cnt_last = b.pe(lambda e, _j=j, _g=g, _st=(g == 0 and j == 3): e.matmul(
                    b.psum_f[3][0:1, 0:8], st["ones_col_f"][:, :], oh_sb[:, _j * 8:(_j + 1) * 8],
                    start=(_j == 0), stop=_st),
                    deps=[t_base] if (j == 0 and g > 0) else [], inc=(g == 0 and j == 3))
            if g > 0:
                cnt_last = b.pe(lambda e: e.matmul(
                    b.psum_f[3][0:1, 0:8], st["ones_row_f"][0:1, 0:1], base_sb[0:1, :],
                    start=False, stop=True), deps=[t_base], inc=True)
                base_rd = cnt_last
        cnt_prev = cnt_last

    b.drain_casts(len(b.cast_sched))
    b.fence()

    b.ar_off["bf"] = 0
    w1sb = b.abf(8 * 2048)
    w2sb = b.abf(16 * 1024)
    hs_fm2 = [b.abf(16 * 512) for _ in range(2)]
    xs_fm2 = [b.abf(8 * 512) for _ in range(2)]
    xs_tm = [b.abf(1024) for _ in range(2)]
    og_tm = [b.abf(1024) for _ in range(3)]
    bank_tr = [None, None]
    NSG = CAP // 512
    w1_free = None
    w2_free = None
    b2_free = None
    bank_h = {0: None, 1: None, 2: None}
    bank_o = {3: None, 4: None, 5: None}
    xsf_done = [None, None]
    og_store_prev = [None, None, None]
    ostore_toks = []
    wload = {}
    w1b = scratch["w1e_b"]
    w2b = scratch["w2e_b"]
    def load_w1(e):
        tw1 = None
        for dc in range(0, 8, 2):
            tw1 = b.dma("sync", b.sq[5], lambda e_, _dc=dc, _e=e: e_.dma_start(
                out=w1sb[:, _dc * 2048:(_dc + 2) * 2048].rearrange("p (c k) -> p c k", c=2),
                in_=w1b.rearrange("e (c p) h -> e p c h", p=128)[_e, :, _dc:_dc + 2, :]),
                deps=[w1_free, b.wcast.get(f"w1_{e}")])
        return tw1
    def load_w2(e):
        tw2 = None
        for hc in range(0, 16, 4):
            tw2 = b.dma("sync", b.sq[6], lambda e_, _hc=hc, _e=e: e_.dma_start(
                out=w2sb[:, _hc * 1024:(_hc + 4) * 1024].rearrange("p (c k) -> p c k", c=4),
                in_=w2b.rearrange("e (c p) o -> e p c o", p=128)[_e, :, _hc:_hc + 4, :]),
                deps=[w2_free, b.wcast.get(f"w2_{e}")])
        tb2 = b.dma("sync", b.sq[4], lambda e_, _e=e: e_.dma_start(
            out=b2row[:, :], in_=W["L0_ffn_b2"][_e:_e + 1, :]), deps=[b2_free])
        return tw2, tb2
    wload[0] = (load_w1(0), load_w2(0))
    pairs = []
    sg0 = 0
    while sg0 < NSG:
        pairs.append(tuple(range(sg0, min(sg0 + 2, NSG))))
        sg0 += 2
    for e in range(8):
        tw1, (tw2, tb2) = wload[e]
        for ip, pair in enumerate(pairs):
            t_xs = {}
            for a, sg in enumerate(pair):
                row0 = e * CAP + sg * 512
                xs_fm = xs_fm2[a]
                xs_r = xs_fm.rearrange("p (c t) -> p c t", c=8)
                for t in range(4):
                    slot = t % 2
                    tl = b.dma("sync", b.sq[2 + slot], lambda e_, _r=row0 + t * 128, _s=slot: e_.dma_start(
                        out=xs_tm[_s][:, :], in_=xn_sorted[_r:_r + 128, :]), deps=[bank_tr[t % 2]])
                    tt = None
                    for c in range(8):
                        tt = b.pe(lambda e_, _c=c, _s=slot, _b=t % 2: e_.transpose(
                            b.psum_b[_b][:, _c * 128:(_c + 1) * 128], xs_tm[_s][:, _c * 128:(_c + 1) * 128],
                            st["id_b"][:, :]), deps=[tl, b.pbd[t % 2], xsf_done[a]], inc=(c == 7))
                    tev = b.dve(lambda e_, _t=t, _b=t % 2, _r=xs_r: e_.tensor_copy(
                        _r[:, :, _t * 128:(_t + 1) * 128],
                        b.psum_b[_b][:, :].rearrange("p (c t) -> p c t", c=8)), [tt])
                    b.pbd[t % 2] = tev
                    bank_tr[t % 2] = tev
                t_xs[a] = tev
            t_h = {}
            tm_h = {}
            for hc in range(16):
                for a, sg in enumerate(pair):
                    pb = (hc * len(pair) + a) % 3
                    tm = None
                    for dc in range(8):
                        tm = b.pe(lambda e_, _dc=dc, _hc=hc, _pb=pb, _a=a: e_.matmul(
                            b.psum_f[_pb][:, :], w1sb[:, _dc * 2048 + _hc * 128:_dc * 2048 + _hc * 128 + 128],
                            xs_fm2[_a][:, _dc * 512:(_dc + 1) * 512],
                            start=(_dc == 0), stop=(_dc == 7)),
                            deps=[t_xs[a], tw1, bank_h[pb]], inc=(dc == 7))
                    tg = b.act(lambda e_, _hc=hc, _e=e, _pb=pb, _a=a: e_.activation(
                        hs_fm2[_a][:, _hc * 512:(_hc + 1) * 512], b.psum_f[_pb][:, :], AF.Gelu,
                        bias=st["l0b1_col"][:, _e * 16 + _hc:_e * 16 + _hc + 1]), [tm])
                    bank_h[pb] = tg
                    t_h[a] = tg
                    tm_h[a] = tm
            for a, sg in enumerate(pair):
                xsf_done[a] = tm_h[a]
            if ip == len(pairs) - 1:
                w1_free = tm_h[len(pair) - 1]
                if e < 7:
                    next_w1 = load_w1(e + 1)
            for a, sg in enumerate(pair):
                row0 = e * CAP + sg * 512
                for t in range(4):
                    gt2 = a * 4 + t
                    ob0, ob1 = 3 + (2 * gt2) % 3, 3 + (2 * gt2 + 1) % 3
                    tm = None
                    for hc in range(16):
                        for og, ob in ((0, ob0), (1, ob1)):
                            tm = b.pe(lambda e_, _hc=hc, _t=t, _og=og, _ob=ob, _a=a: e_.matmul(
                                b.psum_f[_ob][:, :],
                                hs_fm2[_a][:, _hc * 512 + _t * 128:_hc * 512 + _t * 128 + 128],
                                w2sb[:, _hc * 1024 + _og * 512:_hc * 1024 + _og * 512 + 512],
                                start=(_hc == 0), stop=False),
                                deps=[t_h[a], tw2, bank_o[ob0], bank_o[ob1]] if (hc == 0 and og == 0) else (),
                                inc=False)
                    oevs = []
                    for og, ob in ((0, ob0), (1, ob1)):
                        tm = b.pe(lambda e_, _og=og, _ob=ob: e_.matmul(
                            b.psum_f[_ob][:, :], st["ones_row_f"][:, :],
                            b2row[:, _og * 512:_og * 512 + 512],
                            start=False, stop=True), deps=[tb2, st["tok_ones"][0]], inc=True)
                        oev = b.act(lambda e_, _og=og, _t=t, _s=t % 2, _ob=ob: e_.activation(
                            og_tm[_s][:, _og * 512:(_og + 1) * 512], b.psum_f[_ob][:, :], AF.Copy),
                            [tm, og_store_prev[t % 2]])
                        oevs.append(oev)
                        bank_o[ob] = oev
                    tst = b.dma("scalar", b.aq[t % 2], lambda e_, _r=row0 + t * 128, _s=t % 2: e_.dma_start(
                        out=osort[_r:_r + 128, :], in_=og_tm[_s][:, :]), deps=oevs)
                    ostore_toks.append(tst)
                    og_store_prev[t % 2] = tst
                    if ip == len(pairs) - 1 and a == len(pair) - 1 and t == 3:
                        w2_free = tm
                        b2_free = tm
                        if e < 7:
                            wload[e + 1] = (next_w1, load_w2(e + 1))

    b.fence()

    res_toks = []
    R = 4
    og_tm = og_tm + [xs_tm[0]]
    resx = resx + [b.af32(1024)]
    outr = outr + [b.af32(1024)]
    gdone = [None] * R
    stprev = [None] * R
    for gt in range(NT):
        slot = gt % R
        tg = b.dma("gpsimd", b.gq[3 + slot], lambda e_, _gt=gt, _s=slot: e_.indirect_dma_start(
            out=og_tm[_s][:, :], out_offset=None,
            in_=osort[:, :], in_offset=IndirectOffsetOnAxis(slots[:, _gt:_gt + 1], 0),
            bounds_check=b.moe_shared["bnd"], oob_is_err=False), deps=[gdone[slot]])
        tx = b.dma("sync", b.sq[slot if slot < 3 else 7], lambda e_, _gt=gt, _s=slot: e_.dma_start(
            out=resx[_s][:, :], in_=x_in[_gt * 128:(_gt + 1) * 128, :]), deps=[gdone[slot]])
        to = b.dve(lambda e_, _gt=gt, _s=slot: e_.scalar_tensor_tensor(
            outr[_s][:, :], og_tm[_s][:, :], sc_half[:, _gt:_gt + 1], resx[_s][:, :],
            ALU.mult, ALU.add), [tg, tx, stprev[slot]])
        gdone[slot] = to
        tst = b.dma("scalar", b.aq[2 + slot % 2], lambda e_, _gt=gt, _s=slot: e_.dma_start(
            out=x_out[_gt * 128:(_gt + 1) * 128, :], in_=outr[_s][:, :]), deps=[to])
        stprev[slot] = tst
        res_toks.append(tst)
    return res_toks



def build_program(cfg, phases=("moe1",)):
    nc = bass.Bass("TRN2", target_bir_lowering=False)
    TOK = cfg["TOK"]
    b = B(nc, cfg)
    b.make_arenas(cfg.get("BF_COLS", 62464), cfg.get("F_COLS", 9216))

    x_ap = nc.dram_tensor("x", [TOK, D], F32, kind="ExternalInput").ap()
    y_ap = nc.dram_tensor("y", [TOK, D], F32, kind="ExternalOutput").ap()

    W = {}
    wspecs = {
        "L0_attn_W_DKV": [D, 256], "L0_attn_W_UK": [256, D], "L0_attn_W_UV": [256, D],
        "L0_attn_W_UQ": [D, D], "L0_attn_W_O": [D, D],
        "L0_ffn_w_red": [D, 16], "L0_ffn_wg": [8, 16],
        "L0_ffn_W1": [8, D, 2048], "L0_ffn_b1": [8, 2048],
        "L0_ffn_W2": [8, 2048, D], "L0_ffn_b2": [8, D],
        "L1_attn_w_qkv": [D, 3 * D], "L1_attn_b_qkv": [3 * D],
        "L1_attn_w_gate": [D, D], "L1_attn_b_gate": [D],
        "L1_attn_w_proj": [D, D], "L1_attn_b_proj": [D],
        "L1_attn_ln_g": [D], "L1_attn_ln_b": [D],
        "L1_ffn_W1": [L1_H, D], "L1_ffn_b1": [L1_H],
        "L1_ffn_W2": [D, L1_H], "L1_ffn_b2": [D],
        "L1_ffn_scale": [1],
    }
    need = set()
    for p in phases:
        if p.startswith("moe"):
            need.update(k for k in wspecs if k.startswith("L0_ffn"))
        elif p == "mla":
            need.update(k for k in wspecs if k.startswith("L0_attn"))
        elif p.startswith("lff"):
            need.update(k for k in wspecs if k.startswith("L1_ffn"))
        elif p == "lat":
            need.update(k for k in wspecs if k.startswith("L1_attn"))
    for k in sorted(need):
        W[k] = nc.dram_tensor(k, wspecs[k], F32, kind="ExternalInput").ap()

    scratch = {}
    NSROW = 8 * cfg["CAP"]
    if any(p.startswith("moe") for p in phases):
        scratch["xn_sorted"] = nc.dram_tensor("xn_sorted", [NSROW, D], BF16, kind="Internal").ap()
        scratch["osort"] = nc.dram_tensor("osort", [NSROW, D], BF16, kind="Internal").ap()
    if any(p.startswith("lff") for p in phases):
        scratch["w1t"] = nc.dram_tensor("w1t_scr", [D, L1_H], BF16, kind="Internal").ap()
        scratch["w2t"] = nc.dram_tensor("w2t_scr", [L1_H, D], BF16, kind="Internal").ap()
    if "mla" in phases:
        scratch["qscr"] = nc.dram_tensor("qscr", [D, TOK], BF16, kind="Internal").ap()
        scratch["wuq_b"] = nc.dram_tensor("wuq_b", [D, D], BF16, kind="Internal").ap()
        scratch["wo_b"] = nc.dram_tensor("wo_b", [D, D], BF16, kind="Internal").ap()
    if "lat" in phases:
        scratch["wqkv_b"] = nc.dram_tensor("wqkv_b", [D, 3 * D], BF16, kind="Internal").ap()
        scratch["wg_b"] = nc.dram_tensor("wg_b", [D, D], BF16, kind="Internal").ap()
        scratch["wp_b"] = nc.dram_tensor("wp_b", [D, D], BF16, kind="Internal").ap()
    if any(p.startswith("moe") for p in phases):
        scratch["w1e_b"] = nc.dram_tensor("w1e_b", [8, D, 2048], BF16, kind="Internal").ap()
        scratch["w2e_b"] = nc.dram_tensor("w2e_b", [8, 2048, D], BF16, kind="Internal").ap()
    xa = nc.dram_tensor("xbufA", [TOK, D], F32, kind="Internal").ap()
    xb_ = nc.dram_tensor("xbufB", [TOK, D], F32, kind="Internal").ap()

    preamble(b, W)
    b.ts_bf = TStep(b, "tsb", BF16)
    if any(p.startswith("lff") for p in phases):
        prep_l1w(b, W, scratch)
    b.fence()

    wc = {}
    b.cast_sched = []
    def cast_group(gname, names, items, chunk=512):
        sem = Sem(nc, f"cast_{gname}")
        n_dma = 0
        closures = []
        for (src_ap, dst_ap, rows) in items:
            for r0 in range(0, rows, chunk):
                r1 = min(rows, r0 + chunk)
                closures.append(lambda _a=src_ap, _d=dst_ap, _r0=r0, _r1=r1, _s=sem: b.op(
                    "gpsimd", lambda e: e.dma_start(out=_d[_r0:_r1, :], in_=_a[_r0:_r1, :]),
                    deps=(), inc=_s, inc_by=16))
                n_dma += 1
        tok = (sem, 16 * n_dma)
        for nm in names:
            wc[nm] = tok
        b.cast_sched.append(closures)
    if any(p.startswith("moe") for p in phases):
        for e_i in range(8):
            cast_group(f"moe{e_i}", [f"w1_{e_i}", f"w2_{e_i}"], [
                (W["L0_ffn_W1"][e_i], scratch["w1e_b"][e_i], D),
                (W["L0_ffn_W2"][e_i], scratch["w2e_b"][e_i], 2048)])
    if "mla" in phases:
        cast_group("mla", ["wuq", "wo"], [
            (W["L0_attn_W_UQ"], scratch["wuq_b"], D),
            (W["L0_attn_W_O"], scratch["wo_b"], D)])
    if "lat" in phases:
        cast_group("lat", ["wqkv", "wg", "wp"], [
            (W["L1_attn_w_qkv"], scratch["wqkv_b"], D),
            (W["L1_attn_w_gate"], scratch["wg_b"], D),
            (W["L1_attn_w_proj"], scratch["wp_b"], D)])
    b.wcast = wc

    def drain_casts(k=1):
        for _ in range(k):
            if b.cast_sched:
                for cl in b.cast_sched.pop(0):
                    cl()
    b.drain_casts = drain_casts
    if not any(p.startswith("moe") for p in phases):
        b.drain_casts(len(b.cast_sched))

    chain = [x_ap]
    bufs = [xa, xb_]
    for i in range(len(phases) - 1):
        chain.append(bufs[i % 2])
    chain.append(y_ap)

    last_toks = None
    for i, p in enumerate(phases):
        x_in, x_out = chain[i], chain[i + 1]
        if p.startswith("moe"):
            last_toks = emit_moe(b, x_in, x_out, W, scratch, ph=p)
        elif p == "mla":
            last_toks = emit_mla(b, x_in, x_out, W, scratch)
        elif p.startswith("lff"):
            last_toks = emit_lff(b, x_in, x_out, W, scratch, ph=p)
        elif p == "lat":
            last_toks = emit_lat(b, x_in, x_out, W, scratch)
        else:
            raise ValueError(p)
        b.fence()

    def final_wait(e, toks=[t for t in (last_toks or [])]):
        for sem, v in toks:
            e.wait_ge(sem.h, v)
    b.streams["sync"].append(final_wait)

    b.assemble()
    nc.finalize()
    return nc


def make_cfg(TOK, cgrp=CGRP):
    NG = TOK // 512
    return dict(TOK=TOK, NG=NG, NT=TOK // 128, CAP=cgrp * NG, CGRP=cgrp,
                NSEQ=max(1, TOK // 2048), SEQ=min(TOK, 2048))


PHASES_ALL = ("moe1", "mla", "moe2", "lff1", "lat", "lff2")


def kernel(**inputs):
    x = np.ascontiguousarray(np.asarray(inputs["L0_attn_x"]))
    Bt, T, Dm = x.shape
    n_cores = 8
    per = Bt // n_cores
    cfg = make_cfg(per * T, cgrp=96)
    nc = build_program(cfg, phases=PHASES_ALL)
    wnames = [k for k in inputs if k != "L0_attn_x"]
    base = {}
    for k in wnames:
        a = np.ascontiguousarray(np.asarray(inputs[k], dtype=np.float32))
        if a.ndim == 0:
            a = a.reshape(1)
        base[k] = a
    in_maps = []
    for c in range(n_cores):
        m = dict(base)
        m["x"] = np.ascontiguousarray(x[c * per:(c + 1) * per].reshape(per * T, Dm))
        in_maps.append(m)
    res = run_bass_kernel_spmd(nc, in_maps, core_ids=list(range(n_cores)))
    out = np.concatenate([r["y"].reshape(per, T, Dm) for r in res.results], axis=0)
    return out.astype(np.float32)



def prep_l1w(b, W, scratch):
    nc = b.nc
    st = b.st
    b.arena_reset()
    wrow = [b.abf(4096) for i in range(2)]
    wcol = [b.abf(4096) for i in range(2)]
    rdone = [None, None]
    cdone = [None, None]
    bank_done = None
    st_last = [None, None]

    def trans_chunks(src_ap, dst_dram, n_in_chunks, in_cols, dst_rows):
        nonlocal rdone, cdone, bank_done
        dstr = dst_dram.rearrange("(c p) o -> p c o", p=128)
        for rc in range(n_in_chunks):
            sl = rc % 2
            tl = b.dma("gpsimd", b.gq[5], lambda e, _rc=rc, _sl=sl, _s=src_ap, _w=in_cols: e.dma_start(
                out=wrow[_sl][:, 0:_w], in_=_s[_rc * 128:(_rc + 1) * 128, :]),
                deps=[rdone[sl]])
            nblk = in_cols // 1024
            csl = rc % 2
            evs = []
            for blk in range(nblk):
                tt = None
                for c in range(8):
                    cc = blk * 8 + c
                    tt = b.pe(lambda e, _cc=cc, _c=c, _sl=sl: e.transpose(
                        b.psum_b[1][:, _c * 128:(_c + 1) * 128],
                        wrow[_sl][:, _cc * 128:(_cc + 1) * 128], st["id_b"][:, :]),
                        deps=[tl, bank_done] + st["tok_id"], inc=(c == 7))
                tev = b.dve(lambda e, _csl=csl, _blk=blk: e.tensor_copy(
                    wcol[_csl][:, _blk * 1024:(_blk + 1) * 1024].rearrange("p (c t) -> p c t", c=8),
                    b.psum_b[1][:, :].rearrange("p (c t) -> p c t", c=8)),
                    [tt, cdone[csl]])
                bank_done = tev
                evs.append(tev)
            tst = b.dma("scalar", b.aq[4 + rc % 2], lambda e, _rc=rc, _csl=csl, _nb=nblk: e.dma_start(
                out=dstr[:, 0:8 * _nb, _rc * 128:(_rc + 1) * 128],
                in_=wcol[_csl][:, 0:1024 * _nb].rearrange("p (c t) -> p c t", c=8 * _nb)),
                deps=evs)
            st_last[rc % 2] = tst
            cdone[csl] = tst
            rdone[sl] = evs[-1]
    trans_chunks(W["L1_ffn_W1"], scratch["w1t"], 32, 1024, None)
    trans_chunks(W["L1_ffn_W2"], scratch["w2t"], 8, 4096, None)
    b.l1w_tok = [t for t in st_last if t is not None]



def emit_lff(b, x_in, x_out, W, scratch, ph):
    nc = b.nc
    st = b.st
    cfg = b.cfg
    NG = cfg["NG"]
    SEQ = cfg["SEQ"]
    GPS = SEQ // 512
    w1t, w2t = scratch["w1t"], scratch["w2t"]

    b.arena_reset()
    w1win = [b.abf(8 * 1024) for _ in range(2)]
    w2win = [b.abf(8 * 1024) for _ in range(2)]
    hs = [b.abf(8 * 512) for _ in range(2)]
    xnf = [b.abf(8 * 512) for _ in range(2)]
    evo = b.abf(128)
    od3p = b.abf(8 * 4)
    od4p = b.abf(8 * 4)
    m2a = b.abf(4 * 1024)
    m2c = m2a[0:4, :]
    w4a = b.abf(4096)
    w4blk = w4a[0:4, :]
    erow = w4a[32:33, 0:1024]
    erow2 = w4a[64:65, 0:1024]
    acc = b.af32(4 * 1024)
    t1a = b.abf(512)
    t1all = t1a[0:4, :]
    t2a = b.abf(4 * 512)
    t2sb = t2a[0:4, :]
    resx = [b.af32(1024) for _ in range(2)]

    ts = b.ts_bf
    pre = [st["tok_cst"], st["tok_l1ffn_bias"][0], st["tok_l1ffn_bias"][1],
           st["tok_l1ffn_bias"][2], st["tok_ones"][0]] + st["tok_id"]

    xnf_done = [[], []]
    seq_tok = {}
    bank = {"tr": None, 0: None, 1: None, 2: None, 3: None, 4: None, 5: None}
    w1done = [None, None]
    w2done = [None, None]
    hs_done = [None, None]
    acc_done = [None] * 4
    rx_done = [None, None]
    res_toks = []

    for g in range(NG):
        gsl = g % 2
        xnf_g = xnf[gsl]
        xnf_r = xnf_g.rearrange("p (c t) -> p c t", c=8)
        for j in range(4):
            slot = j % 2
            xn, t_xn, xt, tl = ts.tile(x_in, g * 512 + j * 128, slot, extra_x_deps=pre)
            tev = ts.transpose_to(xn, t_xn, xnf_r[:, :, j * 128:(j + 1) * 128],
                                  None, extra=xnf_done[gsl], pb=j % 2)
            ts.xt_done[slot] = t_xn
            ts.xn_done[slot] = tev
            t_fm = tev

        if g % GPS == 0:
            prev_done = res_toks[-1] if res_toks else None
            t_evo = b.dve(lambda e, _x=xnf_r: e.tensor_copy(
                evo[:, :].rearrange("p (c k) -> p c k", c=8), _x[:, :, 1:17]), [t_fm, prev_done])
            t_o3 = b.dve(lambda e: e.tensor_copy(
                od3p[:, :].rearrange("p (c k) -> p c k", c=8),
                evo[:, :].rearrange("p (c k) -> p c k", c=8)[:, :, 1:9:2]), [t_evo])
            t_o4 = b.dve(lambda e: e.tensor_copy(
                od4p[:, :].rearrange("p (c k) -> p c k", c=8),
                evo[:, :].rearrange("p (c k) -> p c k", c=8)[:, :, 9:16:2]), [t_evo])
            t_mz = b.gp(lambda e: e.memset(w4a[:, :], 0.0), [prev_done])
            t_mz2 = b.gp(lambda e: e.memset(m2a[:, :], 0.0), [prev_done])
            t_ev_prev = None
            w4tok = None
            m2tok = None
            for e_i in range(4):
                for colk, dst_is_w4 in [(2 * e_i, True), (8 + 2 * e_i, False)]:
                    tt = None
                    for c in range(8):
                        tt = b.pe(lambda e, _c=c, _k=colk: e.transpose(
                            b.psum_b[1][0:1, _c * 128:(_c + 1) * 128],
                            evo[:, :].rearrange("p (c k) -> p c k", c=8)[:, _c, _k:_k + 1],
                            st["id_b"][:, :]), deps=[t_evo, t_ev_prev, b.pbd[1]], inc=(c == 7))
                    t_er = b.dve(lambda e: e.tensor_single_scalar(
                        erow[:, :], b.psum_b[1][0:1, 0:1024], st["scale_t"][0:1, 0:1], ALU.mult),
                        [tt, t_ev_prev, t_mz])
                    b.pbd[1] = t_er
                    t_ev_prev = t_er
                    if dst_is_w4:
                        w4tok = b.dma("gpsimd", b.gq[5], lambda e, _e=e_i: e.dma_start(
                            out=w4blk[_e:_e + 1, _e * 1024:(_e + 1) * 1024], in_=erow[:, :]),
                            deps=[t_er, t_mz])
                        t_ev_prev = w4tok
                    else:
                        t_es = b.dve(lambda e: e.tensor_copy(
                            erow2[:, :],
                            erow[:, :].rearrange("p (k c) -> p c k", c=4)), [t_er, t_mz2])
                        for cc in range(4):
                            m2tok = b.dma("gpsimd", b.gq[5], lambda e, _e=e_i, _cc=cc: e.dma_start(
                                out=m2c[_e:_e + 1, _cc * 1024 + _e * 256:_cc * 1024 + (_e + 1) * 256],
                                in_=erow2[:, _cc * 256:(_cc + 1) * 256]), deps=[t_es, t_mz2])
                        t_ev_prev = m2tok
            seq_tok = {"w4": w4tok, "m2": m2tok, "o3": t_o3, "o4": t_o4}

        tm = None
        for dc in range(8):
            tm = b.pe(lambda e, _dc=dc, _x=xnf_g: e.matmul(
                b.psum_f[2][0:4, :], od3p[:, _dc * 4:(_dc + 1) * 4], _x[:, _dc * 512:(_dc + 1) * 512],
                start=(_dc == 0), stop=(_dc == 7)),
                deps=[t_fm, seq_tok["o3"], bank[2]], inc=(dc == 7))
        t_t1 = b.dve(lambda e: e.tensor_copy(t1all[:, :], b.psum_f[2][0:4, :]), [tm])
        bank[2] = t_t1

        for blk in range(4):
            wsl = blk % 2
            tw1 = b.dma("sync", b.sq[4 + wsl], lambda e, _blk=blk, _w=wsl: e.dma_start(
                out=w1win[_w][:, :].rearrange("p (c h) -> p c h", c=8),
                in_=w1t.rearrange("(c p) h -> p c h", p=128)[:, :, _blk * 1024:(_blk + 1) * 1024]),
                deps=[w1done[wsl]] + b.l1w_tok)
            t_h = None
            hsb = hs[wsl]
            for hl in range(8):
                hc = blk * 8 + hl
                pb = hl % 2
                for dc in range(8):
                    b.pe(lambda e, _dc=dc, _hl=hl, _pb=pb, _x=xnf_g, _w=wsl: e.matmul(
                        b.psum_f[_pb][:, :],
                        w1win[_w][:, _dc * 1024 + _hl * 128:_dc * 1024 + _hl * 128 + 128],
                        _x[:, _dc * 512:(_dc + 1) * 512],
                        start=(_dc == 0), stop=False),
                        deps=[t_fm, tw1, bank[pb]] if dc == 0 else (), inc=False)
                tm = b.pe(lambda e, _hc=hc, _pb=pb: e.matmul(
                    b.psum_f[_pb][:, :], w4blk[:, _hc * 128:(_hc + 1) * 128], t1all[:, :],
                    start=False, stop=True), deps=[seq_tok["w4"], t_t1], inc=True)
                if hl == 7:
                    w1done[wsl] = tm
                t_h = b.act(lambda e, _hc=hc, _hl=hl, _pb=pb, _hs=hsb: e.activation(
                    _hs[:, _hl * 512:(_hl + 1) * 512], b.psum_f[_pb][:, :], AF.Gelu,
                    bias=st["l1b1_col"][:, _hc:_hc + 1]), [tm, hs_done[wsl]])
                bank[pb] = t_h

            tm = None
            for jc in range(8):
                tm = b.pe(lambda e, _jc=jc, _hs=hsb: e.matmul(
                    b.psum_f[2][0:4, :], od4p[:, _jc * 4:(_jc + 1) * 4],
                    _hs[:, _jc * 512:(_jc + 1) * 512],
                    start=(_jc == 0), stop=(_jc == 7)),
                    deps=[t_h, seq_tok["o4"], bank[2]], inc=(jc == 7))
            tt2 = b.dve(lambda e, _blk=blk: e.tensor_copy(
                t2sb[0:4, _blk * 512:(_blk + 1) * 512], b.psum_f[2][0:4, :]), [tm])
            bank[2] = tt2
            if blk == 3:
                t_t2 = tt2

            tw2 = b.dma("sync", b.sq[6], lambda e, _blk=blk, _w=wsl: e.dma_start(
                out=w2win[_w][:, :].rearrange("p (c o) -> p c o", c=8),
                in_=w2t.rearrange("(c p) o -> p c o", p=128)[:, _blk * 8:(_blk + 1) * 8, :]),
                deps=[w2done[wsl]] + b.l1w_tok)
            w2rd = None
            hs_rd = None
            for t in range(4):
                tsl = t % 2
                if blk == 3:
                    trx = b.dma("sync", b.sq[tsl], lambda e, _g=g, _t=t, _s=tsl: e.dma_start(
                        out=resx[_s][:, :], in_=x_in[_g * 512 + _t * 128:_g * 512 + (_t + 1) * 128, :]),
                        deps=[rx_done[tsl]])
                tms = {}
                ob0 = 3 + (2 * t) % 3
                ob1 = 3 + (2 * t + 1) % 3
                for hl in range(8):
                    for og in range(2):
                        ob = ob0 if og == 0 else ob1
                        tms[og] = b.pe(lambda e, _hl=hl, _t=t, _og=og, _w=wsl, _ob=ob, _hs=hsb,
                                       _st=(blk < 3 and hl == 7): e.matmul(
                            b.psum_f[_ob][:, :],
                            _hs[:, _hl * 512 + _t * 128:_hl * 512 + _t * 128 + 128],
                            w2win[_w][:, _hl * 1024 + _og * 512:_hl * 1024 + _og * 512 + 512],
                            start=(_hl == 0), stop=_st),
                            deps=[t_h, tw2, bank[ob0], bank[ob1]] if (hl == 0 and og == 0) else (),
                            inc=(blk < 3 and hl == 7))
                for og in range(2):
                    ob = ob0 if og == 0 else ob1
                    tm = tms[og]
                    w2rd = tm
                    hs_rd = tm
                    if blk == 3:
                        for cq in range(4):
                            b.pe(lambda e, _cq=cq, _t=t, _og=og, _ob=ob: e.matmul(
                                b.psum_f[_ob][:, :], t2sb[0:4, _cq * 512 + _t * 128:_cq * 512 + _t * 128 + 128],
                                m2c[:, _cq * 1024 + _og * 512:_cq * 1024 + _og * 512 + 512],
                                start=False, stop=False), deps=[t_t2, seq_tok["m2"]] if cq == 0 else (),
                                inc=False)
                        tm = b.pe(lambda e, _og=og, _ob=ob: e.matmul(
                            b.psum_f[_ob][:, :], st["ones_row_f"][:, :],
                            st["l1b2_row"][:, _og * 512:(_og + 1) * 512],
                            start=False, stop=True), deps=[], inc=True)
                        ta = b.dve(lambda e, _t=t, _og=og, _ob=ob: e.tensor_add(
                            acc[:, _t * 1024 + _og * 512:_t * 1024 + (_og + 1) * 512],
                            acc[:, _t * 1024 + _og * 512:_t * 1024 + (_og + 1) * 512],
                            b.psum_f[_ob][:, :]), [tm])
                        to = b.dve(lambda e, _t=t, _og=og, _s=tsl: e.scalar_tensor_tensor(
                            resx[_s][:, _og * 512:(_og + 1) * 512],
                            acc[:, _t * 1024 + _og * 512:_t * 1024 + (_og + 1) * 512], 0.5,
                            resx[_s][:, _og * 512:(_og + 1) * 512],
                            ALU.mult, ALU.add), [ta, trx])
                        bank[ob] = to
                        acc_done[t] = to
                        if og == 1:
                            tst = b.dma("scalar", b.aq[tsl], lambda e, _g=g, _t=t, _s=tsl: e.dma_start(
                                out=x_out[_g * 512 + _t * 128:_g * 512 + (_t + 1) * 128, :],
                                in_=resx[_s][:, :]), deps=[to])
                            rx_done[tsl] = tst
                            res_toks.append(tst)
                    else:
                        if blk == 0:
                            ta = b.dve(lambda e, _t=t, _og=og, _ob=ob: e.tensor_copy(
                                acc[:, _t * 1024 + _og * 512:_t * 1024 + (_og + 1) * 512],
                                b.psum_f[_ob][:, :]), [tm, acc_done[t]])
                        else:
                            ta = b.dve(lambda e, _t=t, _og=og, _ob=ob: e.tensor_add(
                                acc[:, _t * 1024 + _og * 512:_t * 1024 + (_og + 1) * 512],
                                acc[:, _t * 1024 + _og * 512:_t * 1024 + (_og + 1) * 512],
                                b.psum_f[_ob][:, :]), [tm])
                        bank[ob] = ta
            w2done[wsl] = w2rd
            hs_done[wsl] = hs_rd
        xnf_done[gsl] = [t_h, t_t1]
    return res_toks







def emit_lat(b, x_in, x_out, W, scratch):
    nc = b.nc
    st = b.st
    cfg = b.cfg
    NG = cfg["NG"]
    GPS = cfg["SEQ"] // 512

    b.arena_reset()
    xnf = [b.abf(8 * 512) for _ in range(2)]
    q_fm = b.abf(8 * 512)
    k_fm = b.abf(8 * 512)
    v_ext = b.abf(4 * 1032)
    av = b.abf(4 * 512)
    ktm = b.abf(4 * 1024)
    s_ext = b.abf(8 * 1025)
    wqk = [b.abf(8 * 128) for _ in range(2)]
    wv = [b.abf(8 * 512) for _ in range(2)]
    wp = b.abf(8 * 1024)
    nrm = [b.abf(1024) for _ in range(2)]
    nrmf = b.abf(1024)
    tmp1s = [b.abf(512) for _ in range(2)]
    tmp2s = [b.abf(512) for _ in range(2)]
    tmp3s = [b.abf(512) for _ in range(2)]
    gates = [b.abf(512) for _ in range(2)]
    S32 = b.af32(8 * 1024)
    rx = b.af32(1024)
    sc = nc.alloc_sbuf_tensor("lat_sc", [128, 16], F32)
    kcs = nc.alloc_sbuf_tensor("lat_kcs", [128, 8], F32)

    ts = b.ts_bf
    pre = [st["tok_cst"], st["tok_ones"][0], st["tok_ones"][1], st["tok_bvb"]] \
        + st["tok_id"] + st["tok_mk"] + st["tok_l1attn_bias"]
    wq_ap = scratch["wqkv_b"]
    wg_ap = scratch["wg_b"]
    wp_ap = scratch["wp_b"]

    bank = {"tr": None, 0: None, 1: None, 2: None, 3: None, 4: None, 5: None, "b1": None}
    xnf_done = [[], []]
    wqk_done = [None, None]
    wv_done = [None, None]
    wp_done = None
    vext_done = None
    av_done = None
    q_used = None
    k_used = None
    S_done = []
    sext_rd = None
    nrm_done = [None, None]
    nrmf_done = None
    rx_done = None
    tmp_done = {}
    res_toks = []
    wcnt = [0]

    t_vones = b.gp(lambda e: e.memset(
        v_ext[:, :].rearrange("p (t c) -> p t c", t=4)[:, :, 1024:1025], 1.0), pre)
    vones_tok = t_vones

    for c in range(NG):
        cin = c % GPS
        csl = c % 2
        xnf_g = xnf[csl]
        xnf_r = xnf_g.rearrange("p (c t) -> p c t", c=8)
        for j in range(4):
            slot = j % 2
            xn, t_xn, xt, tl = ts.tile(x_in, c * 512 + j * 128, slot, extra_x_deps=pre, ln2=True)
            tev = ts.transpose_to(xn, t_xn, xnf_r[:, :, j * 128:(j + 1) * 128],
                                  None, extra=xnf_done[csl],
                                  gain=st["lng_col"], beta=st["lnb_col"], pb=j % 2)
            ts.xt_done[slot] = t_xn
            ts.xn_done[slot] = tev
            t_fm = tev

        if cin == 0:
            t_s0 = b.dve(lambda e: e.memset(S32[:, :], 0.0), S_done)
            t_k0 = b.dve(lambda e: e.memset(kcs[:, :], 0.0), S_done)
            S_done = [t_s0, t_k0]

        def win_load(src_ap, col0, extra=()):
            wsl = wcnt[0] % 2
            wcnt[0] += 1
            q = ("sync", b.sq[3]) if wsl == 0 else ("scalar", b.aq[5])
            tw = b.dma(q[0], q[1], lambda e, _c0=col0, _w=wsl, _s=src_ap: e.dma_start(
                out=wqk[_w][:, :].rearrange("p (d f) -> p d f", d=8),
                in_=_s.rearrange("(d p) f -> p d f", p=128)[:, :, _c0:_c0 + 128]),
                deps=[wqk_done[wsl], b.wcast.get("wqkv"), b.wcast.get("wg")] + list(extra))
            return tw, wsl

        def fproj(src_ap, col0, pb, extra_mm=()):
            tw, wsl = win_load(src_ap, col0)
            tm = None
            for dc in range(8):
                tm = b.pe(lambda e, _dc=dc, _pb=pb, _w=wsl, _x=xnf_g: e.matmul(
                    b.psum_f[_pb][:, :], wqk[_w][:, _dc * 128:(_dc + 1) * 128],
                    _x[:, _dc * 512:(_dc + 1) * 512],
                    start=(_dc == 0), stop=(_dc == 7)),
                    deps=[t_fm, tw, bank[pb]] + list(extra_mm), inc=(dc == 7))
            wqk_done[wsl] = tm
            return tm

        for fc in range(8):
            fsl = fc % 2
            tmp1, tmp2, tmp3, gate = tmp1s[fsl], tmp2s[fsl], tmp3s[fsl], gates[fsl]
            pb_g = fsl
            pb_k = 2 + fsl
            pb_q = fsl
            tm = fproj(wg_ap, fc * 128, pb_g)
            t_g = b.act(lambda e, _fc=fc, _pb=pb_g, _o=gate: e.activation(
                _o[:, :], b.psum_f[_pb][:, :], AF.Sigmoid,
                bias=st["bg_col"][:, _fc:_fc + 1]), [tm, tmp_done.get(("gate", fsl))])
            bank[pb_g] = t_g
            tm = fproj(wq_ap, 1024 + fc * 128, pb_k)
            t_kg = b.dve(lambda e, _fc=fc, _pb=pb_k, _t1=tmp1, _g=gate: e.scalar_tensor_tensor(
                _t1[:, :], b.psum_f[_pb][:, :], st["bk_col"][:, _fc:_fc + 1], _g[:, :],
                ALU.add, ALU.mult), [tm, t_g, tmp_done.get(("t1", fsl))])
            tmp_done[("gate", fsl)] = t_kg
            t_r1 = b.act(lambda e, _t1=tmp1, _t2=tmp2: e.activation(_t2[:, :], _t1[:, :], AF.Relu,
                                                                    scale=-1.0),
                         [t_kg, tmp_done.get(("t2", fsl))])
            t_e = b.act(lambda e, _t2=tmp2, _t3=tmp3: e.activation(_t3[:, :], _t2[:, :], AF.Exp,
                                                                   scale=-1.0),
                        [t_r1, tmp_done.get(("t3", fsl))])
            t_r2 = b.act(lambda e, _t1=tmp1, _t2=tmp2: e.activation(_t2[:, :], _t1[:, :], AF.Relu),
                         [t_e])
            t_k = b.dve(lambda e, _fc=fc, _t2=tmp2, _t3=tmp3: e.tensor_add(
                k_fm[:, _fc * 512:(_fc + 1) * 512], _t2[:, :], _t3[:, :]), [t_r2, t_e, k_used])
            bank[pb_k] = t_kg
            tm = fproj(wq_ap, fc * 128, pb_q)
            t_r3 = b.act(lambda e, _fc=fc, _pb=pb_q, _t1=tmp1: e.activation(
                _t1[:, :], b.psum_f[_pb][:, :], AF.Relu, scale=-1.0,
                bias=st["nbq_col"][:, _fc:_fc + 1]), [tm, t_k])
            t_e2 = b.act(lambda e, _t1=tmp1, _t3=tmp3: e.activation(_t3[:, :], _t1[:, :], AF.Exp,
                                                                    scale=-1.0),
                         [t_r3, t_k])
            t_r4 = b.act(lambda e, _fc=fc, _pb=pb_q, _t1=tmp1: e.activation(
                _t1[:, :], b.psum_f[_pb][:, :], AF.Relu,
                bias=st["bq_col"][:, _fc:_fc + 1]), [t_e2])
            t_q = b.dve(lambda e, _fc=fc, _t1=tmp1, _t3=tmp3: e.tensor_add(
                q_fm[:, _fc * 512:(_fc + 1) * 512], _t1[:, :], _t3[:, :]), [t_r4, t_e2, q_used])
            bank[pb_q] = t_r4
            tmp_done[("t1", fsl)] = t_q
            tmp_done[("t2", fsl)] = t_k
            tmp_done[("t3", fsl)] = t_q
        t_qk = t_q

        for og in range(2):
            wsl = og
            tw_v = b.dma("scalar", b.aq[4], lambda e, _og=og, _w=wsl: e.dma_start(
                out=wv[_w][:, :].rearrange("p (d f) -> p d f", d=8),
                in_=wq_ap.rearrange("(d p) f -> p d f", p=128)[:, :, 2048 + _og * 512:2048 + (_og + 1) * 512]),
                deps=[wv_done[wsl], b.wcast.get("wqkv")])
            for tt in range(4):
                pb = 2 + tt % 2
                tm = None
                for dc in range(8):
                    tm = b.pe(lambda e, _dc=dc, _tt=tt, _pb=pb, _w=wsl, _x=xnf_g: e.matmul(
                        b.psum_f[_pb][:, :], _x[:, _dc * 512 + _tt * 128:_dc * 512 + _tt * 128 + 128],
                        wv[_w][:, _dc * 512:(_dc + 1) * 512],
                        start=(_dc == 0), stop=False),
                        deps=[t_fm, tw_v, bank[pb]], inc=False)
                tm = b.pe(lambda e, _og=og, _pb=pb: e.matmul(
                    b.psum_f[_pb][:, :], st["ones_row_b"][:, :],
                    st["bv_row_b"][:, _og * 512:(_og + 1) * 512],
                    start=False, stop=True), deps=[], inc=True)
                if tt == 3:
                    wv_done[wsl] = tm
                t_ve = b.act(lambda e, _tt=tt, _og=og, _pb=pb: e.activation(
                    v_ext[:, _tt * 1032 + _og * 512:_tt * 1032 + _og * 512 + 512],
                    b.psum_f[_pb][:, :], AF.Copy), [tm, vext_done, vones_tok])
                bank[pb] = t_ve
        t_v = t_ve

        t_av = []
        for stl in range(4):
            pb = stl % 2
            tm = None
            for dc in range(8):
                tm = b.pe(lambda e, _dc=dc, _stl=stl, _pb=pb: e.matmul(
                    b.psum_f[_pb][:, :], k_fm[:, _dc * 512 + _stl * 128:_dc * 512 + _stl * 128 + 128],
                    q_fm[:, _dc * 512:(_dc + 1) * 512],
                    start=(_dc == 0), stop=(_dc == 7)),
                    deps=[t_qk, bank[pb]], inc=(dc == 7))
            t_c2 = b.dve(lambda e, _stl=stl, _pb=pb: e.tensor_mul(
                av[:, _stl * 512:(_stl + 1) * 512], b.psum_f[_pb][:, :],
                st["mk_b"][:, _stl * 512:(_stl + 1) * 512]), [tm, av_done])
            bank[pb] = t_c2
            t_av.append(t_c2)

        tw_p = None
        for dc in range(0, 8, 4):
            tw_p = b.dma("sync", b.sq[6], lambda e, _dc=dc: e.dma_start(
                out=wp[:, _dc * 1024:(_dc + 4) * 1024].rearrange("p (c k) -> p c k", c=4),
                in_=wp_ap.rearrange("(c p) f -> p c f", p=128)[:, _dc:_dc + 4, :]),
                deps=[wp_done, b.wcast.get("wp")])

        out_mm_toks = []
        for tt in range(4):
            nsl = tt % 2
            groups = []
            tm4 = tm5 = tmd = None
            first = True
            for stl in range(tt + 1):
                lastin = (cin == 0 and stl == tt)
                tm4 = b.pe(lambda e, _stl=stl, _tt=tt, _f=first, _l=lastin: e.matmul(
                    b.psum_f[4][:, :], av[:, _stl * 512 + _tt * 128:_stl * 512 + _tt * 128 + 128],
                    v_ext[:, _stl * 1032:_stl * 1032 + 512],
                    start=_f, stop=_l), deps=[t_av[stl], t_v, bank[4]], inc=lastin)
                tm5 = b.pe(lambda e, _stl=stl, _tt=tt, _f=first, _l=lastin: e.matmul(
                    b.psum_f[5][:, :], av[:, _stl * 512 + _tt * 128:_stl * 512 + _tt * 128 + 128],
                    v_ext[:, _stl * 1032 + 512:_stl * 1032 + 1024],
                    start=_f, stop=_l), deps=[bank[5]], inc=lastin)
                tmd = b.pe(lambda e, _stl=stl, _tt=tt, _f=first, _l=lastin: e.matmul(
                    b.psum_f[3][:, 0:1], av[:, _stl * 512 + _tt * 128:_stl * 512 + _tt * 128 + 128],
                    v_ext[:, _stl * 1032 + 1024:_stl * 1032 + 1025],
                    start=_f, stop=_l), deps=[bank[3]], inc=lastin)
                first = False
            if cin > 0:
                for dc in range(8):
                    last = dc == 7
                    tm4 = b.pe(lambda e, _dc=dc, _tt=tt, _l=last: e.matmul(
                        b.psum_f[4][:, :], q_fm[:, _dc * 512 + _tt * 128:_dc * 512 + _tt * 128 + 128],
                        s_ext[:, _dc * 1025:_dc * 1025 + 512],
                        start=False, stop=_l), deps=[t_qk] + S_done, inc=last)
                    tm5 = b.pe(lambda e, _dc=dc, _tt=tt, _l=last: e.matmul(
                        b.psum_f[5][:, :], q_fm[:, _dc * 512 + _tt * 128:_dc * 512 + _tt * 128 + 128],
                        s_ext[:, _dc * 1025 + 512:_dc * 1025 + 1024],
                        start=False, stop=_l), deps=[], inc=last)
                    tmd = b.pe(lambda e, _dc=dc, _tt=tt, _l=last: e.matmul(
                        b.psum_f[3][:, 0:1], q_fm[:, _dc * 512 + _tt * 128:_dc * 512 + _tt * 128 + 128],
                        s_ext[:, _dc * 1025 + 1024:_dc * 1025 + 1025],
                        start=False, stop=_l), deps=[], inc=last)
            out_mm_toks += [tm4, tm5, tmd]
            t_d1 = b.dve(lambda e: e.tensor_single_scalar(sc[:, 0:1], b.psum_f[3][:, 0:1],
                                                          DEN_EPS, ALU.add), [tmd])
            t_rc = b.dve(lambda e: e.reciprocal(sc[:, 1:2], sc[:, 0:1]), [t_d1])
            t_n0 = b.dve(lambda e, _n=nsl: e.tensor_scalar(
                nrm[_n][:, 0:512], b.psum_f[4][:, :], sc[:, 1:2], None, ALU.mult),
                [t_rc, tm4, nrm_done[nsl]])
            t_n1 = b.dve(lambda e, _n=nsl: e.tensor_scalar(
                nrm[_n][:, 512:1024], b.psum_f[5][:, :], sc[:, 1:2], None, ALU.mult),
                [t_rc, tm5])
            bank[4] = t_n0
            bank[5] = t_n1
            bank[3] = t_d1
            tt_tr = None
            for ch in range(8):
                tt_tr = b.pe(lambda e, _ch=ch, _n=nsl: e.transpose(
                    b.psum_b[1][:, _ch * 128:(_ch + 1) * 128],
                    nrm[_n][:, _ch * 128:(_ch + 1) * 128], st["id_b"][:, :]),
                    deps=[t_n0, t_n1, b.pbd[1]], inc=(ch == 7))
            t_nf = b.dve(lambda e: e.tensor_copy(
                nrmf[:, :].rearrange("p (c t) -> p c t", c=8),
                b.psum_b[1][:, :].rearrange("p (c t) -> p c t", c=8)), [tt_tr, nrmf_done])
            b.pbd[1] = t_nf
            nrm_done[nsl] = t_nf
            trx = b.dma("sync", b.sq[2], lambda e, _c=c, _tt=tt: e.dma_start(
                out=rx[:, :], in_=x_in[_c * 512 + _tt * 128:_c * 512 + (_tt + 1) * 128, :]),
                deps=[rx_done])
            for og in range(2):
                pb = 4 if og == 0 else 5
                tm = None
                for dc in range(8):
                    tm = b.pe(lambda e, _dc=dc, _og=og, _pb=pb: e.matmul(
                        b.psum_f[_pb][:, :], nrmf[:, _dc * 128:(_dc + 1) * 128],
                        wp[:, _dc * 1024 + _og * 512:_dc * 1024 + _og * 512 + 512],
                        start=(_dc == 0), stop=False),
                        deps=[t_nf, tw_p, bank[pb]], inc=False)
                tm = b.pe(lambda e, _og=og, _pb=pb: e.matmul(
                    b.psum_f[_pb][:, :], st["ones_row_f"][:, :],
                    st["b_proj_row"][:, _og * 512:(_og + 1) * 512],
                    start=False, stop=True), deps=[], inc=True)
                wp_done = tm
                t_o = b.dve(lambda e, _og=og, _pb=pb: e.tensor_add(
                    rx[:, _og * 512:(_og + 1) * 512], b.psum_f[_pb][:, :],
                    rx[:, _og * 512:(_og + 1) * 512]), [tm, trx])
                bank[pb] = t_o
            tst = b.dma("scalar", b.aq[tt % 2], lambda e, _c=c, _tt=tt: e.dma_start(
                out=x_out[_c * 512 + _tt * 128:_c * 512 + (_tt + 1) * 128, :], in_=rx[:, :]),
                deps=[bank[4], bank[5]])
            rx_done = tst
            res_toks.append(tst)
            nrmf_done = tm

        if cin < GPS - 1:
            ktm_ev = []
            for stl in range(4):
                tt_tr = None
                for ch in range(8):
                    tt_tr = b.pe(lambda e, _ch=ch, _stl=stl: e.transpose(
                        b.psum_b[1][:, _ch * 128:(_ch + 1) * 128],
                        k_fm[:, _ch * 512 + _stl * 128:_ch * 512 + _stl * 128 + 128],
                        st["id_b"][:, :]), deps=[t_qk, b.pbd[1]], inc=(ch == 7))
                t_ke = b.dve(lambda e, _stl=stl: e.tensor_copy(
                    ktm[:, _stl * 1024:(_stl + 1) * 1024].rearrange("p (c t) -> p c t", c=8),
                    b.psum_b[1][:, :].rearrange("p (c t) -> p c t", c=8)), [tt_tr])
                b.pbd[1] = t_ke
                ktm_ev.append(t_ke)
            new_S_done = []
            for d1c in range(8):
                for og in range(2):
                    pb = 2 + (d1c * 2 + og) % 2
                    tm = None
                    for stl in range(4):
                        tm = b.pe(lambda e, _stl=stl, _d=d1c, _og=og, _pb=pb: e.matmul(
                            b.psum_f[_pb][:, :], ktm[:, _stl * 1024 + _d * 128:_stl * 1024 + _d * 128 + 128],
                            v_ext[:, _stl * 1032 + _og * 512:_stl * 1032 + _og * 512 + 512],
                            start=(_stl == 0), stop=(_stl == 3)),
                            deps=[ktm_ev[stl], t_v, bank[pb]] + out_mm_toks[-3:], inc=(stl == 3))
                    t_sa = b.dve(lambda e, _d=d1c, _og=og, _pb=pb: e.tensor_add(
                        S32[:, _d * 1024 + _og * 512:_d * 1024 + _og * 512 + 512],
                        S32[:, _d * 1024 + _og * 512:_d * 1024 + _og * 512 + 512],
                        b.psum_f[_pb][:, :]), [tm] + S_done)
                    t_sc = b.dve(lambda e, _d=d1c, _og=og: e.tensor_copy(
                        s_ext[:, _d * 1025 + _og * 512:_d * 1025 + _og * 512 + 512],
                        S32[:, _d * 1024 + _og * 512:_d * 1024 + _og * 512 + 512]), [t_sa])
                    bank[pb] = t_sa
                    new_S_done.append(t_sc)
            for dc in range(8):
                t_rs = b.dve(lambda e, _dc=dc: e.reduce_sum(
                    sc[:, 2:3], k_fm[:, _dc * 512:(_dc + 1) * 512], axis=AX.X), [t_qk])
                t_ka = b.dve(lambda e, _dc=dc: e.tensor_add(
                    kcs[:, _dc:_dc + 1], kcs[:, _dc:_dc + 1], sc[:, 2:3]), [t_rs] + S_done)
                t_kc = b.dve(lambda e, _dc=dc: e.tensor_copy(
                    s_ext[:, _dc * 1025 + 1024:_dc * 1025 + 1025], kcs[:, _dc:_dc + 1]), [t_ka])
                new_S_done.append(t_kc)
            S_done = new_S_done
            vext_done = tm
            k_used = tm
        else:
            vext_done = out_mm_toks[-3]
            k_used = t_av[-1]
        av_done = out_mm_toks[-3]
        q_used = out_mm_toks[-3]
        xnf_done[csl] = [t_v, t_qk]
    return res_toks



def emit_mla(b, x_in, x_out, W, scratch):
    nc = b.nc
    st = b.st
    cfg = b.cfg
    NG = cfg["NG"]
    SEQ = cfg["SEQ"]
    GPS = SEQ // 512
    NSEQ = cfg["TOK"] // SEQ
    NTILE_SEQ = SEQ // 128
    qscr = scratch["qscr"]

    b.arena_reset()
    wdkv = b.abf(8 * 256)
    wuk = b.abf(2 * 1024)
    wuv = b.abf(2 * 1024)
    wuq = [b.abf(8 * 128) for _ in range(2)]
    wo = b.abf(8 * 512)
    xnf = b.abf(8 * 512)
    ckv_g = b.abf(2 * 512)
    k_fm = b.abf(8 * SEQ)
    v_ext = b.abf(NTILE_SEQ * 1040)
    qld = b.abf(8 * 512)
    qst = [b.abf(512) for _ in range(2)]
    pT = [b.abf(512) for _ in range(3)]
    ctxn = b.abf(4 * 1024)
    ctxf = b.abf(1024)
    rx = [b.af32(1024) for _ in range(2)]
    sc = nc.alloc_sbuf_tensor("mla_sc", [128, 8], F32)

    ts = b.ts_bf
    pre = [st["tok_cst"], st["tok_ones"][1], st["tok_ones"][3]] + st["tok_id"] + st["tok_mk"]

    tws = []
    for dc in range(8):
        tws.append(b.dma("gpsimd", b.gq[6], lambda e, _dc=dc: e.dma_start(
            out=wdkv[:, _dc * 256:(_dc + 1) * 256],
            in_=W["L0_attn_W_DKV"][_dc * 128:(_dc + 1) * 128, :])))
    for lc in range(2):
        tws.append(b.dma("gpsimd", b.gq[6], lambda e, _lc=lc: e.dma_start(
            out=wuk[:, _lc * 1024:(_lc + 1) * 1024],
            in_=W["L0_attn_W_UK"][_lc * 128:(_lc + 1) * 128, :])))
        tws.append(b.dma("gpsimd", b.gq[6], lambda e, _lc=lc: e.dma_start(
            out=wuv[:, _lc * 1024:(_lc + 1) * 1024],
            in_=W["L0_attn_W_UV"][_lc * 128:(_lc + 1) * 128, :])))
    tw_res = tws[-1]

    bank = {"tr": None, 0: None, 1: None, 2: None, 3: None, 4: None, 5: None, "b1": None}
    xnf_done = []
    ckv_done = []
    kfm_done = []
    vext_done = []
    qst_done = [None, None]
    wuq_done = [None, None]
    qld_done = None
    pT_done = [None, None, None]
    ctxn_done = [None] * 4
    ctxf_done = None
    wo_done = None
    rx_done = [None, None]
    res_toks = []
    t_vones = b.gp(lambda e: e.memset(
        v_ext[:, :].rearrange("p (t h c) -> p t h c", t=NTILE_SEQ, h=16)[:, :, :, 64:65], 1.0), pre)

    for s in range(NSEQ):
        s0 = s * SEQ
        new_kfm_done = []
        new_vext_done = []
        new_ckv_done = []
        for g in range(GPS):
            t0 = s0 + g * 512
            xnf_r = xnf.rearrange("p (c t) -> p c t", c=8)
            for j in range(4):
                slot = j % 2
                xn, t_xn, xt, tl = ts.tile(x_in, t0 + j * 128, slot, extra_x_deps=pre)
                tev = ts.transpose_to(xn, t_xn, xnf_r[:, :, j * 128:(j + 1) * 128],
                                      None, extra=xnf_done, pb=j % 2)
                ts.xt_done[slot] = t_xn
                ts.xn_done[slot] = tev
                t_fm = tev
            t_ckv = None
            for lc in range(2):
                pb = lc
                tm = None
                for dc in range(8):
                    tm = b.pe(lambda e, _dc=dc, _lc=lc, _pb=pb: e.matmul(
                        b.psum_f[_pb][:, :], wdkv[:, _dc * 256 + _lc * 128:_dc * 256 + _lc * 128 + 128],
                        xnf[:, _dc * 512:(_dc + 1) * 512],
                        start=(_dc == 0), stop=(_dc == 7)),
                        deps=[t_fm, tw_res, bank[pb]] + ckv_done, inc=(dc == 7))
                t_ckv = b.dve(lambda e, _lc=lc, _pb=pb: e.tensor_copy(
                    ckv_g[:, _lc * 512:(_lc + 1) * 512], b.psum_f[_pb][:, :]), [tm])
                bank[pb] = t_ckv
            new_ckv_done = [t_ckv]
            for fc in range(8):
                pb = fc % 2
                tm = None
                for lc in range(2):
                    tm = b.pe(lambda e, _fc=fc, _lc=lc, _pb=pb: e.matmul(
                        b.psum_f[_pb][:, :], wuk[:, _lc * 1024 + _fc * 128:_lc * 1024 + _fc * 128 + 128],
                        ckv_g[:, _lc * 512:(_lc + 1) * 512],
                        start=(_lc == 0), stop=(_lc == 1)),
                        deps=[t_ckv, bank[pb]] + kfm_done, inc=(lc == 1))
                t_k = b.dve(lambda e, _fc=fc, _g=g, _pb=pb: e.tensor_copy(
                    k_fm[:, _fc * SEQ + _g * 512:_fc * SEQ + (_g + 1) * 512], b.psum_f[_pb][:, :]), [tm])
                bank[pb] = t_k
                new_kfm_done.append(t_k)
            for fc in range(8):
                wsl = fc % 2
                pb = fc % 3
                tw = b.dma("sync", b.sq[3], lambda e, _fc=fc, _w=wsl: e.dma_start(
                    out=wuq[_w][:, :].rearrange("p (d f) -> p d f", d=8),
                    in_=scratch["wuq_b"].rearrange("(d p) f -> p d f", p=128)[:, :, _fc * 128:(_fc + 1) * 128]),
                    deps=[wuq_done[wsl], b.wcast.get("wuq")])
                tm = None
                for dc in range(8):
                    tm = b.pe(lambda e, _dc=dc, _pb=pb, _w=wsl: e.matmul(
                        b.psum_f[_pb][:, :], wuq[_w][:, _dc * 128:(_dc + 1) * 128],
                        xnf[:, _dc * 512:(_dc + 1) * 512],
                        start=(_dc == 0), stop=(_dc == 7)),
                        deps=[t_fm, tw, bank[pb]], inc=(dc == 7))
                wuq_done[wsl] = tm
                t_q = b.act(lambda e, _pb=pb, _w=wsl: e.activation(
                    qst[_w][:, :], b.psum_f[_pb][:, :], AF.Copy), [tm, qst_done[wsl]])
                bank[pb] = t_q
                qst_done[wsl] = b.dma("sync", b.sq[4 + wsl], lambda e, _fc=fc, _g=g, _s=s, _w=wsl: e.dma_start(
                    out=qscr[_fc * 128:(_fc + 1) * 128, _s * SEQ + _g * 512:_s * SEQ + (_g + 1) * 512],
                    in_=qst[_w][:, :]), deps=[t_q])
            for tt in range(4):
                gt = g * 4 + tt
                pb = tt % 2
                for og in range(2):
                    tm = None
                    for lc in range(2):
                        tm = b.pe(lambda e, _tt=tt, _lc=lc, _og=og, _pb=pb: e.matmul(
                            b.psum_f[_pb][:, :], ckv_g[:, _lc * 512 + _tt * 128:_lc * 512 + _tt * 128 + 128],
                            wuv[:, _lc * 1024 + _og * 512:_lc * 1024 + _og * 512 + 512],
                            start=(_lc == 0), stop=(_lc == 1)),
                            deps=[t_ckv, bank[pb]] + vext_done, inc=(lc == 1))
                    t_v = b.act(lambda e, _gt=gt, _og=og, _pb=pb: e.activation(
                        v_ext[:, :].rearrange("p (t h c) -> p t h c", t=NTILE_SEQ, h=16)
                        [:, _gt, _og * 8:(_og + 1) * 8, 0:64],
                        b.psum_f[_pb][:, :].rearrange("p (h c) -> p h c", h=8), AF.Copy), [tm, t_vones])
                    bank[pb] = t_v
                    new_vext_done.append(t_v)
        xnf_done = [t_q, t_ckv]
        ckv_done = new_ckv_done
        kfm_done_prev = kfm_done
        kfm_done = new_kfm_done
        vext_done = new_vext_done
        t_kv_ready = [kfm_done[-1], new_vext_done[-1], t_vones]

        for qtg in range(GPS):
            tq = b.dma("sync", b.sq[6], lambda e, _s=s, _qtg=qtg: e.dma_start(
                out=qld[:, :].rearrange("p (c t) -> p c t", c=8),
                in_=qscr.rearrange("(c p) t -> p c t", p=128)[:, :, _s * SEQ + _qtg * 512:_s * SEQ + (_qtg + 1) * 512]),
                deps=[qld_done, qst_done[0], qst_done[1]])
            jmax = 4 * qtg + 3
            ctx_last = None
            ctx_banks_done = [bank[0], bank[1], bank[2], bank[3]]
            for h in range(16):
                hb = (h % 2) * 64
                fcq = h // 2
                pend = []

                def emit_ctx(pend):
                    nonlocal ctx_last
                    j, psl, t_e = pend
                    for tt in range(4):
                        if j > 4 * qtg + tt:
                            continue
                        first = (j == 0)
                        lastj = (j == 4 * qtg + tt)
                        tm_c = b.pe(lambda e, _h=h, _j=j, _tt=tt, _sl=psl, _f=first, _l=lastj: e.matmul(
                            b.psum_f[_tt][:, 0:65],
                            pT[_sl][:, _tt * 128:(_tt + 1) * 128],
                            v_ext[:, _j * 1040 + _h * 65:_j * 1040 + (_h + 1) * 65],
                            start=_f, stop=_l),
                            deps=[t_e] + ([ctx_banks_done[tt]] if first else []),
                            inc=lastj)
                        if lastj:
                            ctx_last = tm_c
                        pT_done[psl] = tm_c

                for j in range(jmax + 1):
                    skey = 4 + j % 2
                    tm_s = b.pe(lambda e, _h=h, _j=j, _hb=hb, _fcq=fcq, _pb=skey: e.matmul(
                        b.psum_f[_pb][:, :],
                        k_fm[_hb:_hb + 64, _fcq * SEQ + _j * 128:_fcq * SEQ + (_j + 1) * 128],
                        qld[_hb:_hb + 64, _fcq * 512:(_fcq + 1) * 512],
                        start=True, stop=True),
                        deps=[tq, bank[skey]] + t_kv_ready, inc=True)
                    psl = j % 3
                    t_e = b.act(lambda e, _pb=skey, _sl=psl: e.activation(
                        pT[_sl][:, :], b.psum_f[_pb][:, :], AF.Exp, scale=0.125),
                        [tm_s, pT_done[psl]])
                    bank[skey] = t_e
                    a = j - 4 * qtg
                    if 0 <= a <= 3:
                        t_e = b.dve(lambda e, _sl=psl, _a=a: e.tensor_mul(
                            pT[_sl][:, :], pT[_sl][:, :],
                            st["mk_b"][:, _a * 512:(_a + 1) * 512]), [t_e])
                    pend.append((j, psl, t_e))
                    if len(pend) > 2:
                        emit_ctx(pend.pop(0))
                for p_ in pend:
                    emit_ctx(p_)
                pend = []
                for tt in range(4):
                    t_d = b.dve(lambda e, _tt=tt: e.reciprocal(sc[:, _tt:_tt + 1],
                                                               b.psum_f[_tt][:, 64:65]), [ctx_last])
                    t_n = b.dve(lambda e, _h=h, _tt=tt: e.tensor_scalar(
                        ctxn[:, _tt * 1024 + _h * 64:_tt * 1024 + (_h + 1) * 64],
                        b.psum_f[_tt][:, 0:64], sc[:, _tt:_tt + 1], None, ALU.mult),
                        [t_d, ctxn_done[tt]])
                    ctx_banks_done[tt] = t_n
                    bank[tt] = t_n
            for tt in range(4):
                ttg = qtg * 4 + tt
                trc = None
                for ch in range(8):
                    trc = b.pe(lambda e, _ch=ch, _tt=tt: e.transpose(
                        b.psum_b[1][:, _ch * 128:(_ch + 1) * 128],
                        ctxn[:, _tt * 1024 + _ch * 128:_tt * 1024 + _ch * 128 + 128],
                        st["id_b"][:, :]), deps=[bank[tt], b.pbd[1]], inc=(ch == 7))
                t_cf = b.dve(lambda e: e.tensor_copy(
                    ctxf[:, :].rearrange("p (c t) -> p c t", c=8),
                    b.psum_b[1][:, :].rearrange("p (c t) -> p c t", c=8)), [trc, ctxf_done])
                b.pbd[1] = t_cf
                ctxn_done[tt] = t_cf
                rsl = tt % 2
                trx = b.dma("sync", b.sq[rsl], lambda e, _s=s, _ttg=ttg, _r=rsl: e.dma_start(
                    out=rx[_r][:, :], in_=x_in[_s * SEQ + _ttg * 128:_s * SEQ + (_ttg + 1) * 128, :]),
                    deps=[rx_done[rsl]])
                for og in range(2):
                    pb = 4 + og
                    tw_o = b.dma("scalar", b.aq[4], lambda e, _og=og: e.dma_start(
                        out=wo[:, :].rearrange("p (d f) -> p d f", d=8),
                        in_=scratch["wo_b"].rearrange("(d p) f -> p d f", p=128)[:, :, _og * 512:(_og + 1) * 512]),
                        deps=[wo_done, b.wcast.get("wo")])
                    tm = None
                    for dc in range(8):
                        tm = b.pe(lambda e, _dc=dc, _og=og, _pb=pb: e.matmul(
                            b.psum_f[_pb][:, :], ctxf[:, _dc * 128:(_dc + 1) * 128],
                            wo[:, _dc * 512:(_dc + 1) * 512],
                            start=(_dc == 0), stop=(_dc == 7)),
                            deps=[t_cf, tw_o, bank[pb]], inc=(dc == 7))
                    wo_done = tm
                    t_o = b.dve(lambda e, _og=og, _pb=pb, _r=rsl: e.tensor_add(
                        rx[_r][:, _og * 512:(_og + 1) * 512], b.psum_f[_pb][:, :],
                        rx[_r][:, _og * 512:(_og + 1) * 512]), [tm, trx])
                    bank[pb] = t_o
                tst = b.dma("scalar", b.aq[rsl], lambda e, _s=s, _ttg=ttg, _r=rsl: e.dma_start(
                    out=x_out[_s * SEQ + _ttg * 128:_s * SEQ + (_ttg + 1) * 128, :], in_=rx[_r][:, :]),
                    deps=[bank[4], bank[5]])
                rx_done[rsl] = tst
                ctxf_done = tm
                res_toks.append(tst)
            qld_done = ctx_last
    return res_toks


# version 18
# speedup vs baseline: 1.0046x; 1.0032x over previous
import numpy as np
import ml_dtypes
import concourse.bass as bass
import concourse.mybir as mybir
from concourse.bass_utils import run_bass_kernel_spmd
from concourse.bass_core_rust import IndirectOffsetOnAxis

F32 = mybir.dt.float32
BF16 = mybir.dt.bfloat16
I32 = mybir.dt.int32
U32 = mybir.dt.uint32
AF = mybir.ActivationFunctionType
ALU = mybir.AluOpType
AX = mybir.AxisListType

D = 1024
CGRP = 160
LN_EPS = 1e-5
DEN_EPS = 1e-6
L1_H = 4096

ENGINES = ("sync", "tensor", "vector", "scalar", "gpsimd")


class Sem:
    def __init__(self, nc, name):
        self.h = nc.alloc_semaphore(name)
        self.n = 0
        self.name = name
        self.last = None

    def bump(self, k):
        self.n += k
        return self.n

    def tok(self):
        return (self, self.n)


class B:

    def __init__(self, nc, cfg):
        self.nc = nc
        self.cfg = cfg
        self.streams = {e: [] for e in ENGINES}
        self.maxwait = {e: {} for e in ENGINES}
        self.eP = Sem(nc, "eP")
        self.eV = Sem(nc, "eV")
        self.eA = Sem(nc, "eA")
        self.eG = Sem(nc, "eG")
        self.sq = [Sem(nc, f"sq{i}") for i in range(8)]
        self.gq = [Sem(nc, f"gq{i}") for i in range(8)]
        self.aq = [Sem(nc, f"aq{i}") for i in range(6)]
        self.psum_f = [nc.alloc_psum_tensor(f"psf{i}", [128, 512], F32) for i in range(6)]
        self.psum_b = [nc.alloc_psum_tensor(f"psb{i}", [128, 1024], BF16) for i in range(2)]
        self.pbd = {0: None, 1: None}

    def make_arenas(self, bf_cols, f32_cols):
        self.ar_bf = self.nc.alloc_sbuf_tensor("ar_bf", [128, bf_cols], BF16)
        self.ar_f = self.nc.alloc_sbuf_tensor("ar_f", [128, f32_cols], F32)
        self.ar_off = {"bf": 0, "f": 0}

    def arena_reset(self):
        self.ar_off = {"bf": 0, "f": 0}

    def abf(self, cols):
        o = self.ar_off["bf"]
        assert o + cols <= self.ar_bf.shape[1], f"bf arena overflow {o}+{cols}"
        self.ar_off["bf"] = o + cols
        return self.ar_bf[:, o:o + cols]

    def af32(self, cols):
        o = self.ar_off["f"]
        assert o + cols <= self.ar_f.shape[1], f"f32 arena overflow {o}+{cols}"
        self.ar_off["f"] = o + cols
        return self.ar_f[:, o:o + cols]

    def _waits(self, eng, deps):
        out = []
        mw = self.maxwait[eng]
        for d in deps:
            if d is None:
                continue
            sem, val = d
            if val <= 0:
                continue
            if mw.get(sem.name, -1) >= val:
                continue
            mw[sem.name] = val
            out.append((sem.h, val))
        return out

    def op(self, eng, f, deps=(), inc=None, inc_by=1):
        waits = self._waits(eng, deps)
        tok = None
        if inc is not None:
            tok = (inc, inc.bump(inc_by))

        def run(e, waits=waits, f=f, inc=inc, inc_by=inc_by):
            for (h, v) in waits:
                e.wait_ge(h, v)
            ins = f(e)
            if inc is not None and ins is not None:
                ins.then_inc(inc.h, inc_by)
        self.streams[eng].append(run)
        return tok

    def pe(self, f, deps=(), inc=True):
        return self.op("tensor", f, deps, self.eP if inc else None)

    def dve(self, f, deps=()):
        return self.op("vector", f, deps, self.eV)

    def act(self, f, deps=()):
        return self.op("scalar", f, deps, self.eA)

    def gp(self, f, deps=()):
        return self.op("gpsimd", f, deps, self.eG)

    def dma(self, queue, slot, f, deps=()):
        deps = list(deps)
        if slot.last is not None:
            deps.append(slot.last)
        tok = self.op(queue, f, deps, slot, inc_by=16)
        slot.last = tok
        return tok

    def fence(self):
        toks = [self.eP.tok(), self.eV.tok(), self.eA.tok(), self.eG.tok()]
        toks += [s.tok() for s in self.sq + self.gq + self.aq]
        for eng in ENGINES:
            waits = self._waits(eng, toks)
            if waits:
                def run(e, waits=waits):
                    for (h, v) in waits:
                        e.wait_ge(h, v)
                self.streams[eng].append(run)

    def assemble(self):
        block = self.nc.Block()
        with block:
            for eng in ENGINES:
                ops = self.streams[eng]

                def make(ops):
                    def body(e):
                        for f in ops:
                            f(e)
                    return body
                getattr(block, eng)(make(ops))



def preamble(b, W):
    nc = b.nc
    cfg = b.cfg
    st = {}

    consts = [LN_EPS, -0.5, 0.5, 1.0 / 1024.0, DEN_EPS, 1e9, 1e4,
              float(cfg["CAP"]), 0.125, 1e-10, 1.0 + LN_EPS]
    consts = sorted(set(consts))
    cst = nc.alloc_sbuf_tensor("cst", [128, len(consts)], F32)
    tok_cst = []
    for i, v in enumerate(consts):
        nc.const_aps.register(F32, v, cst[:, i:i + 1])
        tok_cst.append(b.gp(lambda e, _i=i, _v=v: e.memset(cst[:, _i:_i + 1], _v)))
    st["tok_cst"] = tok_cst[-1]

    b.arena_reset()
    iv = b.af32(128)
    st["id_f"] = nc.alloc_sbuf_tensor("id_f", [128, 128], F32)
    st["id_b"] = nc.alloc_sbuf_tensor("id_b", [128, 128], BF16)
    st["u01"] = nc.alloc_sbuf_tensor("u01", [128, 128], F32)
    t_iv = b.gp(lambda e: e.iota(iv[:, :], pattern=[[-1, 128]], base=0,
                                 channel_multiplier=1, allow_small_or_imprecise_dtypes=True))
    t1 = b.dve(lambda e: e.tensor_single_scalar(st["id_f"][:, :], iv[:, :], 0.0, ALU.is_equal), [t_iv, st["tok_cst"]])
    t2 = b.dve(lambda e: e.tensor_single_scalar(st["id_b"][:, :], iv[:, :], 0.0, ALU.is_equal), [t_iv])
    t3 = b.dve(lambda e: e.tensor_single_scalar(st["u01"][:, :], iv[:, :], 0.0, ALU.is_lt), [t_iv])
    st["tok_id"] = [t1, t2, t3]

    miv = b.af32(512)
    st["mk_b"] = nc.alloc_sbuf_tensor("mk_b", [128, 4 * 512], BF16)
    tmks = []
    for a in range(4):
        ti = b.gp(lambda e, _a=a: e.iota(miv[:, :], pattern=[[1, 512]], base=-128 * _a,
                                         channel_multiplier=-1, allow_small_or_imprecise_dtypes=True),
                  deps=tmks[-1:])
        tmks.append(b.dve(lambda e, _a=a: e.tensor_single_scalar(
            st["mk_b"][:, _a * 512:(_a + 1) * 512], miv[:, :], 0.0, ALU.is_ge), [ti]))
    st["tok_mk"] = tmks[-1:]

    st["iota8"] = nc.alloc_sbuf_tensor("iota8", [128, 8], F32)
    st["tok_iota8"] = b.gp(lambda e: e.iota(st["iota8"][:, :], pattern=[[1, 8]], base=0,
                                            channel_multiplier=0, allow_small_or_imprecise_dtypes=True))

    st["ones128_f"] = nc.alloc_sbuf_tensor("ones128_f", [128, 128], F32)
    st["tok_ones128"] = b.gp(lambda e: e.memset(st["ones128_f"][:, :], 1.0))
    st["ones_row_f"] = nc.alloc_sbuf_tensor("ones_row_f", [1, 128], F32)
    st["ones_row_b"] = nc.alloc_sbuf_tensor("ones_row_b", [1, 128], BF16)
    st["ones_col_f"] = nc.alloc_sbuf_tensor("ones_col_f", [128, 1], F32)
    st["ones_col_b"] = nc.alloc_sbuf_tensor("ones_col_b", [128, 1], BF16)
    st["tok_ones"] = [
        b.gp(lambda e: e.memset(st["ones_row_f"][:, :], 1.0)),
        b.gp(lambda e: e.memset(st["ones_row_b"][:, :], 1.0)),
        b.gp(lambda e: e.memset(st["ones_col_f"][:, :], 1.0)),
        b.gp(lambda e: e.memset(st["ones_col_b"][:, :], 1.0)),
    ]

    ps_t = b.psum_f[0]
    brow = b.af32(1024)[0:1, :]
    brow_state = {"tok": None}

    def bias_cols(name, row_ap, F):
        cols = nc.alloc_sbuf_tensor(f"{name}_col", [128, F // 128], F32)
        tev = None
        for c0 in range(0, F, 1024):
            w = min(1024, F - c0)
            tl = b.dma("sync", b.sq[7], lambda e, _c0=c0, _w=w: e.dma_start(
                out=brow[:, 0:_w], in_=row_ap[:, _c0:_c0 + _w]), deps=[brow_state["tok"]])
            for c in range(w // 128):
                cc = c0 // 128 + c
                tt = b.pe(lambda e, _c=c, _cc=cc: e.transpose(
                    ps_t[:, (_cc % 4) * 128:(_cc % 4) * 128 + 1],
                    brow[:, _c * 128:(_c + 1) * 128],
                    st["id_f"][0:1, 0:1]),
                    deps=[tl, t1, tev])
                tev = b.dve(lambda e, _cc=cc: e.tensor_copy(
                    cols[:, _cc:_cc + 1], ps_t[:, (_cc % 4) * 128:(_cc % 4) * 128 + 1]), [tt])
            brow_state["tok"] = tev
        return cols, tev

    if "L1_attn_b_qkv" in W:
        st["bv_row"] = nc.alloc_sbuf_tensor("bv_row", [1, 1024], F32)
        tr0 = b.dma("sync", b.sq[7], lambda e: e.dma_start(
            out=st["bv_row"][:, :], in_=W["L1_attn_b_qkv"][None, 2048:3072]))
        st["bv_row_b"] = nc.alloc_sbuf_tensor("bv_row_b", [1, 1024], BF16)
        st["tok_bvb"] = b.dve(lambda e: e.tensor_copy(st["bv_row_b"][:, :], st["bv_row"][:, :]), [tr0])
        st["bq_col"], tq = bias_cols("bq", W["L1_attn_b_qkv"][None, 0:1024], 1024)
        st["bk_col"], tk = bias_cols("bk", W["L1_attn_b_qkv"][None, 1024:2048], 1024)
        st["bg_col"], tg = bias_cols("bg", W["L1_attn_b_gate"][None, :], 1024)
        st["b_proj_row"] = nc.alloc_sbuf_tensor("b_proj_row", [1, 1024], F32)
        tr1 = b.dma("sync", b.sq[7], lambda e: e.dma_start(
            out=st["b_proj_row"][:, :], in_=W["L1_attn_b_proj"][None, :]))
        st["lng_col"], tlg = bias_cols("lng", W["L1_attn_ln_g"][None, :], 1024)
        st["lnb_col"], tlb = bias_cols("lnb", W["L1_attn_ln_b"][None, :], 1024)
        st["nbq_col"] = nc.alloc_sbuf_tensor("nbq_col", [128, 8], F32)
        tnq = b.dve(lambda e: e.tensor_scalar_mul(st["nbq_col"][:, :], st["bq_col"][:, :], -1.0), [tq])
        st["tok_l1attn_bias"] = [tq, tk, tg, tlg, tlb, tr0, tr1, tnq]
    if "L1_ffn_b1" in W:
        st["l1b1_col"], tb1 = bias_cols("l1b1", W["L1_ffn_b1"][None, :], 4096)
        st["l1b2_row"] = nc.alloc_sbuf_tensor("l1b2_row", [1, 1024], F32)
        tr2 = b.dma("sync", b.sq[7], lambda e: e.dma_start(
            out=st["l1b2_row"][:, :], in_=W["L1_ffn_b2"][None, :]))
        st["scale_t"] = nc.alloc_sbuf_tensor("scale_t", [1, 1], F32)
        tr3 = b.dma("sync", b.sq[7], lambda e: e.dma_start(
            out=st["scale_t"][:, :], in_=W["L1_ffn_scale"][None, :]))
        st["tok_l1ffn_bias"] = [tb1, tr2, tr3]
    if "L0_ffn_b1" in W:
        st["l0b1_col"], tb = bias_cols("l0b1", W["L0_ffn_b1"].rearrange("e h -> (e h)")[None, :], 8 * 2048)
        st["tok_l0ffn_bias"] = [tb]

    if "L0_ffn_wg" in W:
        wgt = nc.alloc_sbuf_tensor("wgt", [8, 16], F32)
        wg_ss = nc.alloc_sbuf_tensor("wg_ss", [8, 1], F32)
        wg_sq = nc.alloc_sbuf_tensor("wg_sq", [8, 16], F32)
        st["wg_nT"] = nc.alloc_sbuf_tensor("wg_nT", [16, 8], F32)
        st["w_red"] = nc.alloc_sbuf_tensor("w_red", [128, 8 * 16], F32)
        tw = b.dma("sync", b.sq[7], lambda e: e.dma_start(out=wgt[:, :], in_=W["L0_ffn_wg"][:, :]))
        twr = None
        for c in range(8):
            twr = b.dma("sync", b.sq[7], lambda e, _c=c: e.dma_start(
                out=st["w_red"][:, _c * 16:(_c + 1) * 16],
                in_=W["L0_ffn_w_red"][_c * 128:(_c + 1) * 128, :]))
        t_sq = b.act(lambda e: e.activation(wg_sq[:, :], wgt[:, :], AF.Square,
                                            accum_out=wg_ss[:, :]), [tw, st["tok_cst"]])
        t_ln = b.act(lambda e: e.activation(wg_ss[:, :], wg_ss[:, :], AF.Ln), [t_sq])
        t_ex = b.act(lambda e: e.activation(wg_ss[:, :], wg_ss[:, :], AF.Exp, scale=-0.5), [t_ln])
        t_cl = b.dve(lambda e: e.tensor_scalar_min(wg_ss[:, :], wg_ss[:, :], 1e4), [t_ex])
        t_mu = b.dve(lambda e: e.tensor_scalar(wgt[:, :], wgt[:, :], wg_ss[:, :], None, ALU.mult), [t_cl])
        t_tr = b.pe(lambda e: e.transpose(ps_t[0:16, 0:8], wgt[:, :], st["id_f"][0:8, 0:8]),
                    [t_mu, t1])
        tok_wg = b.dve(lambda e: e.tensor_copy(st["wg_nT"][:, :], ps_t[0:16, 0:8]), [t_tr])
        st["wg_nT_b"] = nc.alloc_sbuf_tensor("wg_nT_b", [16, 8], BF16)
        st["w_red_b"] = nc.alloc_sbuf_tensor("w_red_b", [128, 8 * 16], BF16)
        st["tok_wg"] = b.dve(lambda e: e.tensor_copy(st["wg_nT_b"][:, :], st["wg_nT"][:, :]), [tok_wg])
        st["tok_wred"] = b.dve(lambda e: e.tensor_copy(st["w_red_b"][:, :], st["w_red"][:, :]), [twr])

    b.st = st
    return st



class TStep:

    def __init__(self, b, name, dtype, share=None):
        nc = b.nc
        self.b = b
        self.dtype = dtype
        if share is None:
            self.xt = [nc.alloc_sbuf_tensor(f"{name}_xt{i}", [128, 1024], F32) for i in range(3)]
            self.sqd = nc.alloc_sbuf_tensor(f"{name}_sqd", [128, 1024], F32)
        else:
            self.xt = share.xt
            self.sqd = share.sqd
        self.xn = [nc.alloc_sbuf_tensor(f"{name}_xn{i}", [128, 1024], dtype) for i in range(3)]
        self.stats = [nc.alloc_sbuf_tensor(f"{name}_st{i}", [128, 8], F32) for i in range(3)]
        self.xt_done = [None, None, None]
        self.xn_done = [None, None, None]
        self.st_done = [(), (), ()]

    def tile(self, x_dram, tok0, slot, extra_x_deps=(), ln2=False):
        b = self.b
        xt = self.xt[slot]
        xn = self.xn[slot]
        s = self.stats[slot]
        prev = list(self.st_done[slot])
        tl = b.dma("sync", b.sq[slot], lambda e: e.dma_start(
            out=xt[:, :], in_=x_dram[tok0:tok0 + 128, :]),
            deps=list(extra_x_deps) + [self.xt_done[slot]])
        t_sq = b.act(lambda e: e.activation(self.sqd[:, :], xt[:, :], AF.Square,
                                            accum_out=s[:, 0:1]), [tl] + prev)
        t_s = b.dve(lambda e: e.reduce_sum(s[:, 1:2], xt[:, :], axis=AX.X), [tl] + prev)
        t_m = b.dve(lambda e: e.tensor_scalar_mul(s[:, 2:3], s[:, 1:2], 1.0 / 1024.0), [t_s])
        t_e = b.dve(lambda e: e.tensor_scalar_mul(s[:, 3:4], s[:, 0:1], 1.0 / 1024.0), [t_sq])
        t_ms = b.dve(lambda e: e.tensor_mul(s[:, 4:5], s[:, 2:3], s[:, 2:3]), [t_m])
        t_v = b.dve(lambda e: e.tensor_sub(s[:, 5:6], s[:, 3:4], s[:, 4:5]), [t_e, t_ms])
        if ln2:
            t_ln = b.act(lambda e: e.activation(s[:, 6:7], s[:, 5:6], AF.Ln,
                                                bias=1e-10, scale=1.0 + LN_EPS), [t_v])
        else:
            t_ln = b.act(lambda e: e.activation(s[:, 6:7], s[:, 5:6], AF.Ln, bias=LN_EPS), [t_v])
        t_rs = b.act(lambda e: e.activation(s[:, 7:8], s[:, 6:7], AF.Exp, scale=-0.5), [t_ln])
        t_nmr = b.dve(lambda e: e.tensor_scalar(s[:, 4:5], s[:, 2:3], s[:, 7:8], -1.0,
                                                ALU.mult, ALU.mult), [t_rs, t_m])
        t_xn = b.act(lambda e: e.activation(xn[:, :], xt[:, :], AF.Identity,
                                            scale=s[:, 7:8], bias=s[:, 4:5]),
                     [t_nmr, tl, self.xn_done[slot]])
        self.st_done[slot] = (t_xn, t_nmr)
        return xn, t_xn, xt, tl

    def transpose_to(self, xn_ap, t_xn, dst_view, bank_done, extra=(), gain=None, beta=None, pb=0):
        b = self.b
        st = b.st
        if self.dtype == BF16:
            bank = b.psum_b[pb]
            ident = st["id_b"]
            tt = None
            for c in range(8):
                tt = b.pe(lambda e, _c=c: e.transpose(
                    bank[:, _c * 128:(_c + 1) * 128], xn_ap[:, _c * 128:(_c + 1) * 128],
                    ident[:, :]), deps=[t_xn, bank_done, b.pbd[pb]] + list(extra), inc=(c == 7))
            if gain is None:
                tev = b.dve(lambda e: e.tensor_copy(
                    dst_view, bank[:, :].rearrange("p (c t) -> p c t", c=8)), [tt])
            else:
                tev = None
                for c in range(8):
                    tev = b.dve(lambda e, _c=c: e.tensor_scalar(
                        dst_view[:, _c, :], bank[:, _c * 128:(_c + 1) * 128],
                        gain[:, _c:_c + 1], beta[:, _c:_c + 1], ALU.mult, ALU.add),
                        [tt])
            b.pbd[pb] = tev
            return tev
        else:
            ident = st["id_f"]
            banks = (b.psum_f[4], b.psum_f[5])
            tev = None
            tts = []
            for half in range(2):
                bank = banks[half]
                tt = None
                for c in range(4):
                    cc = half * 4 + c
                    tt = b.pe(lambda e, _cc=cc, _c=c, _bank=bank: e.transpose(
                        _bank[:, _c * 128:(_c + 1) * 128], xn_ap[:, _cc * 128:(_cc + 1) * 128],
                        ident[:, :]), deps=[t_xn, bank_done] + list(extra), inc=(c == 3))
                tts.append(tt)
            tev = b.dve(lambda e: e.tensor_copy(
                dst_view[:, 0:4, :], banks[0][:, :].rearrange("p (c t) -> p c t", c=4)), [tts[0]])
            tev2 = b.dve(lambda e: e.tensor_copy(
                dst_view[:, 4:8, :], banks[1][:, :].rearrange("p (c t) -> p c t", c=4)), [tts[1]])
            return tev2



def emit_moe(b, x_in, x_out, W, scratch, ph):
    nc = b.nc
    st = b.st
    cfg = b.cfg
    NG, NT, CAP = cfg["NG"], cfg["NT"], cfg["CAP"]
    CG = cfg["CGRP"]
    NS = 8 * CAP
    xn_sorted = scratch["xn_sorted"]
    osort = scratch["osort"]
    if not hasattr(b, "moe_shared"):
        b.moe_shared = {
            "mi8": nc.alloc_sbuf_tensor("mi8_sh", [128, 16], U32),
            "slots": nc.alloc_sbuf_tensor("slots_sh", [128, NT], I32),
            "bnd": nc.gpsimd.alloc_register("moebnd"),
        }
        b.gp(lambda e: e.reg_mov(b.moe_shared["bnd"], NS - 1))

    b.arena_reset()
    xnf2 = [b.abf(8 * 512) for _ in range(2)]
    xredT = b.abf(512)
    xnbf = [b.abf(1024) for _ in range(4)]
    oh_sb = b.af32(4 * 8)
    sc_half = b.af32(NT)
    esb2 = [b.af32(8) for _ in range(2)]
    mx82 = [b.af32(8) for _ in range(2)]
    tmp82 = [b.af32(8) for _ in range(2)]
    scol2 = [b.af32(8) for _ in range(2)]
    b2row = b.af32(1024)[0:1, :]
    mi8 = b.moe_shared["mi8"]
    slots = b.moe_shared["slots"]
    base_sb = b.af32(8)[0:1, :]
    zero_sb = b.af32(8)[0:1, :]
    t_z = b.gp(lambda e: e.memset(zero_sb[:, :], 0.0))

    ts = b.ts_bf
    resx = [ts.xt[0], ts.xt[1], b.af32(1024)]
    outr = [b.af32(1024) for _ in range(3)]
    pre_deps = [st["tok_wg"], st["tok_wred"], st["tok_iota8"], st["tok_cst"],
                st["tok_ones"][-1], st["tok_l0ffn_bias"][-1]] + st["tok_id"]

    bank_g = {0: [], (1, 0): [], (1, 1): [], (2, 0): [], (2, 1): []}
    xnf_done = [None, None]
    scatter_toks = []
    scat_prev = [None] * 4
    cnt_prev = None
    cnt_last = None
    base_rd = None
    mi_done = [None, None]

    for g in range(NG):
        gsl = g % 2
        xnf_fm = xnf2[gsl]
        xnf_r = xnf_fm.rearrange("p (c t) -> p c t", c=8)
        tile_toks = []
        for j in range(4):
            gt = g * 4 + j
            slot = gt % 3
            xn, t_xn, xt, tl = ts.tile(x_in, g * 512 + j * 128, slot, extra_x_deps=pre_deps)
            t_bf = b.dve(lambda e, _j=j, _xn=xn: e.tensor_copy(xnbf[_j][:, :], _xn[:, :]),
                         [t_xn, scat_prev[j]])
            tev = ts.transpose_to(xn, t_xn, xnf_r[:, :, j * 128:(j + 1) * 128],
                                  None, extra=[xnf_done[gsl]], pb=j % 2)
            ts.xt_done[slot] = t_xn
            ts.xn_done[slot] = tev
            tile_toks.append((gt, slot, t_bf, tev))
        t_fm = tile_toks[-1][3]

        pxr = b.psum_f[0]
        tm = None
        for c in range(8):
            tm = b.pe(lambda e, _c=c, _x=xnf_fm: e.matmul(pxr[0:16, :], st["w_red_b"][:, _c * 16:(_c + 1) * 16],
                                                          _x[:, _c * 512:(_c + 1) * 512],
                                                          start=(_c == 0), stop=(_c == 7)),
                      deps=[t_fm, st["tok_wred"], st["tok_wg"], st["tok_ones128"]] + bank_g[0],
                      inc=(c == 7))
        t_xr = b.dve(lambda e: e.tensor_copy(xredT[0:16, :], pxr[0:16, :]), [tm])
        bank_g[0] = [t_xr]
        xnf_done[gsl] = t_xr

        if g == 0:
            t_base = None
        else:
            t_base = b.dve(lambda e: e.tensor_copy(base_sb[0:1, :], b.psum_f[3][0:1, 0:8]),
                           [cnt_last, base_rd])
        for (gt, slot, t_bf, tev) in tile_toks:
            j = gt % 4
            jp = j % 2
            esb, mx8, tmp8, scol = esb2[jp], mx82[jp], tmp82[jp], scol2[jp]
            plg = b.psum_f[1] if jp == 0 else b.psum_f[4]
            t_lg = b.pe(lambda e, _j=j, _p=plg: e.matmul(_p[:, 0:8], xredT[0:16, _j * 128:(_j + 1) * 128],
                                                 st["wg_nT_b"][:, :], start=True, stop=True),
                        deps=[t_xr] + bank_g[(1, jp)])
            t_nm = b.dve(lambda e, _p=plg, _s=scol: e.reduce_max(_s[:, 0:1], _p[:, 0:8], axis=AX.X, negate=True), [t_lg])
            t_ex = b.act(lambda e, _p=plg, _s=scol, _e=esb: e.activation(_e[:, :], _p[:, 0:8], AF.Exp,
                                                bias=_s[:, 0:1], accum_out=_s[:, 1:2]), [t_nm, t_lg])
            t_rc = b.dve(lambda e, _s=scol: e.reciprocal(_s[:, 2:3], _s[:, 1:2]), [t_ex])
            t_sh = b.dve(lambda e, _gt=gt, _s=scol: e.tensor_scalar_mul(sc_half[:, _gt:_gt + 1], _s[:, 2:3], 0.5), [t_rc])
            t_mx = b.dve(lambda e, _p=plg, _m=mx8: e.max(_m[:, :], _p[:, 0:8]), [t_lg, mi_done[jp]])
            t_mi = b.dve(lambda e, _p=plg, _m=mx8, _jp=jp: e.max_index(
                mi8[:, _jp * 8:(_jp + 1) * 8], _m[:, :], _p[:, 0:8]), [t_mx])
            t_if = b.dve(lambda e, _s=scol, _jp=jp: e.tensor_copy(_s[:, 3:4], mi8[:, _jp * 8:_jp * 8 + 1]), [t_mi])
            mi_done[jp] = t_if
            bank_g[(1, jp)] = [t_if, t_ex]
            t_oh = b.dve(lambda e, _j=j, _s=scol: e.tensor_scalar(oh_sb[:, _j * 8:(_j + 1) * 8], st["iota8"][:, :],
                                                         _s[:, 3:4], None, ALU.is_equal),
                         [t_if, cnt_prev])
            prk = b.psum_f[2] if jp == 0 else b.psum_f[5]
            tr = None
            for i in range(j + 1):
                lhs = st["ones128_f"] if i < j else st["u01"]
                tr = b.pe(lambda e, _i=i, _j=j, _lhs=lhs, _p=prk, _st=(i == j and g == 0): e.matmul(
                    _p[:, 0:8], _lhs[:, :], oh_sb[:, _i * 8:(_i + 1) * 8],
                    start=(_i == 0), stop=_st),
                    deps=[t_oh] + bank_g[(2, jp)], inc=(i == j and g == 0))
            if g > 0:
                tr = b.pe(lambda e, _p=prk: e.matmul(
                    _p[:, 0:8], st["ones_row_f"][:, :], base_sb[0:1, :],
                    start=False, stop=True), deps=[t_base], inc=True)
            t_rm = b.dve(lambda e, _j=j, _t=tmp8, _p=prk: e.tensor_mul(_t[:, :], _p[:, 0:8], oh_sb[:, _j * 8:(_j + 1) * 8]), [tr])
            t_ra = b.dve(lambda e, _s=scol, _t=tmp8: e.reduce_sum(_s[:, 4:5], _t[:, :], axis=AX.X), [t_rm])
            bank_g[(2, jp)] = [t_rm]
            t_s1 = b.dve(lambda e, _s=scol: e.scalar_tensor_tensor(_s[:, 5:6], _s[:, 3:4], float(CAP),
                                                          _s[:, 4:5], ALU.mult, ALU.add), [t_ra, t_if])
            t_ov = b.dve(lambda e, _s=scol: e.tensor_single_scalar(_s[:, 6:7], _s[:, 4:5], float(CAP), ALU.is_ge), [t_ra])
            t_s2 = b.dve(lambda e, _s=scol: e.scalar_tensor_tensor(_s[:, 7:8], _s[:, 6:7], 1e9,
                                                          _s[:, 5:6], ALU.mult, ALU.add), [t_ov, t_s1])
            t_si = b.dve(lambda e, _gt=gt, _s=scol: e.tensor_copy(slots[:, _gt:_gt + 1], _s[:, 7:8]), [t_s2])
            t_sc = b.op("gpsimd", lambda e, _gt=gt, _j=j: e.indirect_dma_start(
                out=xn_sorted[:, :], out_offset=IndirectOffsetOnAxis(slots[:, _gt:_gt + 1], 0),
                in_=xnbf[_j][:, :], in_offset=None,
                bounds_check=b.moe_shared["bnd"], oob_is_err=False),
                deps=[t_si, t_bf], inc=b.gq[0], inc_by=16)
            scatter_toks.append(t_sc)
        b.drain_casts(1)
        for j in range(4):
            scat_prev[j] = b.gq[0].tok()
        if g < NG - 1:
            for j in range(4):
                cnt_last = b.pe(lambda e, _j=j, _g=g, _st=(g == 0 and j == 3): e.matmul(
                    b.psum_f[3][0:1, 0:8], st["ones_col_f"][:, :], oh_sb[:, _j * 8:(_j + 1) * 8],
                    start=(_j == 0), stop=_st),
                    deps=[t_base] if (j == 0 and g > 0) else [], inc=(g == 0 and j == 3))
            if g > 0:
                cnt_last = b.pe(lambda e: e.matmul(
                    b.psum_f[3][0:1, 0:8], st["ones_row_f"][0:1, 0:1], base_sb[0:1, :],
                    start=False, stop=True), deps=[t_base], inc=True)
                base_rd = cnt_last
        cnt_prev = cnt_last

    b.drain_casts(len(b.cast_sched))
    b.fence()

    b.ar_off["bf"] = 0
    w1sb = b.abf(8 * 2048)
    w2sb = b.abf(16 * 1024)
    hs_fm2 = [b.abf(16 * 512) for _ in range(2)]
    xs_fm2 = [b.abf(8 * 512) for _ in range(2)]
    xs_tm = [b.abf(1024) for _ in range(2)]
    og_tm = [b.abf(1024) for _ in range(3)]
    bank_tr = [None, None]
    NSG = CAP // 512
    w1_free = None
    w2_free = None
    b2_free = None
    bank_h = {0: None, 1: None, 2: None}
    bank_o = {3: None, 4: None, 5: None}
    xsf_done = [None, None]
    og_store_prev = [None, None, None]
    ostore_toks = []
    wload = {}
    w1b = scratch["w1e_b"]
    w2b = scratch["w2e_b"]
    def load_w1(e):
        tw1 = None
        for dc in range(0, 8, 2):
            tw1 = b.dma("sync", b.sq[5], lambda e_, _dc=dc, _e=e: e_.dma_start(
                out=w1sb[:, _dc * 2048:(_dc + 2) * 2048].rearrange("p (c k) -> p c k", c=2),
                in_=w1b.rearrange("e (c p) h -> e p c h", p=128)[_e, :, _dc:_dc + 2, :]),
                deps=[w1_free, b.wcast.get(f"w1_{e}")])
        return tw1
    def load_w2(e):
        tw2 = None
        for hc in range(0, 16, 4):
            tw2 = b.dma("sync", b.sq[6], lambda e_, _hc=hc, _e=e: e_.dma_start(
                out=w2sb[:, _hc * 1024:(_hc + 4) * 1024].rearrange("p (c k) -> p c k", c=4),
                in_=w2b.rearrange("e (c p) o -> e p c o", p=128)[_e, :, _hc:_hc + 4, :]),
                deps=[w2_free, b.wcast.get(f"w2_{e}")])
        tb2 = b.dma("sync", b.sq[4], lambda e_, _e=e: e_.dma_start(
            out=b2row[:, :], in_=W["L0_ffn_b2"][_e:_e + 1, :]), deps=[b2_free])
        return tw2, tb2
    wload[0] = (load_w1(0), load_w2(0))
    pairs = []
    sg0 = 0
    while sg0 < NSG:
        pairs.append(tuple(range(sg0, min(sg0 + 2, NSG))))
        sg0 += 2
    for e in range(8):
        tw1, (tw2, tb2) = wload[e]
        for ip, pair in enumerate(pairs):
            t_xs = {}
            for a, sg in enumerate(pair):
                row0 = e * CAP + sg * 512
                xs_fm = xs_fm2[a]
                xs_r = xs_fm.rearrange("p (c t) -> p c t", c=8)
                for t in range(4):
                    slot = t % 2
                    tl = b.dma("sync", b.sq[2 + slot], lambda e_, _r=row0 + t * 128, _s=slot: e_.dma_start(
                        out=xs_tm[_s][:, :], in_=xn_sorted[_r:_r + 128, :]), deps=[bank_tr[t % 2]])
                    tt = None
                    for c in range(8):
                        tt = b.pe(lambda e_, _c=c, _s=slot, _b=t % 2: e_.transpose(
                            b.psum_b[_b][:, _c * 128:(_c + 1) * 128], xs_tm[_s][:, _c * 128:(_c + 1) * 128],
                            st["id_b"][:, :]), deps=[tl, b.pbd[t % 2], xsf_done[a]], inc=(c == 7))
                    tev = b.dve(lambda e_, _t=t, _b=t % 2, _r=xs_r: e_.tensor_copy(
                        _r[:, :, _t * 128:(_t + 1) * 128],
                        b.psum_b[_b][:, :].rearrange("p (c t) -> p c t", c=8)), [tt])
                    b.pbd[t % 2] = tev
                    bank_tr[t % 2] = tev
                t_xs[a] = tev
            t_h = {}
            tm_h = {}
            for hc in range(16):
                for a, sg in enumerate(pair):
                    pb = (hc * len(pair) + a) % 3
                    tm = None
                    for dc in range(8):
                        tm = b.pe(lambda e_, _dc=dc, _hc=hc, _pb=pb, _a=a: e_.matmul(
                            b.psum_f[_pb][:, :], w1sb[:, _dc * 2048 + _hc * 128:_dc * 2048 + _hc * 128 + 128],
                            xs_fm2[_a][:, _dc * 512:(_dc + 1) * 512],
                            start=(_dc == 0), stop=(_dc == 7)),
                            deps=[t_xs[a], tw1, bank_h[pb]], inc=(dc == 7))
                    tg = b.act(lambda e_, _hc=hc, _e=e, _pb=pb, _a=a: e_.activation(
                        hs_fm2[_a][:, _hc * 512:(_hc + 1) * 512], b.psum_f[_pb][:, :], AF.Gelu,
                        bias=st["l0b1_col"][:, _e * 16 + _hc:_e * 16 + _hc + 1]), [tm])
                    bank_h[pb] = tg
                    t_h[a] = tg
                    tm_h[a] = tm
            for a, sg in enumerate(pair):
                xsf_done[a] = tm_h[a]
            if ip == len(pairs) - 1:
                w1_free = tm_h[len(pair) - 1]
                if e < 7:
                    next_w1 = load_w1(e + 1)
            for a, sg in enumerate(pair):
                row0 = e * CAP + sg * 512
                for t in range(4):
                    gt2 = a * 4 + t
                    ob0, ob1 = 3 + (2 * gt2) % 3, 3 + (2 * gt2 + 1) % 3
                    tm = None
                    for hc in range(16):
                        for og, ob in ((0, ob0), (1, ob1)):
                            tm = b.pe(lambda e_, _hc=hc, _t=t, _og=og, _ob=ob, _a=a: e_.matmul(
                                b.psum_f[_ob][:, :],
                                hs_fm2[_a][:, _hc * 512 + _t * 128:_hc * 512 + _t * 128 + 128],
                                w2sb[:, _hc * 1024 + _og * 512:_hc * 1024 + _og * 512 + 512],
                                start=(_hc == 0), stop=False),
                                deps=[t_h[a], tw2, bank_o[ob0], bank_o[ob1]] if (hc == 0 and og == 0) else (),
                                inc=False)
                    oevs = []
                    for og, ob in ((0, ob0), (1, ob1)):
                        tm = b.pe(lambda e_, _og=og, _ob=ob: e_.matmul(
                            b.psum_f[_ob][:, :], st["ones_row_f"][:, :],
                            b2row[:, _og * 512:_og * 512 + 512],
                            start=False, stop=True), deps=[tb2, st["tok_ones"][0]], inc=True)
                        oev = b.act(lambda e_, _og=og, _t=t, _s=t % 2, _ob=ob: e_.activation(
                            og_tm[_s][:, _og * 512:(_og + 1) * 512], b.psum_f[_ob][:, :], AF.Copy),
                            [tm, og_store_prev[t % 2]])
                        oevs.append(oev)
                        bank_o[ob] = oev
                    tst = b.dma("scalar", b.aq[t % 2], lambda e_, _r=row0 + t * 128, _s=t % 2: e_.dma_start(
                        out=osort[_r:_r + 128, :], in_=og_tm[_s][:, :]), deps=oevs)
                    ostore_toks.append(tst)
                    og_store_prev[t % 2] = tst
                    if ip == len(pairs) - 1 and a == len(pair) - 1 and t == 3:
                        w2_free = tm
                        b2_free = tm
                        if e < 7:
                            wload[e + 1] = (next_w1, load_w2(e + 1))

    b.fence()

    res_toks = []
    R = 4
    og_tm = og_tm + [xs_tm[0]]
    resx = resx + [b.af32(1024)]
    outr = outr + [b.af32(1024)]
    gdone = [None] * R
    stprev = [None] * R
    for gt in range(NT):
        slot = gt % R
        tg = b.dma("gpsimd", b.gq[3 + slot], lambda e_, _gt=gt, _s=slot: e_.indirect_dma_start(
            out=og_tm[_s][:, :], out_offset=None,
            in_=osort[:, :], in_offset=IndirectOffsetOnAxis(slots[:, _gt:_gt + 1], 0),
            bounds_check=b.moe_shared["bnd"], oob_is_err=False), deps=[gdone[slot]])
        tx = b.dma("sync", b.sq[slot if slot < 3 else 7], lambda e_, _gt=gt, _s=slot: e_.dma_start(
            out=resx[_s][:, :], in_=x_in[_gt * 128:(_gt + 1) * 128, :]), deps=[gdone[slot]])
        to = b.dve(lambda e_, _gt=gt, _s=slot: e_.scalar_tensor_tensor(
            outr[_s][:, :], og_tm[_s][:, :], sc_half[:, _gt:_gt + 1], resx[_s][:, :],
            ALU.mult, ALU.add), [tg, tx, stprev[slot]])
        gdone[slot] = to
        tst = b.dma("scalar", b.aq[2 + slot % 2], lambda e_, _gt=gt, _s=slot: e_.dma_start(
            out=x_out[_gt * 128:(_gt + 1) * 128, :], in_=outr[_s][:, :]), deps=[to])
        stprev[slot] = tst
        res_toks.append(tst)
    return res_toks



def build_program(cfg, phases=("moe1",)):
    nc = bass.Bass("TRN2", target_bir_lowering=False)
    TOK = cfg["TOK"]
    b = B(nc, cfg)
    b.make_arenas(cfg.get("BF_COLS", 62464), cfg.get("F_COLS", 9216))

    x_ap = nc.dram_tensor("x", [TOK, D], F32, kind="ExternalInput").ap()
    y_ap = nc.dram_tensor("y", [TOK, D], F32, kind="ExternalOutput").ap()

    W = {}
    wspecs = {
        "L0_attn_W_DKV": [D, 256], "L0_attn_W_UK": [256, D], "L0_attn_W_UV": [256, D],
        "L0_attn_W_UQ": [D, D], "L0_attn_W_O": [D, D],
        "L0_ffn_w_red": [D, 16], "L0_ffn_wg": [8, 16],
        "L0_ffn_W1": [8, D, 2048], "L0_ffn_b1": [8, 2048],
        "L0_ffn_W2": [8, 2048, D], "L0_ffn_b2": [8, D],
        "L1_attn_w_qkv": [D, 3 * D], "L1_attn_b_qkv": [3 * D],
        "L1_attn_w_gate": [D, D], "L1_attn_b_gate": [D],
        "L1_attn_w_proj": [D, D], "L1_attn_b_proj": [D],
        "L1_attn_ln_g": [D], "L1_attn_ln_b": [D],
        "L1_ffn_W1": [L1_H, D], "L1_ffn_b1": [L1_H],
        "L1_ffn_W2": [D, L1_H], "L1_ffn_b2": [D],
        "L1_ffn_scale": [1],
    }
    need = set()
    for p in phases:
        if p.startswith("moe"):
            need.update(k for k in wspecs if k.startswith("L0_ffn"))
        elif p == "mla":
            need.update(k for k in wspecs if k.startswith("L0_attn"))
        elif p.startswith("lff"):
            need.update(k for k in wspecs if k.startswith("L1_ffn"))
        elif p == "lat":
            need.update(k for k in wspecs if k.startswith("L1_attn"))
    for k in sorted(need):
        W[k] = nc.dram_tensor(k, wspecs[k], F32, kind="ExternalInput").ap()

    scratch = {}
    NSROW = 8 * cfg["CAP"]
    if any(p.startswith("moe") for p in phases):
        scratch["xn_sorted"] = nc.dram_tensor("xn_sorted", [NSROW, D], BF16, kind="Internal").ap()
        scratch["osort"] = nc.dram_tensor("osort", [NSROW, D], BF16, kind="Internal").ap()
    if any(p.startswith("lff") for p in phases):
        scratch["w1t"] = nc.dram_tensor("w1t_scr", [D, L1_H], BF16, kind="Internal").ap()
        scratch["w2t"] = nc.dram_tensor("w2t_scr", [L1_H, D], BF16, kind="Internal").ap()
    if "mla" in phases:
        scratch["qscr"] = nc.dram_tensor("qscr", [D, TOK], BF16, kind="Internal").ap()
        scratch["wuq_b"] = nc.dram_tensor("wuq_b", [D, D], BF16, kind="Internal").ap()
        scratch["wo_b"] = nc.dram_tensor("wo_b", [D, D], BF16, kind="Internal").ap()
    if "lat" in phases:
        scratch["wqkv_b"] = nc.dram_tensor("wqkv_b", [D, 3 * D], BF16, kind="Internal").ap()
        scratch["wg_b"] = nc.dram_tensor("wg_b", [D, D], BF16, kind="Internal").ap()
        scratch["wp_b"] = nc.dram_tensor("wp_b", [D, D], BF16, kind="Internal").ap()
    if any(p.startswith("moe") for p in phases):
        scratch["w1e_b"] = nc.dram_tensor("w1e_b", [8, D, 2048], BF16, kind="Internal").ap()
        scratch["w2e_b"] = nc.dram_tensor("w2e_b", [8, 2048, D], BF16, kind="Internal").ap()
    xa = nc.dram_tensor("xbufA", [TOK, D], F32, kind="Internal").ap()
    xb_ = nc.dram_tensor("xbufB", [TOK, D], F32, kind="Internal").ap()

    preamble(b, W)
    b.ts_bf = TStep(b, "tsb", BF16)
    if any(p.startswith("lff") for p in phases):
        prep_l1w(b, W, scratch)
    b.fence()

    wc = {}
    b.cast_sched = []
    def cast_group(gname, names, items, chunk=512):
        sem = Sem(nc, f"cast_{gname}")
        n_dma = 0
        closures = []
        for (src_ap, dst_ap, rows) in items:
            for r0 in range(0, rows, chunk):
                r1 = min(rows, r0 + chunk)
                closures.append(lambda _a=src_ap, _d=dst_ap, _r0=r0, _r1=r1, _s=sem: b.op(
                    "gpsimd", lambda e: e.dma_start(out=_d[_r0:_r1, :], in_=_a[_r0:_r1, :]),
                    deps=(), inc=_s, inc_by=16))
                n_dma += 1
        tok = (sem, 16 * n_dma)
        for nm in names:
            wc[nm] = tok
        b.cast_sched.append(closures)
    if any(p.startswith("moe") for p in phases):
        for e_i in range(8):
            cast_group(f"moe{e_i}", [f"w1_{e_i}", f"w2_{e_i}"], [
                (W["L0_ffn_W1"][e_i], scratch["w1e_b"][e_i], D),
                (W["L0_ffn_W2"][e_i], scratch["w2e_b"][e_i], 2048)])
    if "mla" in phases:
        cast_group("mla", ["wuq", "wo"], [
            (W["L0_attn_W_UQ"], scratch["wuq_b"], D),
            (W["L0_attn_W_O"], scratch["wo_b"], D)])
    if "lat" in phases:
        cast_group("lat", ["wqkv", "wg", "wp"], [
            (W["L1_attn_w_qkv"], scratch["wqkv_b"], D),
            (W["L1_attn_w_gate"], scratch["wg_b"], D),
            (W["L1_attn_w_proj"], scratch["wp_b"], D)])
    b.wcast = wc

    def drain_casts(k=1):
        for _ in range(k):
            if b.cast_sched:
                for cl in b.cast_sched.pop(0):
                    cl()
    b.drain_casts = drain_casts
    if not any(p.startswith("moe") for p in phases):
        b.drain_casts(len(b.cast_sched))

    chain = [x_ap]
    bufs = [xa, xb_]
    for i in range(len(phases) - 1):
        chain.append(bufs[i % 2])
    chain.append(y_ap)

    last_toks = None
    for i, p in enumerate(phases):
        x_in, x_out = chain[i], chain[i + 1]
        if p.startswith("moe"):
            last_toks = emit_moe(b, x_in, x_out, W, scratch, ph=p)
        elif p == "mla":
            last_toks = emit_mla(b, x_in, x_out, W, scratch)
        elif p.startswith("lff"):
            last_toks = emit_lff(b, x_in, x_out, W, scratch, ph=p)
        elif p == "lat":
            last_toks = emit_lat(b, x_in, x_out, W, scratch)
        else:
            raise ValueError(p)
        b.fence()

    def final_wait(e, toks=[t for t in (last_toks or [])]):
        for sem, v in toks:
            e.wait_ge(sem.h, v)
    b.streams["sync"].append(final_wait)

    b.assemble()
    nc.finalize()
    return nc


def make_cfg(TOK, cgrp=CGRP):
    NG = TOK // 512
    return dict(TOK=TOK, NG=NG, NT=TOK // 128, CAP=cgrp * NG, CGRP=cgrp,
                NSEQ=max(1, TOK // 2048), SEQ=min(TOK, 2048))


PHASES_ALL = ("moe1", "mla", "moe2", "lff1", "lat", "lff2")


def kernel(**inputs):
    x = np.ascontiguousarray(np.asarray(inputs["L0_attn_x"]))
    Bt, T, Dm = x.shape
    n_cores = 8
    per = Bt // n_cores
    cfg = make_cfg(per * T, cgrp=96)
    nc = build_program(cfg, phases=PHASES_ALL)
    wnames = [k for k in inputs if k != "L0_attn_x"]
    base = {}
    for k in wnames:
        a = np.ascontiguousarray(np.asarray(inputs[k], dtype=np.float32))
        if a.ndim == 0:
            a = a.reshape(1)
        base[k] = a
    in_maps = []
    for c in range(n_cores):
        m = dict(base)
        m["x"] = np.ascontiguousarray(x[c * per:(c + 1) * per].reshape(per * T, Dm))
        in_maps.append(m)
    res = run_bass_kernel_spmd(nc, in_maps, core_ids=list(range(n_cores)))
    out = np.concatenate([r["y"].reshape(per, T, Dm) for r in res.results], axis=0)
    return out.astype(np.float32)



def prep_l1w(b, W, scratch):
    nc = b.nc
    st = b.st
    b.arena_reset()
    wrow = [b.abf(4096) for i in range(2)]
    wcol = [b.abf(4096) for i in range(2)]
    rdone = [None, None]
    cdone = [None, None]
    bank_done = None
    st_last = [None, None]

    def trans_chunks(src_ap, dst_dram, n_in_chunks, in_cols, dst_rows):
        nonlocal rdone, cdone, bank_done
        dstr = dst_dram.rearrange("(c p) o -> p c o", p=128)
        for rc in range(n_in_chunks):
            sl = rc % 2
            tl = b.dma("gpsimd", b.gq[5], lambda e, _rc=rc, _sl=sl, _s=src_ap, _w=in_cols: e.dma_start(
                out=wrow[_sl][:, 0:_w], in_=_s[_rc * 128:(_rc + 1) * 128, :]),
                deps=[rdone[sl]])
            nblk = in_cols // 1024
            csl = rc % 2
            evs = []
            for blk in range(nblk):
                tt = None
                for c in range(8):
                    cc = blk * 8 + c
                    tt = b.pe(lambda e, _cc=cc, _c=c, _sl=sl: e.transpose(
                        b.psum_b[1][:, _c * 128:(_c + 1) * 128],
                        wrow[_sl][:, _cc * 128:(_cc + 1) * 128], st["id_b"][:, :]),
                        deps=[tl, bank_done] + st["tok_id"], inc=(c == 7))
                tev = b.dve(lambda e, _csl=csl, _blk=blk: e.tensor_copy(
                    wcol[_csl][:, _blk * 1024:(_blk + 1) * 1024].rearrange("p (c t) -> p c t", c=8),
                    b.psum_b[1][:, :].rearrange("p (c t) -> p c t", c=8)),
                    [tt, cdone[csl]])
                bank_done = tev
                evs.append(tev)
            tst = b.dma("scalar", b.aq[4 + rc % 2], lambda e, _rc=rc, _csl=csl, _nb=nblk: e.dma_start(
                out=dstr[:, 0:8 * _nb, _rc * 128:(_rc + 1) * 128],
                in_=wcol[_csl][:, 0:1024 * _nb].rearrange("p (c t) -> p c t", c=8 * _nb)),
                deps=evs)
            st_last[rc % 2] = tst
            cdone[csl] = tst
            rdone[sl] = evs[-1]
    trans_chunks(W["L1_ffn_W1"], scratch["w1t"], 32, 1024, None)
    trans_chunks(W["L1_ffn_W2"], scratch["w2t"], 8, 4096, None)
    b.l1w_tok = [t for t in st_last if t is not None]



def emit_lff(b, x_in, x_out, W, scratch, ph):
    nc = b.nc
    st = b.st
    cfg = b.cfg
    NG = cfg["NG"]
    SEQ = cfg["SEQ"]
    GPS = SEQ // 512
    w1t, w2t = scratch["w1t"], scratch["w2t"]

    b.arena_reset()
    w1win = [b.abf(8 * 1024) for _ in range(2)]
    w2win = [b.abf(8 * 1024) for _ in range(2)]
    hs = [b.abf(8 * 512) for _ in range(2)]
    xnf = [b.abf(8 * 512) for _ in range(2)]
    evo = b.abf(128)
    od3p = b.abf(8 * 4)
    od4p = b.abf(8 * 4)
    m2a = b.abf(4 * 1024)
    m2c = m2a[0:4, :]
    w4a = b.abf(4096)
    w4blk = w4a[0:4, :]
    erow = w4a[32:33, 0:1024]
    erow2 = w4a[64:65, 0:1024]
    acc = b.af32(4 * 1024)
    t1a = b.abf(512)
    t1all = t1a[0:4, :]
    t2a = b.abf(4 * 512)
    t2sb = t2a[0:4, :]
    resx = [b.af32(1024) for _ in range(2)]

    ts = b.ts_bf
    pre = [st["tok_cst"], st["tok_l1ffn_bias"][0], st["tok_l1ffn_bias"][1],
           st["tok_l1ffn_bias"][2], st["tok_ones"][0]] + st["tok_id"]

    xnf_done = [[], []]
    seq_tok = {}
    bank = {"tr": None, 0: None, 1: None, 2: None, 3: None, 4: None, 5: None}
    w1done = [None, None]
    w2done = [None, None]
    hs_done = [None, None]
    acc_done = [None] * 4
    rx_done = [None, None]
    res_toks = []

    for g in range(NG):
        gsl = g % 2
        xnf_g = xnf[gsl]
        xnf_r = xnf_g.rearrange("p (c t) -> p c t", c=8)
        for j in range(4):
            slot = j % 3
            xn, t_xn, xt, tl = ts.tile(x_in, g * 512 + j * 128, slot, extra_x_deps=pre)
            tev = ts.transpose_to(xn, t_xn, xnf_r[:, :, j * 128:(j + 1) * 128],
                                  None, extra=xnf_done[gsl], pb=j % 2)
            ts.xt_done[slot] = t_xn
            ts.xn_done[slot] = tev
            t_fm = tev

        if g % GPS == 0:
            prev_done = res_toks[-1] if res_toks else None
            t_evo = b.dve(lambda e, _x=xnf_r: e.tensor_copy(
                evo[:, :].rearrange("p (c k) -> p c k", c=8), _x[:, :, 1:17]), [t_fm, prev_done])
            t_o3 = b.dve(lambda e: e.tensor_copy(
                od3p[:, :].rearrange("p (c k) -> p c k", c=8),
                evo[:, :].rearrange("p (c k) -> p c k", c=8)[:, :, 1:9:2]), [t_evo])
            t_o4 = b.dve(lambda e: e.tensor_copy(
                od4p[:, :].rearrange("p (c k) -> p c k", c=8),
                evo[:, :].rearrange("p (c k) -> p c k", c=8)[:, :, 9:16:2]), [t_evo])
            t_mz = b.gp(lambda e: e.memset(w4a[:, :], 0.0), [prev_done])
            t_mz2 = b.gp(lambda e: e.memset(m2a[:, :], 0.0), [prev_done])
            t_ev_prev = None
            w4tok = None
            m2tok = None
            for e_i in range(4):
                for colk, dst_is_w4 in [(2 * e_i, True), (8 + 2 * e_i, False)]:
                    tt = None
                    for c in range(8):
                        tt = b.pe(lambda e, _c=c, _k=colk: e.transpose(
                            b.psum_b[1][0:1, _c * 128:(_c + 1) * 128],
                            evo[:, :].rearrange("p (c k) -> p c k", c=8)[:, _c, _k:_k + 1],
                            st["id_b"][:, :]), deps=[t_evo, t_ev_prev, b.pbd[1]], inc=(c == 7))
                    t_er = b.dve(lambda e: e.tensor_single_scalar(
                        erow[:, :], b.psum_b[1][0:1, 0:1024], st["scale_t"][0:1, 0:1], ALU.mult),
                        [tt, t_ev_prev, t_mz])
                    b.pbd[1] = t_er
                    t_ev_prev = t_er
                    if dst_is_w4:
                        w4tok = b.dma("gpsimd", b.gq[5], lambda e, _e=e_i: e.dma_start(
                            out=w4blk[_e:_e + 1, _e * 1024:(_e + 1) * 1024], in_=erow[:, :]),
                            deps=[t_er, t_mz])
                        t_ev_prev = w4tok
                    else:
                        t_es = b.dve(lambda e: e.tensor_copy(
                            erow2[:, :],
                            erow[:, :].rearrange("p (k c) -> p c k", c=4)), [t_er, t_mz2])
                        for cc in range(4):
                            m2tok = b.dma("gpsimd", b.gq[5], lambda e, _e=e_i, _cc=cc: e.dma_start(
                                out=m2c[_e:_e + 1, _cc * 1024 + _e * 256:_cc * 1024 + (_e + 1) * 256],
                                in_=erow2[:, _cc * 256:(_cc + 1) * 256]), deps=[t_es, t_mz2])
                        t_ev_prev = m2tok
            seq_tok = {"w4": w4tok, "m2": m2tok, "o3": t_o3, "o4": t_o4}

        tm = None
        for dc in range(8):
            tm = b.pe(lambda e, _dc=dc, _x=xnf_g: e.matmul(
                b.psum_f[2][0:4, :], od3p[:, _dc * 4:(_dc + 1) * 4], _x[:, _dc * 512:(_dc + 1) * 512],
                start=(_dc == 0), stop=(_dc == 7)),
                deps=[t_fm, seq_tok["o3"], bank[2]], inc=(dc == 7))
        t_t1 = b.dve(lambda e: e.tensor_copy(t1all[:, :], b.psum_f[2][0:4, :]), [tm])
        bank[2] = t_t1

        for blk in range(4):
            wsl = blk % 2
            tw1 = b.dma("sync", b.sq[4 + wsl], lambda e, _blk=blk, _w=wsl: e.dma_start(
                out=w1win[_w][:, :].rearrange("p (c h) -> p c h", c=8),
                in_=w1t.rearrange("(c p) h -> p c h", p=128)[:, :, _blk * 1024:(_blk + 1) * 1024]),
                deps=[w1done[wsl]] + b.l1w_tok)
            t_h = None
            hsb = hs[wsl]
            for hl in range(8):
                hc = blk * 8 + hl
                pb = hl % 2
                for dc in range(8):
                    b.pe(lambda e, _dc=dc, _hl=hl, _pb=pb, _x=xnf_g, _w=wsl: e.matmul(
                        b.psum_f[_pb][:, :],
                        w1win[_w][:, _dc * 1024 + _hl * 128:_dc * 1024 + _hl * 128 + 128],
                        _x[:, _dc * 512:(_dc + 1) * 512],
                        start=(_dc == 0), stop=False),
                        deps=[t_fm, tw1, bank[pb]] if dc == 0 else (), inc=False)
                tm = b.pe(lambda e, _hc=hc, _pb=pb: e.matmul(
                    b.psum_f[_pb][:, :], w4blk[:, _hc * 128:(_hc + 1) * 128], t1all[:, :],
                    start=False, stop=True), deps=[seq_tok["w4"], t_t1], inc=True)
                if hl == 7:
                    w1done[wsl] = tm
                t_h = b.act(lambda e, _hc=hc, _hl=hl, _pb=pb, _hs=hsb: e.activation(
                    _hs[:, _hl * 512:(_hl + 1) * 512], b.psum_f[_pb][:, :], AF.Gelu,
                    bias=st["l1b1_col"][:, _hc:_hc + 1]), [tm, hs_done[wsl]])
                bank[pb] = t_h

            tm = None
            for jc in range(8):
                tm = b.pe(lambda e, _jc=jc, _hs=hsb: e.matmul(
                    b.psum_f[2][0:4, :], od4p[:, _jc * 4:(_jc + 1) * 4],
                    _hs[:, _jc * 512:(_jc + 1) * 512],
                    start=(_jc == 0), stop=(_jc == 7)),
                    deps=[t_h, seq_tok["o4"], bank[2]], inc=(jc == 7))
            tt2 = b.dve(lambda e, _blk=blk: e.tensor_copy(
                t2sb[0:4, _blk * 512:(_blk + 1) * 512], b.psum_f[2][0:4, :]), [tm])
            bank[2] = tt2
            if blk == 3:
                t_t2 = tt2

            tw2 = b.dma("sync", b.sq[6], lambda e, _blk=blk, _w=wsl: e.dma_start(
                out=w2win[_w][:, :].rearrange("p (c o) -> p c o", c=8),
                in_=w2t.rearrange("(c p) o -> p c o", p=128)[:, _blk * 8:(_blk + 1) * 8, :]),
                deps=[w2done[wsl]] + b.l1w_tok)
            w2rd = None
            hs_rd = None
            for t in range(4):
                tsl = t % 2
                if blk == 3:
                    trx = b.dma("sync", b.sq[tsl], lambda e, _g=g, _t=t, _s=tsl: e.dma_start(
                        out=resx[_s][:, :], in_=x_in[_g * 512 + _t * 128:_g * 512 + (_t + 1) * 128, :]),
                        deps=[rx_done[tsl]])
                tms = {}
                ob0 = 3 + (2 * t) % 3
                ob1 = 3 + (2 * t + 1) % 3
                for hl in range(8):
                    for og in range(2):
                        ob = ob0 if og == 0 else ob1
                        tms[og] = b.pe(lambda e, _hl=hl, _t=t, _og=og, _w=wsl, _ob=ob, _hs=hsb,
                                       _st=(blk < 3 and hl == 7): e.matmul(
                            b.psum_f[_ob][:, :],
                            _hs[:, _hl * 512 + _t * 128:_hl * 512 + _t * 128 + 128],
                            w2win[_w][:, _hl * 1024 + _og * 512:_hl * 1024 + _og * 512 + 512],
                            start=(_hl == 0), stop=_st),
                            deps=[t_h, tw2, bank[ob0], bank[ob1]] if (hl == 0 and og == 0) else (),
                            inc=(blk < 3 and hl == 7))
                for og in range(2):
                    ob = ob0 if og == 0 else ob1
                    tm = tms[og]
                    w2rd = tm
                    hs_rd = tm
                    if blk == 3:
                        for cq in range(4):
                            b.pe(lambda e, _cq=cq, _t=t, _og=og, _ob=ob: e.matmul(
                                b.psum_f[_ob][:, :], t2sb[0:4, _cq * 512 + _t * 128:_cq * 512 + _t * 128 + 128],
                                m2c[:, _cq * 1024 + _og * 512:_cq * 1024 + _og * 512 + 512],
                                start=False, stop=False), deps=[t_t2, seq_tok["m2"]] if cq == 0 else (),
                                inc=False)
                        tm = b.pe(lambda e, _og=og, _ob=ob: e.matmul(
                            b.psum_f[_ob][:, :], st["ones_row_f"][:, :],
                            st["l1b2_row"][:, _og * 512:(_og + 1) * 512],
                            start=False, stop=True), deps=[], inc=True)
                        ta = b.dve(lambda e, _t=t, _og=og, _ob=ob: e.tensor_add(
                            acc[:, _t * 1024 + _og * 512:_t * 1024 + (_og + 1) * 512],
                            acc[:, _t * 1024 + _og * 512:_t * 1024 + (_og + 1) * 512],
                            b.psum_f[_ob][:, :]), [tm])
                        to = b.dve(lambda e, _t=t, _og=og, _s=tsl: e.scalar_tensor_tensor(
                            resx[_s][:, _og * 512:(_og + 1) * 512],
                            acc[:, _t * 1024 + _og * 512:_t * 1024 + (_og + 1) * 512], 0.5,
                            resx[_s][:, _og * 512:(_og + 1) * 512],
                            ALU.mult, ALU.add), [ta, trx])
                        bank[ob] = to
                        acc_done[t] = to
                        if og == 1:
                            tst = b.dma("scalar", b.aq[tsl], lambda e, _g=g, _t=t, _s=tsl: e.dma_start(
                                out=x_out[_g * 512 + _t * 128:_g * 512 + (_t + 1) * 128, :],
                                in_=resx[_s][:, :]), deps=[to])
                            rx_done[tsl] = tst
                            res_toks.append(tst)
                    else:
                        if blk == 0:
                            ta = b.dve(lambda e, _t=t, _og=og, _ob=ob: e.tensor_copy(
                                acc[:, _t * 1024 + _og * 512:_t * 1024 + (_og + 1) * 512],
                                b.psum_f[_ob][:, :]), [tm, acc_done[t]])
                        else:
                            ta = b.dve(lambda e, _t=t, _og=og, _ob=ob: e.tensor_add(
                                acc[:, _t * 1024 + _og * 512:_t * 1024 + (_og + 1) * 512],
                                acc[:, _t * 1024 + _og * 512:_t * 1024 + (_og + 1) * 512],
                                b.psum_f[_ob][:, :]), [tm])
                        bank[ob] = ta
            w2done[wsl] = w2rd
            hs_done[wsl] = hs_rd
        xnf_done[gsl] = [t_h, t_t1]
    return res_toks







def emit_lat(b, x_in, x_out, W, scratch):
    nc = b.nc
    st = b.st
    cfg = b.cfg
    NG = cfg["NG"]
    GPS = cfg["SEQ"] // 512

    b.arena_reset()
    xnf = [b.abf(8 * 512) for _ in range(2)]
    q_fm = b.abf(8 * 512)
    k_fm = b.abf(8 * 512)
    v_ext = b.abf(4 * 1032)
    av = b.abf(4 * 512)
    ktm = b.abf(4 * 1024)
    s_ext = b.abf(8 * 1025)
    wqk = [b.abf(8 * 128) for _ in range(2)]
    wv = [b.abf(8 * 512) for _ in range(2)]
    wp = b.abf(8 * 1024)
    nrm = [b.abf(1024) for _ in range(2)]
    nrmf = b.abf(1024)
    tmp1s = [b.abf(512) for _ in range(2)]
    tmp2s = [b.abf(512) for _ in range(2)]
    tmp3s = [b.abf(512) for _ in range(2)]
    gates = [b.abf(512) for _ in range(2)]
    S32 = b.af32(8 * 1024)
    rx = b.af32(1024)
    sc = nc.alloc_sbuf_tensor("lat_sc", [128, 16], F32)
    kcs = nc.alloc_sbuf_tensor("lat_kcs", [128, 8], F32)

    ts = b.ts_bf
    pre = [st["tok_cst"], st["tok_ones"][0], st["tok_ones"][1], st["tok_bvb"]] \
        + st["tok_id"] + st["tok_mk"] + st["tok_l1attn_bias"]
    wq_ap = scratch["wqkv_b"]
    wg_ap = scratch["wg_b"]
    wp_ap = scratch["wp_b"]

    bank = {"tr": None, 0: None, 1: None, 2: None, 3: None, 4: None, 5: None, "b1": None}
    xnf_done = [[], []]
    wqk_done = [None, None]
    wv_done = [None, None]
    wp_done = None
    vext_done = None
    av_done = None
    q_used = None
    k_used = None
    S_done = []
    sext_rd = None
    nrm_done = [None, None]
    nrmf_done = None
    rx_done = None
    tmp_done = {}
    res_toks = []
    wcnt = [0]

    t_vones = b.gp(lambda e: e.memset(
        v_ext[:, :].rearrange("p (t c) -> p t c", t=4)[:, :, 1024:1025], 1.0), pre)
    vones_tok = t_vones

    for c in range(NG):
        cin = c % GPS
        csl = c % 2
        xnf_g = xnf[csl]
        xnf_r = xnf_g.rearrange("p (c t) -> p c t", c=8)
        for j in range(4):
            slot = j % 3
            xn, t_xn, xt, tl = ts.tile(x_in, c * 512 + j * 128, slot, extra_x_deps=pre, ln2=True)
            tev = ts.transpose_to(xn, t_xn, xnf_r[:, :, j * 128:(j + 1) * 128],
                                  None, extra=xnf_done[csl],
                                  gain=st["lng_col"], beta=st["lnb_col"], pb=j % 2)
            ts.xt_done[slot] = t_xn
            ts.xn_done[slot] = tev
            t_fm = tev

        if cin == 0:
            t_s0 = b.dve(lambda e: e.memset(S32[:, :], 0.0), S_done)
            t_k0 = b.dve(lambda e: e.memset(kcs[:, :], 0.0), S_done)
            S_done = [t_s0, t_k0]

        def win_load(src_ap, col0, extra=()):
            wsl = wcnt[0] % 2
            wcnt[0] += 1
            q = ("sync", b.sq[3]) if wsl == 0 else ("scalar", b.aq[5])
            tw = b.dma(q[0], q[1], lambda e, _c0=col0, _w=wsl, _s=src_ap: e.dma_start(
                out=wqk[_w][:, :].rearrange("p (d f) -> p d f", d=8),
                in_=_s.rearrange("(d p) f -> p d f", p=128)[:, :, _c0:_c0 + 128]),
                deps=[wqk_done[wsl], b.wcast.get("wqkv"), b.wcast.get("wg")] + list(extra))
            return tw, wsl

        def fproj(src_ap, col0, pb, extra_mm=()):
            tw, wsl = win_load(src_ap, col0)
            tm = None
            for dc in range(8):
                tm = b.pe(lambda e, _dc=dc, _pb=pb, _w=wsl, _x=xnf_g: e.matmul(
                    b.psum_f[_pb][:, :], wqk[_w][:, _dc * 128:(_dc + 1) * 128],
                    _x[:, _dc * 512:(_dc + 1) * 512],
                    start=(_dc == 0), stop=(_dc == 7)),
                    deps=[t_fm, tw, bank[pb]] + list(extra_mm), inc=(dc == 7))
            wqk_done[wsl] = tm
            return tm

        for fc in range(8):
            fsl = fc % 2
            tmp1, tmp2, tmp3, gate = tmp1s[fsl], tmp2s[fsl], tmp3s[fsl], gates[fsl]
            pb_g = fsl
            pb_k = 2 + fsl
            pb_q = fsl
            tm = fproj(wg_ap, fc * 128, pb_g)
            t_g = b.act(lambda e, _fc=fc, _pb=pb_g, _o=gate: e.activation(
                _o[:, :], b.psum_f[_pb][:, :], AF.Sigmoid,
                bias=st["bg_col"][:, _fc:_fc + 1]), [tm, tmp_done.get(("gate", fsl))])
            bank[pb_g] = t_g
            tm = fproj(wq_ap, 1024 + fc * 128, pb_k)
            t_kg = b.dve(lambda e, _fc=fc, _pb=pb_k, _t1=tmp1, _g=gate: e.scalar_tensor_tensor(
                _t1[:, :], b.psum_f[_pb][:, :], st["bk_col"][:, _fc:_fc + 1], _g[:, :],
                ALU.add, ALU.mult), [tm, t_g, tmp_done.get(("t1", fsl))])
            tmp_done[("gate", fsl)] = t_kg
            t_r1 = b.act(lambda e, _t1=tmp1, _t2=tmp2: e.activation(_t2[:, :], _t1[:, :], AF.Relu,
                                                                    scale=-1.0),
                         [t_kg, tmp_done.get(("t2", fsl))])
            t_e = b.act(lambda e, _t2=tmp2, _t3=tmp3: e.activation(_t3[:, :], _t2[:, :], AF.Exp,
                                                                   scale=-1.0),
                        [t_r1, tmp_done.get(("t3", fsl))])
            t_r2 = b.act(lambda e, _t1=tmp1, _t2=tmp2: e.activation(_t2[:, :], _t1[:, :], AF.Relu),
                         [t_e])
            t_k = b.dve(lambda e, _fc=fc, _t2=tmp2, _t3=tmp3: e.tensor_add(
                k_fm[:, _fc * 512:(_fc + 1) * 512], _t2[:, :], _t3[:, :]), [t_r2, t_e, k_used])
            bank[pb_k] = t_kg
            tm = fproj(wq_ap, fc * 128, pb_q)
            t_r3 = b.act(lambda e, _fc=fc, _pb=pb_q, _t1=tmp1: e.activation(
                _t1[:, :], b.psum_f[_pb][:, :], AF.Relu, scale=-1.0,
                bias=st["nbq_col"][:, _fc:_fc + 1]), [tm, t_k])
            t_e2 = b.act(lambda e, _t1=tmp1, _t3=tmp3: e.activation(_t3[:, :], _t1[:, :], AF.Exp,
                                                                    scale=-1.0),
                         [t_r3, t_k])
            t_r4 = b.act(lambda e, _fc=fc, _pb=pb_q, _t1=tmp1: e.activation(
                _t1[:, :], b.psum_f[_pb][:, :], AF.Relu,
                bias=st["bq_col"][:, _fc:_fc + 1]), [t_e2])
            t_q = b.dve(lambda e, _fc=fc, _t1=tmp1, _t3=tmp3: e.tensor_add(
                q_fm[:, _fc * 512:(_fc + 1) * 512], _t1[:, :], _t3[:, :]), [t_r4, t_e2, q_used])
            bank[pb_q] = t_r4
            tmp_done[("t1", fsl)] = t_q
            tmp_done[("t2", fsl)] = t_k
            tmp_done[("t3", fsl)] = t_q
        t_qk = t_q

        for og in range(2):
            wsl = og
            tw_v = b.dma("scalar", b.aq[4], lambda e, _og=og, _w=wsl: e.dma_start(
                out=wv[_w][:, :].rearrange("p (d f) -> p d f", d=8),
                in_=wq_ap.rearrange("(d p) f -> p d f", p=128)[:, :, 2048 + _og * 512:2048 + (_og + 1) * 512]),
                deps=[wv_done[wsl], b.wcast.get("wqkv")])
            for tt in range(4):
                pb = 2 + tt % 2
                tm = None
                for dc in range(8):
                    tm = b.pe(lambda e, _dc=dc, _tt=tt, _pb=pb, _w=wsl, _x=xnf_g: e.matmul(
                        b.psum_f[_pb][:, :], _x[:, _dc * 512 + _tt * 128:_dc * 512 + _tt * 128 + 128],
                        wv[_w][:, _dc * 512:(_dc + 1) * 512],
                        start=(_dc == 0), stop=False),
                        deps=[t_fm, tw_v, bank[pb]], inc=False)
                tm = b.pe(lambda e, _og=og, _pb=pb: e.matmul(
                    b.psum_f[_pb][:, :], st["ones_row_b"][:, :],
                    st["bv_row_b"][:, _og * 512:(_og + 1) * 512],
                    start=False, stop=True), deps=[], inc=True)
                if tt == 3:
                    wv_done[wsl] = tm
                t_ve = b.act(lambda e, _tt=tt, _og=og, _pb=pb: e.activation(
                    v_ext[:, _tt * 1032 + _og * 512:_tt * 1032 + _og * 512 + 512],
                    b.psum_f[_pb][:, :], AF.Copy), [tm, vext_done, vones_tok])
                bank[pb] = t_ve
        t_v = t_ve

        t_av = []
        for stl in range(4):
            pb = stl % 2
            tm = None
            for dc in range(8):
                tm = b.pe(lambda e, _dc=dc, _stl=stl, _pb=pb: e.matmul(
                    b.psum_f[_pb][:, :], k_fm[:, _dc * 512 + _stl * 128:_dc * 512 + _stl * 128 + 128],
                    q_fm[:, _dc * 512:(_dc + 1) * 512],
                    start=(_dc == 0), stop=(_dc == 7)),
                    deps=[t_qk, bank[pb]], inc=(dc == 7))
            t_c2 = b.dve(lambda e, _stl=stl, _pb=pb: e.tensor_mul(
                av[:, _stl * 512:(_stl + 1) * 512], b.psum_f[_pb][:, :],
                st["mk_b"][:, _stl * 512:(_stl + 1) * 512]), [tm, av_done])
            bank[pb] = t_c2
            t_av.append(t_c2)

        tw_p = None
        for dc in range(0, 8, 4):
            tw_p = b.dma("sync", b.sq[6], lambda e, _dc=dc: e.dma_start(
                out=wp[:, _dc * 1024:(_dc + 4) * 1024].rearrange("p (c k) -> p c k", c=4),
                in_=wp_ap.rearrange("(c p) f -> p c f", p=128)[:, _dc:_dc + 4, :]),
                deps=[wp_done, b.wcast.get("wp")])

        out_mm_toks = []
        for tt in range(4):
            nsl = tt % 2
            groups = []
            tm4 = tm5 = tmd = None
            first = True
            for stl in range(tt + 1):
                lastin = (cin == 0 and stl == tt)
                tm4 = b.pe(lambda e, _stl=stl, _tt=tt, _f=first, _l=lastin: e.matmul(
                    b.psum_f[4][:, :], av[:, _stl * 512 + _tt * 128:_stl * 512 + _tt * 128 + 128],
                    v_ext[:, _stl * 1032:_stl * 1032 + 512],
                    start=_f, stop=_l), deps=[t_av[stl], t_v, bank[4]], inc=lastin)
                tm5 = b.pe(lambda e, _stl=stl, _tt=tt, _f=first, _l=lastin: e.matmul(
                    b.psum_f[5][:, :], av[:, _stl * 512 + _tt * 128:_stl * 512 + _tt * 128 + 128],
                    v_ext[:, _stl * 1032 + 512:_stl * 1032 + 1024],
                    start=_f, stop=_l), deps=[bank[5]], inc=lastin)
                tmd = b.pe(lambda e, _stl=stl, _tt=tt, _f=first, _l=lastin: e.matmul(
                    b.psum_f[3][:, 0:1], av[:, _stl * 512 + _tt * 128:_stl * 512 + _tt * 128 + 128],
                    v_ext[:, _stl * 1032 + 1024:_stl * 1032 + 1025],
                    start=_f, stop=_l), deps=[bank[3]], inc=lastin)
                first = False
            if cin > 0:
                for dc in range(8):
                    last = dc == 7
                    tm4 = b.pe(lambda e, _dc=dc, _tt=tt, _l=last: e.matmul(
                        b.psum_f[4][:, :], q_fm[:, _dc * 512 + _tt * 128:_dc * 512 + _tt * 128 + 128],
                        s_ext[:, _dc * 1025:_dc * 1025 + 512],
                        start=False, stop=_l), deps=[t_qk] + S_done, inc=last)
                    tm5 = b.pe(lambda e, _dc=dc, _tt=tt, _l=last: e.matmul(
                        b.psum_f[5][:, :], q_fm[:, _dc * 512 + _tt * 128:_dc * 512 + _tt * 128 + 128],
                        s_ext[:, _dc * 1025 + 512:_dc * 1025 + 1024],
                        start=False, stop=_l), deps=[], inc=last)
                    tmd = b.pe(lambda e, _dc=dc, _tt=tt, _l=last: e.matmul(
                        b.psum_f[3][:, 0:1], q_fm[:, _dc * 512 + _tt * 128:_dc * 512 + _tt * 128 + 128],
                        s_ext[:, _dc * 1025 + 1024:_dc * 1025 + 1025],
                        start=False, stop=_l), deps=[], inc=last)
            out_mm_toks += [tm4, tm5, tmd]
            t_d1 = b.dve(lambda e: e.tensor_single_scalar(sc[:, 0:1], b.psum_f[3][:, 0:1],
                                                          DEN_EPS, ALU.add), [tmd])
            t_rc = b.dve(lambda e: e.reciprocal(sc[:, 1:2], sc[:, 0:1]), [t_d1])
            t_n0 = b.dve(lambda e, _n=nsl: e.tensor_scalar(
                nrm[_n][:, 0:512], b.psum_f[4][:, :], sc[:, 1:2], None, ALU.mult),
                [t_rc, tm4, nrm_done[nsl]])
            t_n1 = b.dve(lambda e, _n=nsl: e.tensor_scalar(
                nrm[_n][:, 512:1024], b.psum_f[5][:, :], sc[:, 1:2], None, ALU.mult),
                [t_rc, tm5])
            bank[4] = t_n0
            bank[5] = t_n1
            bank[3] = t_d1
            tt_tr = None
            for ch in range(8):
                tt_tr = b.pe(lambda e, _ch=ch, _n=nsl: e.transpose(
                    b.psum_b[1][:, _ch * 128:(_ch + 1) * 128],
                    nrm[_n][:, _ch * 128:(_ch + 1) * 128], st["id_b"][:, :]),
                    deps=[t_n0, t_n1, b.pbd[1]], inc=(ch == 7))
            t_nf = b.dve(lambda e: e.tensor_copy(
                nrmf[:, :].rearrange("p (c t) -> p c t", c=8),
                b.psum_b[1][:, :].rearrange("p (c t) -> p c t", c=8)), [tt_tr, nrmf_done])
            b.pbd[1] = t_nf
            nrm_done[nsl] = t_nf
            trx = b.dma("sync", b.sq[2], lambda e, _c=c, _tt=tt: e.dma_start(
                out=rx[:, :], in_=x_in[_c * 512 + _tt * 128:_c * 512 + (_tt + 1) * 128, :]),
                deps=[rx_done])
            for og in range(2):
                pb = 4 if og == 0 else 5
                tm = None
                for dc in range(8):
                    tm = b.pe(lambda e, _dc=dc, _og=og, _pb=pb: e.matmul(
                        b.psum_f[_pb][:, :], nrmf[:, _dc * 128:(_dc + 1) * 128],
                        wp[:, _dc * 1024 + _og * 512:_dc * 1024 + _og * 512 + 512],
                        start=(_dc == 0), stop=False),
                        deps=[t_nf, tw_p, bank[pb]], inc=False)
                tm = b.pe(lambda e, _og=og, _pb=pb: e.matmul(
                    b.psum_f[_pb][:, :], st["ones_row_f"][:, :],
                    st["b_proj_row"][:, _og * 512:(_og + 1) * 512],
                    start=False, stop=True), deps=[], inc=True)
                wp_done = tm
                t_o = b.dve(lambda e, _og=og, _pb=pb: e.tensor_add(
                    rx[:, _og * 512:(_og + 1) * 512], b.psum_f[_pb][:, :],
                    rx[:, _og * 512:(_og + 1) * 512]), [tm, trx])
                bank[pb] = t_o
            tst = b.dma("scalar", b.aq[tt % 2], lambda e, _c=c, _tt=tt: e.dma_start(
                out=x_out[_c * 512 + _tt * 128:_c * 512 + (_tt + 1) * 128, :], in_=rx[:, :]),
                deps=[bank[4], bank[5]])
            rx_done = tst
            res_toks.append(tst)
            nrmf_done = tm

        if cin < GPS - 1:
            ktm_ev = []
            for stl in range(4):
                tt_tr = None
                for ch in range(8):
                    tt_tr = b.pe(lambda e, _ch=ch, _stl=stl: e.transpose(
                        b.psum_b[1][:, _ch * 128:(_ch + 1) * 128],
                        k_fm[:, _ch * 512 + _stl * 128:_ch * 512 + _stl * 128 + 128],
                        st["id_b"][:, :]), deps=[t_qk, b.pbd[1]], inc=(ch == 7))
                t_ke = b.dve(lambda e, _stl=stl: e.tensor_copy(
                    ktm[:, _stl * 1024:(_stl + 1) * 1024].rearrange("p (c t) -> p c t", c=8),
                    b.psum_b[1][:, :].rearrange("p (c t) -> p c t", c=8)), [tt_tr])
                b.pbd[1] = t_ke
                ktm_ev.append(t_ke)
            new_S_done = []
            for d1c in range(8):
                for og in range(2):
                    pb = 2 + (d1c * 2 + og) % 2
                    tm = None
                    for stl in range(4):
                        tm = b.pe(lambda e, _stl=stl, _d=d1c, _og=og, _pb=pb: e.matmul(
                            b.psum_f[_pb][:, :], ktm[:, _stl * 1024 + _d * 128:_stl * 1024 + _d * 128 + 128],
                            v_ext[:, _stl * 1032 + _og * 512:_stl * 1032 + _og * 512 + 512],
                            start=(_stl == 0), stop=(_stl == 3)),
                            deps=[ktm_ev[stl], t_v, bank[pb]] + out_mm_toks[-3:], inc=(stl == 3))
                    t_sa = b.dve(lambda e, _d=d1c, _og=og, _pb=pb: e.tensor_add(
                        S32[:, _d * 1024 + _og * 512:_d * 1024 + _og * 512 + 512],
                        S32[:, _d * 1024 + _og * 512:_d * 1024 + _og * 512 + 512],
                        b.psum_f[_pb][:, :]), [tm] + S_done)
                    t_sc = b.dve(lambda e, _d=d1c, _og=og: e.tensor_copy(
                        s_ext[:, _d * 1025 + _og * 512:_d * 1025 + _og * 512 + 512],
                        S32[:, _d * 1024 + _og * 512:_d * 1024 + _og * 512 + 512]), [t_sa])
                    bank[pb] = t_sa
                    new_S_done.append(t_sc)
            for dc in range(8):
                t_rs = b.dve(lambda e, _dc=dc: e.reduce_sum(
                    sc[:, 2:3], k_fm[:, _dc * 512:(_dc + 1) * 512], axis=AX.X), [t_qk])
                t_ka = b.dve(lambda e, _dc=dc: e.tensor_add(
                    kcs[:, _dc:_dc + 1], kcs[:, _dc:_dc + 1], sc[:, 2:3]), [t_rs] + S_done)
                t_kc = b.dve(lambda e, _dc=dc: e.tensor_copy(
                    s_ext[:, _dc * 1025 + 1024:_dc * 1025 + 1025], kcs[:, _dc:_dc + 1]), [t_ka])
                new_S_done.append(t_kc)
            S_done = new_S_done
            vext_done = tm
            k_used = tm
        else:
            vext_done = out_mm_toks[-3]
            k_used = t_av[-1]
        av_done = out_mm_toks[-3]
        q_used = out_mm_toks[-3]
        xnf_done[csl] = [t_v, t_qk]
    return res_toks



def emit_mla(b, x_in, x_out, W, scratch):
    nc = b.nc
    st = b.st
    cfg = b.cfg
    NG = cfg["NG"]
    SEQ = cfg["SEQ"]
    GPS = SEQ // 512
    NSEQ = cfg["TOK"] // SEQ
    NTILE_SEQ = SEQ // 128
    qscr = scratch["qscr"]

    b.arena_reset()
    wdkv = b.abf(8 * 256)
    wuk = b.abf(2 * 1024)
    wuv = b.abf(2 * 1024)
    wuq = [b.abf(8 * 128) for _ in range(2)]
    wo = b.abf(8 * 512)
    xnf = b.abf(8 * 512)
    ckv_g = b.abf(2 * 512)
    k_fm = b.abf(8 * SEQ)
    v_ext = b.abf(NTILE_SEQ * 1040)
    qld = b.abf(8 * 512)
    qst = [b.abf(512) for _ in range(2)]
    pT = [b.abf(512) for _ in range(3)]
    ctxn = b.abf(4 * 1024)
    ctxf = b.abf(1024)
    rx = [b.af32(1024) for _ in range(2)]
    sc = nc.alloc_sbuf_tensor("mla_sc", [128, 8], F32)

    ts = b.ts_bf
    pre = [st["tok_cst"], st["tok_ones"][1], st["tok_ones"][3]] + st["tok_id"] + st["tok_mk"]

    tws = []
    for dc in range(8):
        tws.append(b.dma("gpsimd", b.gq[6], lambda e, _dc=dc: e.dma_start(
            out=wdkv[:, _dc * 256:(_dc + 1) * 256],
            in_=W["L0_attn_W_DKV"][_dc * 128:(_dc + 1) * 128, :])))
    for lc in range(2):
        tws.append(b.dma("gpsimd", b.gq[6], lambda e, _lc=lc: e.dma_start(
            out=wuk[:, _lc * 1024:(_lc + 1) * 1024],
            in_=W["L0_attn_W_UK"][_lc * 128:(_lc + 1) * 128, :])))
        tws.append(b.dma("gpsimd", b.gq[6], lambda e, _lc=lc: e.dma_start(
            out=wuv[:, _lc * 1024:(_lc + 1) * 1024],
            in_=W["L0_attn_W_UV"][_lc * 128:(_lc + 1) * 128, :])))
    tw_res = tws[-1]

    bank = {"tr": None, 0: None, 1: None, 2: None, 3: None, 4: None, 5: None, "b1": None}
    xnf_done = []
    ckv_done = []
    kfm_done = []
    vext_done = []
    qst_done = [None, None]
    wuq_done = [None, None]
    qld_done = None
    pT_done = [None, None, None]
    ctxn_done = [None] * 4
    ctxf_done = None
    wo_done = None
    rx_done = [None, None]
    res_toks = []
    t_vones = b.gp(lambda e: e.memset(
        v_ext[:, :].rearrange("p (t h c) -> p t h c", t=NTILE_SEQ, h=16)[:, :, :, 64:65], 1.0), pre)

    for s in range(NSEQ):
        s0 = s * SEQ
        new_kfm_done = []
        new_vext_done = []
        new_ckv_done = []
        for g in range(GPS):
            t0 = s0 + g * 512
            xnf_r = xnf.rearrange("p (c t) -> p c t", c=8)
            for j in range(4):
                slot = j % 3
                xn, t_xn, xt, tl = ts.tile(x_in, t0 + j * 128, slot, extra_x_deps=pre)
                tev = ts.transpose_to(xn, t_xn, xnf_r[:, :, j * 128:(j + 1) * 128],
                                      None, extra=xnf_done, pb=j % 2)
                ts.xt_done[slot] = t_xn
                ts.xn_done[slot] = tev
                t_fm = tev
            t_ckv = None
            for lc in range(2):
                pb = lc
                tm = None
                for dc in range(8):
                    tm = b.pe(lambda e, _dc=dc, _lc=lc, _pb=pb: e.matmul(
                        b.psum_f[_pb][:, :], wdkv[:, _dc * 256 + _lc * 128:_dc * 256 + _lc * 128 + 128],
                        xnf[:, _dc * 512:(_dc + 1) * 512],
                        start=(_dc == 0), stop=(_dc == 7)),
                        deps=[t_fm, tw_res, bank[pb]] + ckv_done, inc=(dc == 7))
                t_ckv = b.dve(lambda e, _lc=lc, _pb=pb: e.tensor_copy(
                    ckv_g[:, _lc * 512:(_lc + 1) * 512], b.psum_f[_pb][:, :]), [tm])
                bank[pb] = t_ckv
            new_ckv_done = [t_ckv]
            for fc in range(8):
                pb = fc % 2
                tm = None
                for lc in range(2):
                    tm = b.pe(lambda e, _fc=fc, _lc=lc, _pb=pb: e.matmul(
                        b.psum_f[_pb][:, :], wuk[:, _lc * 1024 + _fc * 128:_lc * 1024 + _fc * 128 + 128],
                        ckv_g[:, _lc * 512:(_lc + 1) * 512],
                        start=(_lc == 0), stop=(_lc == 1)),
                        deps=[t_ckv, bank[pb]] + kfm_done, inc=(lc == 1))
                t_k = b.dve(lambda e, _fc=fc, _g=g, _pb=pb: e.tensor_copy(
                    k_fm[:, _fc * SEQ + _g * 512:_fc * SEQ + (_g + 1) * 512], b.psum_f[_pb][:, :]), [tm])
                bank[pb] = t_k
                new_kfm_done.append(t_k)
            for fc in range(8):
                wsl = fc % 2
                pb = fc % 3
                tw = b.dma("sync", b.sq[3], lambda e, _fc=fc, _w=wsl: e.dma_start(
                    out=wuq[_w][:, :].rearrange("p (d f) -> p d f", d=8),
                    in_=scratch["wuq_b"].rearrange("(d p) f -> p d f", p=128)[:, :, _fc * 128:(_fc + 1) * 128]),
                    deps=[wuq_done[wsl], b.wcast.get("wuq")])
                tm = None
                for dc in range(8):
                    tm = b.pe(lambda e, _dc=dc, _pb=pb, _w=wsl: e.matmul(
                        b.psum_f[_pb][:, :], wuq[_w][:, _dc * 128:(_dc + 1) * 128],
                        xnf[:, _dc * 512:(_dc + 1) * 512],
                        start=(_dc == 0), stop=(_dc == 7)),
                        deps=[t_fm, tw, bank[pb]], inc=(dc == 7))
                wuq_done[wsl] = tm
                t_q = b.act(lambda e, _pb=pb, _w=wsl: e.activation(
                    qst[_w][:, :], b.psum_f[_pb][:, :], AF.Copy), [tm, qst_done[wsl]])
                bank[pb] = t_q
                qst_done[wsl] = b.dma("sync", b.sq[4 + wsl], lambda e, _fc=fc, _g=g, _s=s, _w=wsl: e.dma_start(
                    out=qscr[_fc * 128:(_fc + 1) * 128, _s * SEQ + _g * 512:_s * SEQ + (_g + 1) * 512],
                    in_=qst[_w][:, :]), deps=[t_q])
            for tt in range(4):
                gt = g * 4 + tt
                pb = tt % 2
                for og in range(2):
                    tm = None
                    for lc in range(2):
                        tm = b.pe(lambda e, _tt=tt, _lc=lc, _og=og, _pb=pb: e.matmul(
                            b.psum_f[_pb][:, :], ckv_g[:, _lc * 512 + _tt * 128:_lc * 512 + _tt * 128 + 128],
                            wuv[:, _lc * 1024 + _og * 512:_lc * 1024 + _og * 512 + 512],
                            start=(_lc == 0), stop=(_lc == 1)),
                            deps=[t_ckv, bank[pb]] + vext_done, inc=(lc == 1))
                    t_v = b.act(lambda e, _gt=gt, _og=og, _pb=pb: e.activation(
                        v_ext[:, :].rearrange("p (t h c) -> p t h c", t=NTILE_SEQ, h=16)
                        [:, _gt, _og * 8:(_og + 1) * 8, 0:64],
                        b.psum_f[_pb][:, :].rearrange("p (h c) -> p h c", h=8), AF.Copy), [tm, t_vones])
                    bank[pb] = t_v
                    new_vext_done.append(t_v)
        xnf_done = [t_q, t_ckv]
        ckv_done = new_ckv_done
        kfm_done_prev = kfm_done
        kfm_done = new_kfm_done
        vext_done = new_vext_done
        t_kv_ready = [kfm_done[-1], new_vext_done[-1], t_vones]

        for qtg in range(GPS):
            tq = b.dma("sync", b.sq[6], lambda e, _s=s, _qtg=qtg: e.dma_start(
                out=qld[:, :].rearrange("p (c t) -> p c t", c=8),
                in_=qscr.rearrange("(c p) t -> p c t", p=128)[:, :, _s * SEQ + _qtg * 512:_s * SEQ + (_qtg + 1) * 512]),
                deps=[qld_done, qst_done[0], qst_done[1]])
            jmax = 4 * qtg + 3
            ctx_last = None
            ctx_banks_done = [bank[0], bank[1], bank[2], bank[3]]
            for h in range(16):
                hb = (h % 2) * 64
                fcq = h // 2
                pend = []

                def emit_ctx(pend):
                    nonlocal ctx_last
                    j, psl, t_e = pend
                    for tt in range(4):
                        if j > 4 * qtg + tt:
                            continue
                        first = (j == 0)
                        lastj = (j == 4 * qtg + tt)
                        tm_c = b.pe(lambda e, _h=h, _j=j, _tt=tt, _sl=psl, _f=first, _l=lastj: e.matmul(
                            b.psum_f[_tt][:, 0:65],
                            pT[_sl][:, _tt * 128:(_tt + 1) * 128],
                            v_ext[:, _j * 1040 + _h * 65:_j * 1040 + (_h + 1) * 65],
                            start=_f, stop=_l),
                            deps=[t_e] + ([ctx_banks_done[tt]] if first else []),
                            inc=lastj)
                        if lastj:
                            ctx_last = tm_c
                        pT_done[psl] = tm_c

                for j in range(jmax + 1):
                    skey = 4 + j % 2
                    tm_s = b.pe(lambda e, _h=h, _j=j, _hb=hb, _fcq=fcq, _pb=skey: e.matmul(
                        b.psum_f[_pb][:, :],
                        k_fm[_hb:_hb + 64, _fcq * SEQ + _j * 128:_fcq * SEQ + (_j + 1) * 128],
                        qld[_hb:_hb + 64, _fcq * 512:(_fcq + 1) * 512],
                        start=True, stop=True),
                        deps=[tq, bank[skey]] + t_kv_ready, inc=True)
                    psl = j % 3
                    t_e = b.act(lambda e, _pb=skey, _sl=psl: e.activation(
                        pT[_sl][:, :], b.psum_f[_pb][:, :], AF.Exp, scale=0.125),
                        [tm_s, pT_done[psl]])
                    bank[skey] = t_e
                    a = j - 4 * qtg
                    if 0 <= a <= 3:
                        t_e = b.dve(lambda e, _sl=psl, _a=a: e.tensor_mul(
                            pT[_sl][:, :], pT[_sl][:, :],
                            st["mk_b"][:, _a * 512:(_a + 1) * 512]), [t_e])
                    pend.append((j, psl, t_e))
                    if len(pend) > 2:
                        emit_ctx(pend.pop(0))
                for p_ in pend:
                    emit_ctx(p_)
                pend = []
                for tt in range(4):
                    t_d = b.dve(lambda e, _tt=tt: e.reciprocal(sc[:, _tt:_tt + 1],
                                                               b.psum_f[_tt][:, 64:65]), [ctx_last])
                    t_n = b.dve(lambda e, _h=h, _tt=tt: e.tensor_scalar(
                        ctxn[:, _tt * 1024 + _h * 64:_tt * 1024 + (_h + 1) * 64],
                        b.psum_f[_tt][:, 0:64], sc[:, _tt:_tt + 1], None, ALU.mult),
                        [t_d, ctxn_done[tt]])
                    ctx_banks_done[tt] = t_n
                    bank[tt] = t_n
            for tt in range(4):
                ttg = qtg * 4 + tt
                trc = None
                for ch in range(8):
                    trc = b.pe(lambda e, _ch=ch, _tt=tt: e.transpose(
                        b.psum_b[1][:, _ch * 128:(_ch + 1) * 128],
                        ctxn[:, _tt * 1024 + _ch * 128:_tt * 1024 + _ch * 128 + 128],
                        st["id_b"][:, :]), deps=[bank[tt], b.pbd[1]], inc=(ch == 7))
                t_cf = b.dve(lambda e: e.tensor_copy(
                    ctxf[:, :].rearrange("p (c t) -> p c t", c=8),
                    b.psum_b[1][:, :].rearrange("p (c t) -> p c t", c=8)), [trc, ctxf_done])
                b.pbd[1] = t_cf
                ctxn_done[tt] = t_cf
                rsl = tt % 2
                trx = b.dma("sync", b.sq[rsl], lambda e, _s=s, _ttg=ttg, _r=rsl: e.dma_start(
                    out=rx[_r][:, :], in_=x_in[_s * SEQ + _ttg * 128:_s * SEQ + (_ttg + 1) * 128, :]),
                    deps=[rx_done[rsl]])
                for og in range(2):
                    pb = 4 + og
                    tw_o = b.dma("scalar", b.aq[4], lambda e, _og=og: e.dma_start(
                        out=wo[:, :].rearrange("p (d f) -> p d f", d=8),
                        in_=scratch["wo_b"].rearrange("(d p) f -> p d f", p=128)[:, :, _og * 512:(_og + 1) * 512]),
                        deps=[wo_done, b.wcast.get("wo")])
                    tm = None
                    for dc in range(8):
                        tm = b.pe(lambda e, _dc=dc, _og=og, _pb=pb: e.matmul(
                            b.psum_f[_pb][:, :], ctxf[:, _dc * 128:(_dc + 1) * 128],
                            wo[:, _dc * 512:(_dc + 1) * 512],
                            start=(_dc == 0), stop=(_dc == 7)),
                            deps=[t_cf, tw_o, bank[pb]], inc=(dc == 7))
                    wo_done = tm
                    t_o = b.dve(lambda e, _og=og, _pb=pb, _r=rsl: e.tensor_add(
                        rx[_r][:, _og * 512:(_og + 1) * 512], b.psum_f[_pb][:, :],
                        rx[_r][:, _og * 512:(_og + 1) * 512]), [tm, trx])
                    bank[pb] = t_o
                tst = b.dma("scalar", b.aq[rsl], lambda e, _s=s, _ttg=ttg, _r=rsl: e.dma_start(
                    out=x_out[_s * SEQ + _ttg * 128:_s * SEQ + (_ttg + 1) * 128, :], in_=rx[_r][:, :]),
                    deps=[bank[4], bank[5]])
                rx_done[rsl] = tst
                ctxf_done = tm
                res_toks.append(tst)
            qld_done = ctx_last
    return res_toks


# version 19
# speedup vs baseline: 1.0097x; 1.0051x over previous
import numpy as np
import ml_dtypes
import concourse.bass as bass
import concourse.mybir as mybir
from concourse.bass_utils import run_bass_kernel_spmd
from concourse.bass_core_rust import IndirectOffsetOnAxis

F32 = mybir.dt.float32
BF16 = mybir.dt.bfloat16
I32 = mybir.dt.int32
U32 = mybir.dt.uint32
AF = mybir.ActivationFunctionType
ALU = mybir.AluOpType
AX = mybir.AxisListType

D = 1024
CGRP = 160
LN_EPS = 1e-5
DEN_EPS = 1e-6
L1_H = 4096

ENGINES = ("sync", "tensor", "vector", "scalar", "gpsimd")


class Sem:
    def __init__(self, nc, name):
        self.h = nc.alloc_semaphore(name)
        self.n = 0
        self.name = name
        self.last = None

    def bump(self, k):
        self.n += k
        return self.n

    def tok(self):
        return (self, self.n)


class B:

    def __init__(self, nc, cfg):
        self.nc = nc
        self.cfg = cfg
        self.streams = {e: [] for e in ENGINES}
        self.maxwait = {e: {} for e in ENGINES}
        self.eP = Sem(nc, "eP")
        self.eV = Sem(nc, "eV")
        self.eA = Sem(nc, "eA")
        self.eG = Sem(nc, "eG")
        self.sq = [Sem(nc, f"sq{i}") for i in range(8)]
        self.gq = [Sem(nc, f"gq{i}") for i in range(8)]
        self.aq = [Sem(nc, f"aq{i}") for i in range(6)]
        self.psum_f = [nc.alloc_psum_tensor(f"psf{i}", [128, 512], F32) for i in range(6)]
        self.psum_b = [nc.alloc_psum_tensor(f"psb{i}", [128, 1024], BF16) for i in range(2)]
        self.pbd = {0: None, 1: None}

    def make_arenas(self, bf_cols, f32_cols):
        self.ar_bf = self.nc.alloc_sbuf_tensor("ar_bf", [128, bf_cols], BF16)
        self.ar_f = self.nc.alloc_sbuf_tensor("ar_f", [128, f32_cols], F32)
        self.ar_off = {"bf": 0, "f": 0}

    def arena_reset(self):
        self.ar_off = {"bf": 0, "f": 0}

    def abf(self, cols):
        o = self.ar_off["bf"]
        assert o + cols <= self.ar_bf.shape[1], f"bf arena overflow {o}+{cols}"
        self.ar_off["bf"] = o + cols
        return self.ar_bf[:, o:o + cols]

    def af32(self, cols):
        o = self.ar_off["f"]
        assert o + cols <= self.ar_f.shape[1], f"f32 arena overflow {o}+{cols}"
        self.ar_off["f"] = o + cols
        return self.ar_f[:, o:o + cols]

    def _waits(self, eng, deps):
        out = []
        mw = self.maxwait[eng]
        for d in deps:
            if d is None:
                continue
            sem, val = d
            if val <= 0:
                continue
            if mw.get(sem.name, -1) >= val:
                continue
            mw[sem.name] = val
            out.append((sem.h, val))
        return out

    def op(self, eng, f, deps=(), inc=None, inc_by=1):
        waits = self._waits(eng, deps)
        tok = None
        if inc is not None:
            tok = (inc, inc.bump(inc_by))

        def run(e, waits=waits, f=f, inc=inc, inc_by=inc_by):
            for (h, v) in waits:
                e.wait_ge(h, v)
            ins = f(e)
            if inc is not None and ins is not None:
                ins.then_inc(inc.h, inc_by)
        self.streams[eng].append(run)
        return tok

    def pe(self, f, deps=(), inc=True):
        return self.op("tensor", f, deps, self.eP if inc else None)

    def dve(self, f, deps=()):
        return self.op("vector", f, deps, self.eV)

    def act(self, f, deps=()):
        return self.op("scalar", f, deps, self.eA)

    def gp(self, f, deps=()):
        return self.op("gpsimd", f, deps, self.eG)

    def dma(self, queue, slot, f, deps=()):
        deps = list(deps)
        if slot.last is not None:
            deps.append(slot.last)
        tok = self.op(queue, f, deps, slot, inc_by=16)
        slot.last = tok
        return tok

    def fence(self):
        toks = [self.eP.tok(), self.eV.tok(), self.eA.tok(), self.eG.tok()]
        toks += [s.tok() for s in self.sq + self.gq + self.aq]
        for eng in ENGINES:
            waits = self._waits(eng, toks)
            if waits:
                def run(e, waits=waits):
                    for (h, v) in waits:
                        e.wait_ge(h, v)
                self.streams[eng].append(run)

    def assemble(self):
        block = self.nc.Block()
        with block:
            for eng in ENGINES:
                ops = self.streams[eng]

                def make(ops):
                    def body(e):
                        for f in ops:
                            f(e)
                    return body
                getattr(block, eng)(make(ops))



def preamble(b, W):
    nc = b.nc
    cfg = b.cfg
    st = {}

    consts = [LN_EPS, -0.5, 0.5, 1.0 / 1024.0, DEN_EPS, 1e9, 1e4,
              float(cfg["CAP"]), 0.125, 1e-10, 1.0 + LN_EPS]
    consts = sorted(set(consts))
    cst = nc.alloc_sbuf_tensor("cst", [128, len(consts)], F32)
    tok_cst = []
    for i, v in enumerate(consts):
        nc.const_aps.register(F32, v, cst[:, i:i + 1])
        tok_cst.append(b.gp(lambda e, _i=i, _v=v: e.memset(cst[:, _i:_i + 1], _v)))
    st["tok_cst"] = tok_cst[-1]

    b.arena_reset()
    iv = b.af32(128)
    st["id_f"] = nc.alloc_sbuf_tensor("id_f", [128, 128], F32)
    st["id_b"] = nc.alloc_sbuf_tensor("id_b", [128, 128], BF16)
    st["u01"] = nc.alloc_sbuf_tensor("u01", [128, 128], F32)
    t_iv = b.gp(lambda e: e.iota(iv[:, :], pattern=[[-1, 128]], base=0,
                                 channel_multiplier=1, allow_small_or_imprecise_dtypes=True))
    t1 = b.dve(lambda e: e.tensor_single_scalar(st["id_f"][:, :], iv[:, :], 0.0, ALU.is_equal), [t_iv, st["tok_cst"]])
    t2 = b.dve(lambda e: e.tensor_single_scalar(st["id_b"][:, :], iv[:, :], 0.0, ALU.is_equal), [t_iv])
    t3 = b.dve(lambda e: e.tensor_single_scalar(st["u01"][:, :], iv[:, :], 0.0, ALU.is_lt), [t_iv])
    st["tok_id"] = [t1, t2, t3]

    miv = b.af32(512)
    st["mk_b"] = nc.alloc_sbuf_tensor("mk_b", [128, 4 * 512], BF16)
    tmks = []
    for a in range(4):
        ti = b.gp(lambda e, _a=a: e.iota(miv[:, :], pattern=[[1, 512]], base=-128 * _a,
                                         channel_multiplier=-1, allow_small_or_imprecise_dtypes=True),
                  deps=tmks[-1:])
        tmks.append(b.dve(lambda e, _a=a: e.tensor_single_scalar(
            st["mk_b"][:, _a * 512:(_a + 1) * 512], miv[:, :], 0.0, ALU.is_ge), [ti]))
    st["tok_mk"] = tmks[-1:]

    st["iota8"] = nc.alloc_sbuf_tensor("iota8", [128, 8], F32)
    st["tok_iota8"] = b.gp(lambda e: e.iota(st["iota8"][:, :], pattern=[[1, 8]], base=0,
                                            channel_multiplier=0, allow_small_or_imprecise_dtypes=True))

    st["ones128_f"] = nc.alloc_sbuf_tensor("ones128_f", [128, 128], F32)
    st["tok_ones128"] = b.gp(lambda e: e.memset(st["ones128_f"][:, :], 1.0))
    st["ones_row_f"] = nc.alloc_sbuf_tensor("ones_row_f", [1, 128], F32)
    st["ones_row_b"] = nc.alloc_sbuf_tensor("ones_row_b", [1, 128], BF16)
    st["ones_col_f"] = nc.alloc_sbuf_tensor("ones_col_f", [128, 1], F32)
    st["ones_col_b"] = nc.alloc_sbuf_tensor("ones_col_b", [128, 1], BF16)
    st["tok_ones"] = [
        b.gp(lambda e: e.memset(st["ones_row_f"][:, :], 1.0)),
        b.gp(lambda e: e.memset(st["ones_row_b"][:, :], 1.0)),
        b.gp(lambda e: e.memset(st["ones_col_f"][:, :], 1.0)),
        b.gp(lambda e: e.memset(st["ones_col_b"][:, :], 1.0)),
    ]

    ps_t = b.psum_f[0]
    brow = b.af32(1024)[0:1, :]
    brow_state = {"tok": None}

    def bias_cols(name, row_ap, F):
        cols = nc.alloc_sbuf_tensor(f"{name}_col", [128, F // 128], F32)
        tev = None
        for c0 in range(0, F, 1024):
            w = min(1024, F - c0)
            tl = b.dma("sync", b.sq[7], lambda e, _c0=c0, _w=w: e.dma_start(
                out=brow[:, 0:_w], in_=row_ap[:, _c0:_c0 + _w]), deps=[brow_state["tok"]])
            for c in range(w // 128):
                cc = c0 // 128 + c
                tt = b.pe(lambda e, _c=c, _cc=cc: e.transpose(
                    ps_t[:, (_cc % 4) * 128:(_cc % 4) * 128 + 1],
                    brow[:, _c * 128:(_c + 1) * 128],
                    st["id_f"][0:1, 0:1]),
                    deps=[tl, t1, tev])
                tev = b.dve(lambda e, _cc=cc: e.tensor_copy(
                    cols[:, _cc:_cc + 1], ps_t[:, (_cc % 4) * 128:(_cc % 4) * 128 + 1]), [tt])
            brow_state["tok"] = tev
        return cols, tev

    if "L1_attn_b_qkv" in W:
        st["bv_row"] = nc.alloc_sbuf_tensor("bv_row", [1, 1024], F32)
        tr0 = b.dma("sync", b.sq[7], lambda e: e.dma_start(
            out=st["bv_row"][:, :], in_=W["L1_attn_b_qkv"][None, 2048:3072]))
        st["bv_row_b"] = nc.alloc_sbuf_tensor("bv_row_b", [1, 1024], BF16)
        st["tok_bvb"] = b.dve(lambda e: e.tensor_copy(st["bv_row_b"][:, :], st["bv_row"][:, :]), [tr0])
        st["bq_col"], tq = bias_cols("bq", W["L1_attn_b_qkv"][None, 0:1024], 1024)
        st["bk_col"], tk = bias_cols("bk", W["L1_attn_b_qkv"][None, 1024:2048], 1024)
        st["bg_col"], tg = bias_cols("bg", W["L1_attn_b_gate"][None, :], 1024)
        st["b_proj_row"] = nc.alloc_sbuf_tensor("b_proj_row", [1, 1024], F32)
        tr1 = b.dma("sync", b.sq[7], lambda e: e.dma_start(
            out=st["b_proj_row"][:, :], in_=W["L1_attn_b_proj"][None, :]))
        st["lng_col"], tlg = bias_cols("lng", W["L1_attn_ln_g"][None, :], 1024)
        st["lnb_col"], tlb = bias_cols("lnb", W["L1_attn_ln_b"][None, :], 1024)
        st["nbq_col"] = nc.alloc_sbuf_tensor("nbq_col", [128, 8], F32)
        tnq = b.dve(lambda e: e.tensor_scalar_mul(st["nbq_col"][:, :], st["bq_col"][:, :], -1.0), [tq])
        st["tok_l1attn_bias"] = [tq, tk, tg, tlg, tlb, tr0, tr1, tnq]
    if "L1_ffn_b1" in W:
        st["l1b1_col"], tb1 = bias_cols("l1b1", W["L1_ffn_b1"][None, :], 4096)
        st["l1b2_row"] = nc.alloc_sbuf_tensor("l1b2_row", [1, 1024], F32)
        tr2 = b.dma("sync", b.sq[7], lambda e: e.dma_start(
            out=st["l1b2_row"][:, :], in_=W["L1_ffn_b2"][None, :]))
        st["scale_t"] = nc.alloc_sbuf_tensor("scale_t", [1, 1], F32)
        tr3 = b.dma("sync", b.sq[7], lambda e: e.dma_start(
            out=st["scale_t"][:, :], in_=W["L1_ffn_scale"][None, :]))
        st["tok_l1ffn_bias"] = [tb1, tr2, tr3]
    if "L0_ffn_b1" in W:
        st["l0b1_col"], tb = bias_cols("l0b1", W["L0_ffn_b1"].rearrange("e h -> (e h)")[None, :], 8 * 2048)
        st["tok_l0ffn_bias"] = [tb]

    if "L0_ffn_wg" in W:
        wgt = nc.alloc_sbuf_tensor("wgt", [8, 16], F32)
        wg_ss = nc.alloc_sbuf_tensor("wg_ss", [8, 1], F32)
        wg_sq = nc.alloc_sbuf_tensor("wg_sq", [8, 16], F32)
        st["wg_nT"] = nc.alloc_sbuf_tensor("wg_nT", [16, 8], F32)
        st["w_red"] = nc.alloc_sbuf_tensor("w_red", [128, 8 * 16], F32)
        tw = b.dma("sync", b.sq[7], lambda e: e.dma_start(out=wgt[:, :], in_=W["L0_ffn_wg"][:, :]))
        twr = None
        for c in range(8):
            twr = b.dma("sync", b.sq[7], lambda e, _c=c: e.dma_start(
                out=st["w_red"][:, _c * 16:(_c + 1) * 16],
                in_=W["L0_ffn_w_red"][_c * 128:(_c + 1) * 128, :]))
        t_sq = b.act(lambda e: e.activation(wg_sq[:, :], wgt[:, :], AF.Square,
                                            accum_out=wg_ss[:, :]), [tw, st["tok_cst"]])
        t_ln = b.act(lambda e: e.activation(wg_ss[:, :], wg_ss[:, :], AF.Ln), [t_sq])
        t_ex = b.act(lambda e: e.activation(wg_ss[:, :], wg_ss[:, :], AF.Exp, scale=-0.5), [t_ln])
        t_cl = b.dve(lambda e: e.tensor_scalar_min(wg_ss[:, :], wg_ss[:, :], 1e4), [t_ex])
        t_mu = b.dve(lambda e: e.tensor_scalar(wgt[:, :], wgt[:, :], wg_ss[:, :], None, ALU.mult), [t_cl])
        t_tr = b.pe(lambda e: e.transpose(ps_t[0:16, 0:8], wgt[:, :], st["id_f"][0:8, 0:8]),
                    [t_mu, t1])
        tok_wg = b.dve(lambda e: e.tensor_copy(st["wg_nT"][:, :], ps_t[0:16, 0:8]), [t_tr])
        st["wg_nT_b"] = nc.alloc_sbuf_tensor("wg_nT_b", [16, 8], BF16)
        st["w_red_b"] = nc.alloc_sbuf_tensor("w_red_b", [128, 8 * 16], BF16)
        st["tok_wg"] = b.dve(lambda e: e.tensor_copy(st["wg_nT_b"][:, :], st["wg_nT"][:, :]), [tok_wg])
        st["tok_wred"] = b.dve(lambda e: e.tensor_copy(st["w_red_b"][:, :], st["w_red"][:, :]), [twr])

    b.st = st
    return st



class TStep:

    def __init__(self, b, name, dtype, share=None):
        nc = b.nc
        self.b = b
        self.dtype = dtype
        if share is None:
            self.xt = [nc.alloc_sbuf_tensor(f"{name}_xt{i}", [128, 1024], F32) for i in range(3)]
            self.sqd = nc.alloc_sbuf_tensor(f"{name}_sqd", [128, 1024], F32)
        else:
            self.xt = share.xt
            self.sqd = share.sqd
        self.xn = [nc.alloc_sbuf_tensor(f"{name}_xn{i}", [128, 1024], dtype) for i in range(3)]
        self.stats = [nc.alloc_sbuf_tensor(f"{name}_st{i}", [128, 8], F32) for i in range(3)]
        self.xt_done = [None, None, None]
        self.xn_done = [None, None, None]
        self.st_done = [(), (), ()]

    def tile(self, x_dram, tok0, slot, extra_x_deps=(), ln2=False):
        b = self.b
        xt = self.xt[slot]
        xn = self.xn[slot]
        s = self.stats[slot]
        prev = list(self.st_done[slot])
        tl = b.dma("sync", b.sq[slot], lambda e: e.dma_start(
            out=xt[:, :], in_=x_dram[tok0:tok0 + 128, :]),
            deps=list(extra_x_deps) + [self.xt_done[slot]])
        t_sq = b.act(lambda e: e.activation(self.sqd[:, :], xt[:, :], AF.Square,
                                            accum_out=s[:, 0:1]), [tl] + prev)
        t_s = b.dve(lambda e: e.reduce_sum(s[:, 1:2], xt[:, :], axis=AX.X), [tl] + prev)
        t_m = b.dve(lambda e: e.tensor_scalar_mul(s[:, 2:3], s[:, 1:2], 1.0 / 1024.0), [t_s])
        t_e = b.dve(lambda e: e.tensor_scalar_mul(s[:, 3:4], s[:, 0:1], 1.0 / 1024.0), [t_sq])
        t_ms = b.dve(lambda e: e.tensor_mul(s[:, 4:5], s[:, 2:3], s[:, 2:3]), [t_m])
        t_v = b.dve(lambda e: e.tensor_sub(s[:, 5:6], s[:, 3:4], s[:, 4:5]), [t_e, t_ms])
        if ln2:
            t_ln = b.act(lambda e: e.activation(s[:, 6:7], s[:, 5:6], AF.Ln,
                                                bias=1e-10, scale=1.0 + LN_EPS), [t_v])
        else:
            t_ln = b.act(lambda e: e.activation(s[:, 6:7], s[:, 5:6], AF.Ln, bias=LN_EPS), [t_v])
        t_rs = b.act(lambda e: e.activation(s[:, 7:8], s[:, 6:7], AF.Exp, scale=-0.5), [t_ln])
        t_nmr = b.dve(lambda e: e.tensor_scalar(s[:, 4:5], s[:, 2:3], s[:, 7:8], -1.0,
                                                ALU.mult, ALU.mult), [t_rs, t_m])
        t_xn = b.act(lambda e: e.activation(xn[:, :], xt[:, :], AF.Identity,
                                            scale=s[:, 7:8], bias=s[:, 4:5]),
                     [t_nmr, tl, self.xn_done[slot]])
        self.st_done[slot] = (t_xn, t_nmr)
        return xn, t_xn, xt, tl

    def transpose_to(self, xn_ap, t_xn, dst_view, bank_done, extra=(), gain=None, beta=None, pb=0):
        b = self.b
        st = b.st
        if self.dtype == BF16:
            bank = b.psum_b[pb]
            ident = st["id_b"]
            tt = None
            for c in range(8):
                tt = b.pe(lambda e, _c=c: e.transpose(
                    bank[:, _c * 128:(_c + 1) * 128], xn_ap[:, _c * 128:(_c + 1) * 128],
                    ident[:, :]), deps=[t_xn, bank_done, b.pbd[pb]] + list(extra), inc=(c == 7))
            if gain is None:
                tev = b.dve(lambda e: e.tensor_copy(
                    dst_view, bank[:, :].rearrange("p (c t) -> p c t", c=8)), [tt])
            else:
                tev = None
                for c in range(8):
                    tev = b.dve(lambda e, _c=c: e.tensor_scalar(
                        dst_view[:, _c, :], bank[:, _c * 128:(_c + 1) * 128],
                        gain[:, _c:_c + 1], beta[:, _c:_c + 1], ALU.mult, ALU.add),
                        [tt])
            b.pbd[pb] = tev
            return tev
        else:
            ident = st["id_f"]
            banks = (b.psum_f[4], b.psum_f[5])
            tev = None
            tts = []
            for half in range(2):
                bank = banks[half]
                tt = None
                for c in range(4):
                    cc = half * 4 + c
                    tt = b.pe(lambda e, _cc=cc, _c=c, _bank=bank: e.transpose(
                        _bank[:, _c * 128:(_c + 1) * 128], xn_ap[:, _cc * 128:(_cc + 1) * 128],
                        ident[:, :]), deps=[t_xn, bank_done] + list(extra), inc=(c == 3))
                tts.append(tt)
            tev = b.dve(lambda e: e.tensor_copy(
                dst_view[:, 0:4, :], banks[0][:, :].rearrange("p (c t) -> p c t", c=4)), [tts[0]])
            tev2 = b.dve(lambda e: e.tensor_copy(
                dst_view[:, 4:8, :], banks[1][:, :].rearrange("p (c t) -> p c t", c=4)), [tts[1]])
            return tev2



def emit_moe(b, x_in, x_out, W, scratch, ph):
    nc = b.nc
    st = b.st
    cfg = b.cfg
    NG, NT, CAP = cfg["NG"], cfg["NT"], cfg["CAP"]
    CG = cfg["CGRP"]
    NS = 8 * CAP
    xn_sorted = scratch["xn_sorted"]
    osort = scratch["osort"]
    if not hasattr(b, "moe_shared"):
        b.moe_shared = {
            "mi8": nc.alloc_sbuf_tensor("mi8_sh", [128, 16], U32),
            "slots": nc.alloc_sbuf_tensor("slots_sh", [128, NT], I32),
            "bnd": nc.gpsimd.alloc_register("moebnd"),
        }
        b.gp(lambda e: e.reg_mov(b.moe_shared["bnd"], NS - 1))

    b.arena_reset()
    xnf2 = [b.abf(8 * 512) for _ in range(2)]
    xredT = b.abf(512)
    xnbf = [b.abf(1024) for _ in range(8)]
    oh_sb = b.af32(4 * 8)
    sc_half = b.af32(NT)
    esb2 = [b.af32(8) for _ in range(2)]
    mx82 = [b.af32(8) for _ in range(2)]
    tmp82 = [b.af32(8) for _ in range(2)]
    scol2 = [b.af32(8) for _ in range(2)]
    b2row = b.af32(1024)[0:1, :]
    mi8 = b.moe_shared["mi8"]
    slots = b.moe_shared["slots"]
    base_sb = b.af32(8)[0:1, :]
    zero_sb = b.af32(8)[0:1, :]
    t_z = b.gp(lambda e: e.memset(zero_sb[:, :], 0.0))

    ts = b.ts_bf
    resx = [ts.xt[0], ts.xt[1], b.af32(1024)]
    outr = [b.af32(1024) for _ in range(3)]
    pre_deps = [st["tok_wg"], st["tok_wred"], st["tok_iota8"], st["tok_cst"],
                st["tok_ones"][-1], st["tok_l0ffn_bias"][-1]] + st["tok_id"]

    bank_g = {0: [], (1, 0): [], (1, 1): [], (2, 0): [], (2, 1): []}
    xnf_done = [None, None]
    scatter_toks = []
    scat_prev = [None] * 8
    cnt_prev = None
    cnt_last = None
    base_rd = None
    mi_done = [None, None]

    for g in range(NG):
        gsl = g % 2
        xnf_fm = xnf2[gsl]
        xnf_r = xnf_fm.rearrange("p (c t) -> p c t", c=8)
        tile_toks = []
        for j in range(4):
            gt = g * 4 + j
            slot = gt % 3
            xn, t_xn, xt, tl = ts.tile(x_in, g * 512 + j * 128, slot, extra_x_deps=pre_deps)
            jj = (g % 2) * 4 + j
            t_bf = b.dve(lambda e, _jj=jj, _xn=xn: e.tensor_copy(xnbf[_jj][:, :], _xn[:, :]),
                         [t_xn, scat_prev[jj]])
            tev = ts.transpose_to(xn, t_xn, xnf_r[:, :, j * 128:(j + 1) * 128],
                                  None, extra=[xnf_done[gsl]], pb=j % 2)
            ts.xt_done[slot] = t_xn
            ts.xn_done[slot] = tev
            tile_toks.append((gt, slot, t_bf, tev))
        t_fm = tile_toks[-1][3]

        pxr = b.psum_f[0]
        tm = None
        for c in range(8):
            tm = b.pe(lambda e, _c=c, _x=xnf_fm: e.matmul(pxr[0:16, :], st["w_red_b"][:, _c * 16:(_c + 1) * 16],
                                                          _x[:, _c * 512:(_c + 1) * 512],
                                                          start=(_c == 0), stop=(_c == 7)),
                      deps=[t_fm, st["tok_wred"], st["tok_wg"], st["tok_ones128"]] + bank_g[0],
                      inc=(c == 7))
        t_xr = b.dve(lambda e: e.tensor_copy(xredT[0:16, :], pxr[0:16, :]), [tm])
        bank_g[0] = [t_xr]
        xnf_done[gsl] = t_xr

        if g == 0:
            t_base = None
        else:
            t_base = b.dve(lambda e: e.tensor_copy(base_sb[0:1, :], b.psum_f[3][0:1, 0:8]),
                           [cnt_last, base_rd])
        for (gt, slot, t_bf, tev) in tile_toks:
            j = gt % 4
            jp = j % 2
            esb, mx8, tmp8, scol = esb2[jp], mx82[jp], tmp82[jp], scol2[jp]
            plg = b.psum_f[1] if jp == 0 else b.psum_f[4]
            t_lg = b.pe(lambda e, _j=j, _p=plg: e.matmul(_p[:, 0:8], xredT[0:16, _j * 128:(_j + 1) * 128],
                                                 st["wg_nT_b"][:, :], start=True, stop=True),
                        deps=[t_xr] + bank_g[(1, jp)])
            t_nm = b.dve(lambda e, _p=plg, _s=scol: e.reduce_max(_s[:, 0:1], _p[:, 0:8], axis=AX.X, negate=True), [t_lg])
            t_ex = b.act(lambda e, _p=plg, _s=scol, _e=esb: e.activation(_e[:, :], _p[:, 0:8], AF.Exp,
                                                bias=_s[:, 0:1], accum_out=_s[:, 1:2]), [t_nm, t_lg])
            t_rc = b.dve(lambda e, _s=scol: e.reciprocal(_s[:, 2:3], _s[:, 1:2]), [t_ex])
            t_sh = b.dve(lambda e, _gt=gt, _s=scol: e.tensor_scalar_mul(sc_half[:, _gt:_gt + 1], _s[:, 2:3], 0.5), [t_rc])
            t_mx = b.dve(lambda e, _p=plg, _m=mx8: e.max(_m[:, :], _p[:, 0:8]), [t_lg, mi_done[jp]])
            t_mi = b.dve(lambda e, _p=plg, _m=mx8, _jp=jp: e.max_index(
                mi8[:, _jp * 8:(_jp + 1) * 8], _m[:, :], _p[:, 0:8]), [t_mx])
            t_if = b.dve(lambda e, _s=scol, _jp=jp: e.tensor_copy(_s[:, 3:4], mi8[:, _jp * 8:_jp * 8 + 1]), [t_mi])
            mi_done[jp] = t_if
            bank_g[(1, jp)] = [t_if, t_ex]
            t_oh = b.dve(lambda e, _j=j, _s=scol: e.tensor_scalar(oh_sb[:, _j * 8:(_j + 1) * 8], st["iota8"][:, :],
                                                         _s[:, 3:4], None, ALU.is_equal),
                         [t_if, cnt_prev])
            prk = b.psum_f[2] if jp == 0 else b.psum_f[5]
            tr = None
            for i in range(j + 1):
                lhs = st["ones128_f"] if i < j else st["u01"]
                tr = b.pe(lambda e, _i=i, _j=j, _lhs=lhs, _p=prk, _st=(i == j and g == 0): e.matmul(
                    _p[:, 0:8], _lhs[:, :], oh_sb[:, _i * 8:(_i + 1) * 8],
                    start=(_i == 0), stop=_st),
                    deps=[t_oh] + bank_g[(2, jp)], inc=(i == j and g == 0))
            if g > 0:
                tr = b.pe(lambda e, _p=prk: e.matmul(
                    _p[:, 0:8], st["ones_row_f"][:, :], base_sb[0:1, :],
                    start=False, stop=True), deps=[t_base], inc=True)
            t_rm = b.dve(lambda e, _j=j, _t=tmp8, _p=prk: e.tensor_mul(_t[:, :], _p[:, 0:8], oh_sb[:, _j * 8:(_j + 1) * 8]), [tr])
            t_ra = b.dve(lambda e, _s=scol, _t=tmp8: e.reduce_sum(_s[:, 4:5], _t[:, :], axis=AX.X), [t_rm])
            bank_g[(2, jp)] = [t_rm]
            t_s1 = b.dve(lambda e, _s=scol: e.scalar_tensor_tensor(_s[:, 5:6], _s[:, 3:4], float(CAP),
                                                          _s[:, 4:5], ALU.mult, ALU.add), [t_ra, t_if])
            t_ov = b.dve(lambda e, _s=scol: e.tensor_single_scalar(_s[:, 6:7], _s[:, 4:5], float(CAP), ALU.is_ge), [t_ra])
            t_s2 = b.dve(lambda e, _s=scol: e.scalar_tensor_tensor(_s[:, 7:8], _s[:, 6:7], 1e9,
                                                          _s[:, 5:6], ALU.mult, ALU.add), [t_ov, t_s1])
            t_si = b.dve(lambda e, _gt=gt, _s=scol: e.tensor_copy(slots[:, _gt:_gt + 1], _s[:, 7:8]), [t_s2])
            t_sc = b.op("gpsimd", lambda e, _gt=gt, _jj=(g % 2) * 4 + (gt % 4): e.indirect_dma_start(
                out=xn_sorted[:, :], out_offset=IndirectOffsetOnAxis(slots[:, _gt:_gt + 1], 0),
                in_=xnbf[_jj][:, :], in_offset=None,
                bounds_check=b.moe_shared["bnd"], oob_is_err=False),
                deps=[t_si, t_bf], inc=b.gq[0], inc_by=16)
            scatter_toks.append(t_sc)
        b.drain_casts(1)
        for j in range(4):
            scat_prev[(g % 2) * 4 + j] = b.gq[0].tok()
        if g < NG - 1:
            for j in range(4):
                cnt_last = b.pe(lambda e, _j=j, _g=g, _st=(g == 0 and j == 3): e.matmul(
                    b.psum_f[3][0:1, 0:8], st["ones_col_f"][:, :], oh_sb[:, _j * 8:(_j + 1) * 8],
                    start=(_j == 0), stop=_st),
                    deps=[t_base] if (j == 0 and g > 0) else [], inc=(g == 0 and j == 3))
            if g > 0:
                cnt_last = b.pe(lambda e: e.matmul(
                    b.psum_f[3][0:1, 0:8], st["ones_row_f"][0:1, 0:1], base_sb[0:1, :],
                    start=False, stop=True), deps=[t_base], inc=True)
                base_rd = cnt_last
        cnt_prev = cnt_last

    b.drain_casts(len(b.cast_sched))
    b.fence()

    b.ar_off["bf"] = 0
    w1sb = b.abf(8 * 2048)
    w2sb = b.abf(16 * 1024)
    hs_fm2 = [b.abf(16 * 512) for _ in range(2)]
    xs_fm2 = [b.abf(8 * 512) for _ in range(2)]
    xs_tm = [b.abf(1024) for _ in range(2)]
    og_tm = [b.abf(1024) for _ in range(3)]
    bank_tr = [None, None]
    NSG = CAP // 512
    w1_free = None
    w2_free = None
    b2_free = None
    bank_h = {0: None, 1: None, 2: None}
    bank_o = {3: None, 4: None, 5: None}
    xsf_done = [None, None]
    og_store_prev = [None, None, None]
    ostore_toks = []
    wload = {}
    w1b = scratch["w1e_b"]
    w2b = scratch["w2e_b"]
    def load_w1(e):
        tw1 = None
        for dc in range(0, 8, 2):
            tw1 = b.dma("sync", b.sq[5], lambda e_, _dc=dc, _e=e: e_.dma_start(
                out=w1sb[:, _dc * 2048:(_dc + 2) * 2048].rearrange("p (c k) -> p c k", c=2),
                in_=w1b.rearrange("e (c p) h -> e p c h", p=128)[_e, :, _dc:_dc + 2, :]),
                deps=[w1_free, b.wcast.get(f"w1_{e}")])
        return tw1
    def load_w2(e):
        tw2 = None
        for hc in range(0, 16, 4):
            tw2 = b.dma("sync", b.sq[6], lambda e_, _hc=hc, _e=e: e_.dma_start(
                out=w2sb[:, _hc * 1024:(_hc + 4) * 1024].rearrange("p (c k) -> p c k", c=4),
                in_=w2b.rearrange("e (c p) o -> e p c o", p=128)[_e, :, _hc:_hc + 4, :]),
                deps=[w2_free, b.wcast.get(f"w2_{e}")])
        tb2 = b.dma("sync", b.sq[4], lambda e_, _e=e: e_.dma_start(
            out=b2row[:, :], in_=W["L0_ffn_b2"][_e:_e + 1, :]), deps=[b2_free])
        return tw2, tb2
    wload[0] = (load_w1(0), load_w2(0))
    pairs = []
    sg0 = 0
    while sg0 < NSG:
        pairs.append(tuple(range(sg0, min(sg0 + 2, NSG))))
        sg0 += 2
    for e in range(8):
        tw1, (tw2, tb2) = wload[e]
        for ip, pair in enumerate(pairs):
            t_xs = {}
            for a, sg in enumerate(pair):
                row0 = e * CAP + sg * 512
                xs_fm = xs_fm2[a]
                xs_r = xs_fm.rearrange("p (c t) -> p c t", c=8)
                for t in range(4):
                    slot = t % 2
                    tl = b.dma("sync", b.sq[2 + slot], lambda e_, _r=row0 + t * 128, _s=slot: e_.dma_start(
                        out=xs_tm[_s][:, :], in_=xn_sorted[_r:_r + 128, :]), deps=[bank_tr[t % 2]])
                    tt = None
                    for c in range(8):
                        tt = b.pe(lambda e_, _c=c, _s=slot, _b=t % 2: e_.transpose(
                            b.psum_b[_b][:, _c * 128:(_c + 1) * 128], xs_tm[_s][:, _c * 128:(_c + 1) * 128],
                            st["id_b"][:, :]), deps=[tl, b.pbd[t % 2], xsf_done[a]], inc=(c == 7))
                    tev = b.dve(lambda e_, _t=t, _b=t % 2, _r=xs_r: e_.tensor_copy(
                        _r[:, :, _t * 128:(_t + 1) * 128],
                        b.psum_b[_b][:, :].rearrange("p (c t) -> p c t", c=8)), [tt])
                    b.pbd[t % 2] = tev
                    bank_tr[t % 2] = tev
                t_xs[a] = tev
            t_h = {}
            tm_h = {}
            for hc in range(16):
                for a, sg in enumerate(pair):
                    pb = (hc * len(pair) + a) % 3
                    tm = None
                    for dc in range(8):
                        tm = b.pe(lambda e_, _dc=dc, _hc=hc, _pb=pb, _a=a: e_.matmul(
                            b.psum_f[_pb][:, :], w1sb[:, _dc * 2048 + _hc * 128:_dc * 2048 + _hc * 128 + 128],
                            xs_fm2[_a][:, _dc * 512:(_dc + 1) * 512],
                            start=(_dc == 0), stop=(_dc == 7)),
                            deps=[t_xs[a], tw1, bank_h[pb]], inc=(dc == 7))
                    tg = b.act(lambda e_, _hc=hc, _e=e, _pb=pb, _a=a: e_.activation(
                        hs_fm2[_a][:, _hc * 512:(_hc + 1) * 512], b.psum_f[_pb][:, :], AF.Gelu,
                        bias=st["l0b1_col"][:, _e * 16 + _hc:_e * 16 + _hc + 1]), [tm])
                    bank_h[pb] = tg
                    t_h[a] = tg
                    tm_h[a] = tm
            for a, sg in enumerate(pair):
                xsf_done[a] = tm_h[a]
            if ip == len(pairs) - 1:
                w1_free = tm_h[len(pair) - 1]
                if e < 7:
                    next_w1 = load_w1(e + 1)
            for a, sg in enumerate(pair):
                row0 = e * CAP + sg * 512
                for t in range(4):
                    gt2 = a * 4 + t
                    ob0, ob1 = 3 + (2 * gt2) % 3, 3 + (2 * gt2 + 1) % 3
                    tm = None
                    for hc in range(16):
                        for og, ob in ((0, ob0), (1, ob1)):
                            tm = b.pe(lambda e_, _hc=hc, _t=t, _og=og, _ob=ob, _a=a: e_.matmul(
                                b.psum_f[_ob][:, :],
                                hs_fm2[_a][:, _hc * 512 + _t * 128:_hc * 512 + _t * 128 + 128],
                                w2sb[:, _hc * 1024 + _og * 512:_hc * 1024 + _og * 512 + 512],
                                start=(_hc == 0), stop=False),
                                deps=[t_h[a], tw2, bank_o[ob0], bank_o[ob1]] if (hc == 0 and og == 0) else (),
                                inc=False)
                    oevs = []
                    for og, ob in ((0, ob0), (1, ob1)):
                        tm = b.pe(lambda e_, _og=og, _ob=ob: e_.matmul(
                            b.psum_f[_ob][:, :], st["ones_row_f"][:, :],
                            b2row[:, _og * 512:_og * 512 + 512],
                            start=False, stop=True), deps=[tb2, st["tok_ones"][0]], inc=True)
                        oev = b.act(lambda e_, _og=og, _t=t, _s=t % 2, _ob=ob: e_.activation(
                            og_tm[_s][:, _og * 512:(_og + 1) * 512], b.psum_f[_ob][:, :], AF.Copy),
                            [tm, og_store_prev[t % 2]])
                        oevs.append(oev)
                        bank_o[ob] = oev
                    tst = b.dma("scalar", b.aq[t % 2], lambda e_, _r=row0 + t * 128, _s=t % 2: e_.dma_start(
                        out=osort[_r:_r + 128, :], in_=og_tm[_s][:, :]), deps=oevs)
                    ostore_toks.append(tst)
                    og_store_prev[t % 2] = tst
                    if ip == len(pairs) - 1 and a == len(pair) - 1 and t == 3:
                        w2_free = tm
                        b2_free = tm
                        if e < 7:
                            wload[e + 1] = (next_w1, load_w2(e + 1))

    b.fence()

    res_toks = []
    R = 4
    og_tm = og_tm + [xs_tm[0]]
    resx = resx + [b.af32(1024)]
    outr = outr + [b.af32(1024)]
    gdone = [None] * R
    stprev = [None] * R
    for gt in range(NT):
        slot = gt % R
        tg = b.dma("gpsimd", b.gq[3 + slot], lambda e_, _gt=gt, _s=slot: e_.indirect_dma_start(
            out=og_tm[_s][:, :], out_offset=None,
            in_=osort[:, :], in_offset=IndirectOffsetOnAxis(slots[:, _gt:_gt + 1], 0),
            bounds_check=b.moe_shared["bnd"], oob_is_err=False), deps=[gdone[slot]])
        tx = b.dma("sync", b.sq[slot if slot < 3 else 7], lambda e_, _gt=gt, _s=slot: e_.dma_start(
            out=resx[_s][:, :], in_=x_in[_gt * 128:(_gt + 1) * 128, :]), deps=[gdone[slot]])
        to = b.dve(lambda e_, _gt=gt, _s=slot: e_.scalar_tensor_tensor(
            outr[_s][:, :], og_tm[_s][:, :], sc_half[:, _gt:_gt + 1], resx[_s][:, :],
            ALU.mult, ALU.add), [tg, tx, stprev[slot]])
        gdone[slot] = to
        tst = b.dma("scalar", b.aq[2 + slot % 2], lambda e_, _gt=gt, _s=slot: e_.dma_start(
            out=x_out[_gt * 128:(_gt + 1) * 128, :], in_=outr[_s][:, :]), deps=[to])
        stprev[slot] = tst
        res_toks.append(tst)
    return res_toks



def build_program(cfg, phases=("moe1",)):
    nc = bass.Bass("TRN2", target_bir_lowering=False)
    TOK = cfg["TOK"]
    b = B(nc, cfg)
    b.make_arenas(cfg.get("BF_COLS", 62464), cfg.get("F_COLS", 9216))

    x_ap = nc.dram_tensor("x", [TOK, D], F32, kind="ExternalInput").ap()
    y_ap = nc.dram_tensor("y", [TOK, D], F32, kind="ExternalOutput").ap()

    W = {}
    wspecs = {
        "L0_attn_W_DKV": [D, 256], "L0_attn_W_UK": [256, D], "L0_attn_W_UV": [256, D],
        "L0_attn_W_UQ": [D, D], "L0_attn_W_O": [D, D],
        "L0_ffn_w_red": [D, 16], "L0_ffn_wg": [8, 16],
        "L0_ffn_W1": [8, D, 2048], "L0_ffn_b1": [8, 2048],
        "L0_ffn_W2": [8, 2048, D], "L0_ffn_b2": [8, D],
        "L1_attn_w_qkv": [D, 3 * D], "L1_attn_b_qkv": [3 * D],
        "L1_attn_w_gate": [D, D], "L1_attn_b_gate": [D],
        "L1_attn_w_proj": [D, D], "L1_attn_b_proj": [D],
        "L1_attn_ln_g": [D], "L1_attn_ln_b": [D],
        "L1_ffn_W1": [L1_H, D], "L1_ffn_b1": [L1_H],
        "L1_ffn_W2": [D, L1_H], "L1_ffn_b2": [D],
        "L1_ffn_scale": [1],
    }
    need = set()
    for p in phases:
        if p.startswith("moe"):
            need.update(k for k in wspecs if k.startswith("L0_ffn"))
        elif p == "mla":
            need.update(k for k in wspecs if k.startswith("L0_attn"))
        elif p.startswith("lff"):
            need.update(k for k in wspecs if k.startswith("L1_ffn"))
        elif p == "lat":
            need.update(k for k in wspecs if k.startswith("L1_attn"))
    for k in sorted(need):
        W[k] = nc.dram_tensor(k, wspecs[k], F32, kind="ExternalInput").ap()

    scratch = {}
    NSROW = 8 * cfg["CAP"]
    if any(p.startswith("moe") for p in phases):
        scratch["xn_sorted"] = nc.dram_tensor("xn_sorted", [NSROW, D], BF16, kind="Internal").ap()
        scratch["osort"] = nc.dram_tensor("osort", [NSROW, D], BF16, kind="Internal").ap()
    if any(p.startswith("lff") for p in phases):
        scratch["w1t"] = nc.dram_tensor("w1t_scr", [D, L1_H], BF16, kind="Internal").ap()
        scratch["w2t"] = nc.dram_tensor("w2t_scr", [L1_H, D], BF16, kind="Internal").ap()
    if "mla" in phases:
        scratch["qscr"] = nc.dram_tensor("qscr", [D, TOK], BF16, kind="Internal").ap()
        scratch["wuq_b"] = nc.dram_tensor("wuq_b", [D, D], BF16, kind="Internal").ap()
        scratch["wo_b"] = nc.dram_tensor("wo_b", [D, D], BF16, kind="Internal").ap()
    if "lat" in phases:
        scratch["wqkv_b"] = nc.dram_tensor("wqkv_b", [D, 3 * D], BF16, kind="Internal").ap()
        scratch["wg_b"] = nc.dram_tensor("wg_b", [D, D], BF16, kind="Internal").ap()
        scratch["wp_b"] = nc.dram_tensor("wp_b", [D, D], BF16, kind="Internal").ap()
    if any(p.startswith("moe") for p in phases):
        scratch["w1e_b"] = nc.dram_tensor("w1e_b", [8, D, 2048], BF16, kind="Internal").ap()
        scratch["w2e_b"] = nc.dram_tensor("w2e_b", [8, 2048, D], BF16, kind="Internal").ap()
    xa = nc.dram_tensor("xbufA", [TOK, D], F32, kind="Internal").ap()
    xb_ = nc.dram_tensor("xbufB", [TOK, D], F32, kind="Internal").ap()

    preamble(b, W)
    b.ts_bf = TStep(b, "tsb", BF16)
    if any(p.startswith("lff") for p in phases):
        prep_l1w(b, W, scratch)
    b.fence()

    wc = {}
    b.cast_sched = []
    def cast_group(gname, names, items, chunk=512):
        sem = Sem(nc, f"cast_{gname}")
        n_dma = 0
        closures = []
        for (src_ap, dst_ap, rows) in items:
            for r0 in range(0, rows, chunk):
                r1 = min(rows, r0 + chunk)
                closures.append(lambda _a=src_ap, _d=dst_ap, _r0=r0, _r1=r1, _s=sem: b.op(
                    "gpsimd", lambda e: e.dma_start(out=_d[_r0:_r1, :], in_=_a[_r0:_r1, :]),
                    deps=(), inc=_s, inc_by=16))
                n_dma += 1
        tok = (sem, 16 * n_dma)
        for nm in names:
            wc[nm] = tok
        b.cast_sched.append(closures)
    if any(p.startswith("moe") for p in phases):
        for e_i in range(8):
            cast_group(f"moe{e_i}", [f"w1_{e_i}", f"w2_{e_i}"], [
                (W["L0_ffn_W1"][e_i], scratch["w1e_b"][e_i], D),
                (W["L0_ffn_W2"][e_i], scratch["w2e_b"][e_i], 2048)])
    if "mla" in phases:
        cast_group("mla", ["wuq", "wo"], [
            (W["L0_attn_W_UQ"], scratch["wuq_b"], D),
            (W["L0_attn_W_O"], scratch["wo_b"], D)])
    if "lat" in phases:
        cast_group("lat", ["wqkv", "wg", "wp"], [
            (W["L1_attn_w_qkv"], scratch["wqkv_b"], D),
            (W["L1_attn_w_gate"], scratch["wg_b"], D),
            (W["L1_attn_w_proj"], scratch["wp_b"], D)])
    b.wcast = wc

    def drain_casts(k=1):
        for _ in range(k):
            if b.cast_sched:
                for cl in b.cast_sched.pop(0):
                    cl()
    b.drain_casts = drain_casts
    if not any(p.startswith("moe") for p in phases):
        b.drain_casts(len(b.cast_sched))

    chain = [x_ap]
    bufs = [xa, xb_]
    for i in range(len(phases) - 1):
        chain.append(bufs[i % 2])
    chain.append(y_ap)

    last_toks = None
    for i, p in enumerate(phases):
        x_in, x_out = chain[i], chain[i + 1]
        if p.startswith("moe"):
            last_toks = emit_moe(b, x_in, x_out, W, scratch, ph=p)
        elif p == "mla":
            last_toks = emit_mla(b, x_in, x_out, W, scratch)
        elif p.startswith("lff"):
            last_toks = emit_lff(b, x_in, x_out, W, scratch, ph=p)
        elif p == "lat":
            last_toks = emit_lat(b, x_in, x_out, W, scratch)
        else:
            raise ValueError(p)
        b.fence()

    def final_wait(e, toks=[t for t in (last_toks or [])]):
        for sem, v in toks:
            e.wait_ge(sem.h, v)
    b.streams["sync"].append(final_wait)

    b.assemble()
    nc.finalize()
    return nc


def make_cfg(TOK, cgrp=CGRP):
    NG = TOK // 512
    return dict(TOK=TOK, NG=NG, NT=TOK // 128, CAP=cgrp * NG, CGRP=cgrp,
                NSEQ=max(1, TOK // 2048), SEQ=min(TOK, 2048))


PHASES_ALL = ("moe1", "mla", "moe2", "lff1", "lat", "lff2")


def kernel(**inputs):
    x = np.ascontiguousarray(np.asarray(inputs["L0_attn_x"]))
    Bt, T, Dm = x.shape
    n_cores = 8
    per = Bt // n_cores
    cfg = make_cfg(per * T, cgrp=96)
    nc = build_program(cfg, phases=PHASES_ALL)
    wnames = [k for k in inputs if k != "L0_attn_x"]
    base = {}
    for k in wnames:
        a = np.ascontiguousarray(np.asarray(inputs[k], dtype=np.float32))
        if a.ndim == 0:
            a = a.reshape(1)
        base[k] = a
    in_maps = []
    for c in range(n_cores):
        m = dict(base)
        m["x"] = np.ascontiguousarray(x[c * per:(c + 1) * per].reshape(per * T, Dm))
        in_maps.append(m)
    res = run_bass_kernel_spmd(nc, in_maps, core_ids=list(range(n_cores)))
    out = np.concatenate([r["y"].reshape(per, T, Dm) for r in res.results], axis=0)
    return out.astype(np.float32)



def prep_l1w(b, W, scratch):
    nc = b.nc
    st = b.st
    b.arena_reset()
    wrow = [b.abf(4096) for i in range(2)]
    wcol = [b.abf(4096) for i in range(2)]
    rdone = [None, None]
    cdone = [None, None]
    bank_done = None
    st_last = [None, None]

    def trans_chunks(src_ap, dst_dram, n_in_chunks, in_cols, dst_rows):
        nonlocal rdone, cdone, bank_done
        dstr = dst_dram.rearrange("(c p) o -> p c o", p=128)
        for rc in range(n_in_chunks):
            sl = rc % 2
            tl = b.dma("gpsimd", b.gq[5], lambda e, _rc=rc, _sl=sl, _s=src_ap, _w=in_cols: e.dma_start(
                out=wrow[_sl][:, 0:_w], in_=_s[_rc * 128:(_rc + 1) * 128, :]),
                deps=[rdone[sl]])
            nblk = in_cols // 1024
            csl = rc % 2
            evs = []
            for blk in range(nblk):
                tt = None
                for c in range(8):
                    cc = blk * 8 + c
                    tt = b.pe(lambda e, _cc=cc, _c=c, _sl=sl: e.transpose(
                        b.psum_b[1][:, _c * 128:(_c + 1) * 128],
                        wrow[_sl][:, _cc * 128:(_cc + 1) * 128], st["id_b"][:, :]),
                        deps=[tl, bank_done] + st["tok_id"], inc=(c == 7))
                tev = b.dve(lambda e, _csl=csl, _blk=blk: e.tensor_copy(
                    wcol[_csl][:, _blk * 1024:(_blk + 1) * 1024].rearrange("p (c t) -> p c t", c=8),
                    b.psum_b[1][:, :].rearrange("p (c t) -> p c t", c=8)),
                    [tt, cdone[csl]])
                bank_done = tev
                evs.append(tev)
            tst = b.dma("scalar", b.aq[4 + rc % 2], lambda e, _rc=rc, _csl=csl, _nb=nblk: e.dma_start(
                out=dstr[:, 0:8 * _nb, _rc * 128:(_rc + 1) * 128],
                in_=wcol[_csl][:, 0:1024 * _nb].rearrange("p (c t) -> p c t", c=8 * _nb)),
                deps=evs)
            st_last[rc % 2] = tst
            cdone[csl] = tst
            rdone[sl] = evs[-1]
    trans_chunks(W["L1_ffn_W1"], scratch["w1t"], 32, 1024, None)
    trans_chunks(W["L1_ffn_W2"], scratch["w2t"], 8, 4096, None)
    b.l1w_tok = [t for t in st_last if t is not None]



def emit_lff(b, x_in, x_out, W, scratch, ph):
    nc = b.nc
    st = b.st
    cfg = b.cfg
    NG = cfg["NG"]
    SEQ = cfg["SEQ"]
    GPS = SEQ // 512
    w1t, w2t = scratch["w1t"], scratch["w2t"]

    b.arena_reset()
    w1win = [b.abf(8 * 1024) for _ in range(2)]
    w2win = [b.abf(8 * 1024) for _ in range(2)]
    hs = [b.abf(8 * 512) for _ in range(2)]
    xnf = [b.abf(8 * 512) for _ in range(2)]
    evo = b.abf(128)
    od3p = b.abf(8 * 4)
    od4p = b.abf(8 * 4)
    m2a = b.abf(4 * 1024)
    m2c = m2a[0:4, :]
    w4a = b.abf(4096)
    w4blk = w4a[0:4, :]
    erow = w4a[32:33, 0:1024]
    erow2 = w4a[64:65, 0:1024]
    acc = b.af32(4 * 1024)
    t1a = b.abf(512)
    t1all = t1a[0:4, :]
    t2a = b.abf(4 * 512)
    t2sb = t2a[0:4, :]
    resx = [b.af32(1024) for _ in range(2)]

    ts = b.ts_bf
    pre = [st["tok_cst"], st["tok_l1ffn_bias"][0], st["tok_l1ffn_bias"][1],
           st["tok_l1ffn_bias"][2], st["tok_ones"][0]] + st["tok_id"]

    xnf_done = [[], []]
    seq_tok = {}
    bank = {"tr": None, 0: None, 1: None, 2: None, 3: None, 4: None, 5: None}
    w1done = [None, None]
    w2done = [None, None]
    hs_done = [None, None]
    acc_done = [None] * 4
    rx_done = [None, None]
    res_toks = []

    for g in range(NG):
        gsl = g % 2
        xnf_g = xnf[gsl]
        xnf_r = xnf_g.rearrange("p (c t) -> p c t", c=8)
        for j in range(4):
            slot = j % 3
            xn, t_xn, xt, tl = ts.tile(x_in, g * 512 + j * 128, slot, extra_x_deps=pre)
            tev = ts.transpose_to(xn, t_xn, xnf_r[:, :, j * 128:(j + 1) * 128],
                                  None, extra=xnf_done[gsl], pb=j % 2)
            ts.xt_done[slot] = t_xn
            ts.xn_done[slot] = tev
            t_fm = tev

        if g % GPS == 0:
            prev_done = res_toks[-1] if res_toks else None
            t_evo = b.dve(lambda e, _x=xnf_r: e.tensor_copy(
                evo[:, :].rearrange("p (c k) -> p c k", c=8), _x[:, :, 1:17]), [t_fm, prev_done])
            t_o3 = b.dve(lambda e: e.tensor_copy(
                od3p[:, :].rearrange("p (c k) -> p c k", c=8),
                evo[:, :].rearrange("p (c k) -> p c k", c=8)[:, :, 1:9:2]), [t_evo])
            t_o4 = b.dve(lambda e: e.tensor_copy(
                od4p[:, :].rearrange("p (c k) -> p c k", c=8),
                evo[:, :].rearrange("p (c k) -> p c k", c=8)[:, :, 9:16:2]), [t_evo])
            t_mz = b.gp(lambda e: e.memset(w4a[:, :], 0.0), [prev_done])
            t_mz2 = b.gp(lambda e: e.memset(m2a[:, :], 0.0), [prev_done])
            t_ev_prev = None
            w4tok = None
            m2tok = None
            for e_i in range(4):
                for colk, dst_is_w4 in [(2 * e_i, True), (8 + 2 * e_i, False)]:
                    tt = None
                    for c in range(8):
                        tt = b.pe(lambda e, _c=c, _k=colk: e.transpose(
                            b.psum_b[1][0:1, _c * 128:(_c + 1) * 128],
                            evo[:, :].rearrange("p (c k) -> p c k", c=8)[:, _c, _k:_k + 1],
                            st["id_b"][:, :]), deps=[t_evo, t_ev_prev, b.pbd[1]], inc=(c == 7))
                    t_er = b.dve(lambda e: e.tensor_single_scalar(
                        erow[:, :], b.psum_b[1][0:1, 0:1024], st["scale_t"][0:1, 0:1], ALU.mult),
                        [tt, t_ev_prev, t_mz])
                    b.pbd[1] = t_er
                    t_ev_prev = t_er
                    if dst_is_w4:
                        w4tok = b.dma("gpsimd", b.gq[5], lambda e, _e=e_i: e.dma_start(
                            out=w4blk[_e:_e + 1, _e * 1024:(_e + 1) * 1024], in_=erow[:, :]),
                            deps=[t_er, t_mz])
                        t_ev_prev = w4tok
                    else:
                        t_es = b.dve(lambda e: e.tensor_copy(
                            erow2[:, :],
                            erow[:, :].rearrange("p (k c) -> p c k", c=4)), [t_er, t_mz2])
                        for cc in range(4):
                            m2tok = b.dma("gpsimd", b.gq[5], lambda e, _e=e_i, _cc=cc: e.dma_start(
                                out=m2c[_e:_e + 1, _cc * 1024 + _e * 256:_cc * 1024 + (_e + 1) * 256],
                                in_=erow2[:, _cc * 256:(_cc + 1) * 256]), deps=[t_es, t_mz2])
                        t_ev_prev = m2tok
            seq_tok = {"w4": w4tok, "m2": m2tok, "o3": t_o3, "o4": t_o4}

        tm = None
        for dc in range(8):
            tm = b.pe(lambda e, _dc=dc, _x=xnf_g: e.matmul(
                b.psum_f[2][0:4, :], od3p[:, _dc * 4:(_dc + 1) * 4], _x[:, _dc * 512:(_dc + 1) * 512],
                start=(_dc == 0), stop=(_dc == 7)),
                deps=[t_fm, seq_tok["o3"], bank[2]], inc=(dc == 7))
        t_t1 = b.dve(lambda e: e.tensor_copy(t1all[:, :], b.psum_f[2][0:4, :]), [tm])
        bank[2] = t_t1

        for blk in range(4):
            wsl = blk % 2
            tw1 = b.dma("sync", b.sq[4 + wsl], lambda e, _blk=blk, _w=wsl: e.dma_start(
                out=w1win[_w][:, :].rearrange("p (c h) -> p c h", c=8),
                in_=w1t.rearrange("(c p) h -> p c h", p=128)[:, :, _blk * 1024:(_blk + 1) * 1024]),
                deps=[w1done[wsl]] + b.l1w_tok)
            t_h = None
            hsb = hs[wsl]
            for hl in range(8):
                hc = blk * 8 + hl
                pb = hl % 2
                for dc in range(8):
                    b.pe(lambda e, _dc=dc, _hl=hl, _pb=pb, _x=xnf_g, _w=wsl: e.matmul(
                        b.psum_f[_pb][:, :],
                        w1win[_w][:, _dc * 1024 + _hl * 128:_dc * 1024 + _hl * 128 + 128],
                        _x[:, _dc * 512:(_dc + 1) * 512],
                        start=(_dc == 0), stop=False),
                        deps=[t_fm, tw1, bank[pb]] if dc == 0 else (), inc=False)
                tm = b.pe(lambda e, _hc=hc, _pb=pb: e.matmul(
                    b.psum_f[_pb][:, :], w4blk[:, _hc * 128:(_hc + 1) * 128], t1all[:, :],
                    start=False, stop=True), deps=[seq_tok["w4"], t_t1], inc=True)
                if hl == 7:
                    w1done[wsl] = tm
                t_h = b.act(lambda e, _hc=hc, _hl=hl, _pb=pb, _hs=hsb: e.activation(
                    _hs[:, _hl * 512:(_hl + 1) * 512], b.psum_f[_pb][:, :], AF.Gelu,
                    bias=st["l1b1_col"][:, _hc:_hc + 1]), [tm, hs_done[wsl]])
                bank[pb] = t_h

            tm = None
            for jc in range(8):
                tm = b.pe(lambda e, _jc=jc, _hs=hsb: e.matmul(
                    b.psum_f[2][0:4, :], od4p[:, _jc * 4:(_jc + 1) * 4],
                    _hs[:, _jc * 512:(_jc + 1) * 512],
                    start=(_jc == 0), stop=(_jc == 7)),
                    deps=[t_h, seq_tok["o4"], bank[2]], inc=(jc == 7))
            tt2 = b.dve(lambda e, _blk=blk: e.tensor_copy(
                t2sb[0:4, _blk * 512:(_blk + 1) * 512], b.psum_f[2][0:4, :]), [tm])
            bank[2] = tt2
            if blk == 3:
                t_t2 = tt2

            tw2 = b.dma("sync", b.sq[6], lambda e, _blk=blk, _w=wsl: e.dma_start(
                out=w2win[_w][:, :].rearrange("p (c o) -> p c o", c=8),
                in_=w2t.rearrange("(c p) o -> p c o", p=128)[:, _blk * 8:(_blk + 1) * 8, :]),
                deps=[w2done[wsl]] + b.l1w_tok)
            w2rd = None
            hs_rd = None
            for t in range(4):
                tsl = t % 2
                if blk == 3:
                    trx = b.dma("sync", b.sq[tsl], lambda e, _g=g, _t=t, _s=tsl: e.dma_start(
                        out=resx[_s][:, :], in_=x_in[_g * 512 + _t * 128:_g * 512 + (_t + 1) * 128, :]),
                        deps=[rx_done[tsl]])
                tms = {}
                ob0 = 3 + (2 * t) % 3
                ob1 = 3 + (2 * t + 1) % 3
                for hl in range(8):
                    for og in range(2):
                        ob = ob0 if og == 0 else ob1
                        tms[og] = b.pe(lambda e, _hl=hl, _t=t, _og=og, _w=wsl, _ob=ob, _hs=hsb,
                                       _st=(blk < 3 and hl == 7): e.matmul(
                            b.psum_f[_ob][:, :],
                            _hs[:, _hl * 512 + _t * 128:_hl * 512 + _t * 128 + 128],
                            w2win[_w][:, _hl * 1024 + _og * 512:_hl * 1024 + _og * 512 + 512],
                            start=(_hl == 0), stop=_st),
                            deps=[t_h, tw2, bank[ob0], bank[ob1]] if (hl == 0 and og == 0) else (),
                            inc=(blk < 3 and hl == 7))
                for og in range(2):
                    ob = ob0 if og == 0 else ob1
                    tm = tms[og]
                    w2rd = tm
                    hs_rd = tm
                    if blk == 3:
                        for cq in range(4):
                            b.pe(lambda e, _cq=cq, _t=t, _og=og, _ob=ob: e.matmul(
                                b.psum_f[_ob][:, :], t2sb[0:4, _cq * 512 + _t * 128:_cq * 512 + _t * 128 + 128],
                                m2c[:, _cq * 1024 + _og * 512:_cq * 1024 + _og * 512 + 512],
                                start=False, stop=False), deps=[t_t2, seq_tok["m2"]] if cq == 0 else (),
                                inc=False)
                        tm = b.pe(lambda e, _og=og, _ob=ob: e.matmul(
                            b.psum_f[_ob][:, :], st["ones_row_f"][:, :],
                            st["l1b2_row"][:, _og * 512:(_og + 1) * 512],
                            start=False, stop=True), deps=[], inc=True)
                        ta = b.dve(lambda e, _t=t, _og=og, _ob=ob: e.tensor_add(
                            acc[:, _t * 1024 + _og * 512:_t * 1024 + (_og + 1) * 512],
                            acc[:, _t * 1024 + _og * 512:_t * 1024 + (_og + 1) * 512],
                            b.psum_f[_ob][:, :]), [tm])
                        to = b.dve(lambda e, _t=t, _og=og, _s=tsl: e.scalar_tensor_tensor(
                            resx[_s][:, _og * 512:(_og + 1) * 512],
                            acc[:, _t * 1024 + _og * 512:_t * 1024 + (_og + 1) * 512], 0.5,
                            resx[_s][:, _og * 512:(_og + 1) * 512],
                            ALU.mult, ALU.add), [ta, trx])
                        bank[ob] = to
                        acc_done[t] = to
                        if og == 1:
                            tst = b.dma("scalar", b.aq[tsl], lambda e, _g=g, _t=t, _s=tsl: e.dma_start(
                                out=x_out[_g * 512 + _t * 128:_g * 512 + (_t + 1) * 128, :],
                                in_=resx[_s][:, :]), deps=[to])
                            rx_done[tsl] = tst
                            res_toks.append(tst)
                    else:
                        if blk == 0:
                            ta = b.dve(lambda e, _t=t, _og=og, _ob=ob: e.tensor_copy(
                                acc[:, _t * 1024 + _og * 512:_t * 1024 + (_og + 1) * 512],
                                b.psum_f[_ob][:, :]), [tm, acc_done[t]])
                        else:
                            ta = b.dve(lambda e, _t=t, _og=og, _ob=ob: e.tensor_add(
                                acc[:, _t * 1024 + _og * 512:_t * 1024 + (_og + 1) * 512],
                                acc[:, _t * 1024 + _og * 512:_t * 1024 + (_og + 1) * 512],
                                b.psum_f[_ob][:, :]), [tm])
                        bank[ob] = ta
            w2done[wsl] = w2rd
            hs_done[wsl] = hs_rd
        xnf_done[gsl] = [t_h, t_t1]
    return res_toks







def emit_lat(b, x_in, x_out, W, scratch):
    nc = b.nc
    st = b.st
    cfg = b.cfg
    NG = cfg["NG"]
    GPS = cfg["SEQ"] // 512

    b.arena_reset()
    xnf = [b.abf(8 * 512) for _ in range(2)]
    q_fm = b.abf(8 * 512)
    k_fm = b.abf(8 * 512)
    v_ext = b.abf(4 * 1032)
    av = b.abf(4 * 512)
    ktm = b.abf(4 * 1024)
    s_ext = b.abf(8 * 1025)
    wqk = [b.abf(8 * 128) for _ in range(2)]
    wv = [b.abf(8 * 512) for _ in range(2)]
    wp = b.abf(8 * 1024)
    nrm = [b.abf(1024) for _ in range(2)]
    nrmf = b.abf(1024)
    tmp1s = [b.abf(512) for _ in range(2)]
    tmp2s = [b.abf(512) for _ in range(2)]
    tmp3s = [b.abf(512) for _ in range(2)]
    gates = [b.abf(512) for _ in range(2)]
    S32 = b.af32(8 * 1024)
    rx = b.af32(1024)
    sc = nc.alloc_sbuf_tensor("lat_sc", [128, 16], F32)
    kcs = nc.alloc_sbuf_tensor("lat_kcs", [128, 8], F32)

    ts = b.ts_bf
    pre = [st["tok_cst"], st["tok_ones"][0], st["tok_ones"][1], st["tok_bvb"]] \
        + st["tok_id"] + st["tok_mk"] + st["tok_l1attn_bias"]
    wq_ap = scratch["wqkv_b"]
    wg_ap = scratch["wg_b"]
    wp_ap = scratch["wp_b"]

    bank = {"tr": None, 0: None, 1: None, 2: None, 3: None, 4: None, 5: None, "b1": None}
    xnf_done = [[], []]
    wqk_done = [None, None]
    wv_done = [None, None]
    wp_done = None
    vext_done = None
    av_done = None
    q_used = None
    k_used = None
    S_done = []
    sext_rd = None
    nrm_done = [None, None]
    nrmf_done = None
    rx_done = None
    tmp_done = {}
    res_toks = []
    wcnt = [0]

    t_vones = b.gp(lambda e: e.memset(
        v_ext[:, :].rearrange("p (t c) -> p t c", t=4)[:, :, 1024:1025], 1.0), pre)
    vones_tok = t_vones

    for c in range(NG):
        cin = c % GPS
        csl = c % 2
        xnf_g = xnf[csl]
        xnf_r = xnf_g.rearrange("p (c t) -> p c t", c=8)
        for j in range(4):
            slot = j % 3
            xn, t_xn, xt, tl = ts.tile(x_in, c * 512 + j * 128, slot, extra_x_deps=pre, ln2=True)
            tev = ts.transpose_to(xn, t_xn, xnf_r[:, :, j * 128:(j + 1) * 128],
                                  None, extra=xnf_done[csl],
                                  gain=st["lng_col"], beta=st["lnb_col"], pb=j % 2)
            ts.xt_done[slot] = t_xn
            ts.xn_done[slot] = tev
            t_fm = tev

        if cin == 0:
            t_s0 = b.dve(lambda e: e.memset(S32[:, :], 0.0), S_done)
            t_k0 = b.dve(lambda e: e.memset(kcs[:, :], 0.0), S_done)
            S_done = [t_s0, t_k0]

        def win_load(src_ap, col0, extra=()):
            wsl = wcnt[0] % 2
            wcnt[0] += 1
            q = ("sync", b.sq[3]) if wsl == 0 else ("scalar", b.aq[5])
            tw = b.dma(q[0], q[1], lambda e, _c0=col0, _w=wsl, _s=src_ap: e.dma_start(
                out=wqk[_w][:, :].rearrange("p (d f) -> p d f", d=8),
                in_=_s.rearrange("(d p) f -> p d f", p=128)[:, :, _c0:_c0 + 128]),
                deps=[wqk_done[wsl], b.wcast.get("wqkv"), b.wcast.get("wg")] + list(extra))
            return tw, wsl

        def fproj(src_ap, col0, pb, extra_mm=()):
            tw, wsl = win_load(src_ap, col0)
            tm = None
            for dc in range(8):
                tm = b.pe(lambda e, _dc=dc, _pb=pb, _w=wsl, _x=xnf_g: e.matmul(
                    b.psum_f[_pb][:, :], wqk[_w][:, _dc * 128:(_dc + 1) * 128],
                    _x[:, _dc * 512:(_dc + 1) * 512],
                    start=(_dc == 0), stop=(_dc == 7)),
                    deps=[t_fm, tw, bank[pb]] + list(extra_mm), inc=(dc == 7))
            wqk_done[wsl] = tm
            return tm

        for fc in range(8):
            fsl = fc % 2
            tmp1, tmp2, tmp3, gate = tmp1s[fsl], tmp2s[fsl], tmp3s[fsl], gates[fsl]
            pb_g = fsl
            pb_k = 2 + fsl
            pb_q = fsl
            tm = fproj(wg_ap, fc * 128, pb_g)
            t_g = b.act(lambda e, _fc=fc, _pb=pb_g, _o=gate: e.activation(
                _o[:, :], b.psum_f[_pb][:, :], AF.Sigmoid,
                bias=st["bg_col"][:, _fc:_fc + 1]), [tm, tmp_done.get(("gate", fsl))])
            bank[pb_g] = t_g
            tm = fproj(wq_ap, 1024 + fc * 128, pb_k)
            t_kg = b.dve(lambda e, _fc=fc, _pb=pb_k, _t1=tmp1, _g=gate: e.scalar_tensor_tensor(
                _t1[:, :], b.psum_f[_pb][:, :], st["bk_col"][:, _fc:_fc + 1], _g[:, :],
                ALU.add, ALU.mult), [tm, t_g, tmp_done.get(("t1", fsl))])
            tmp_done[("gate", fsl)] = t_kg
            t_r1 = b.act(lambda e, _t1=tmp1, _t2=tmp2: e.activation(_t2[:, :], _t1[:, :], AF.Relu,
                                                                    scale=-1.0),
                         [t_kg, tmp_done.get(("t2", fsl))])
            t_e = b.act(lambda e, _t2=tmp2, _t3=tmp3: e.activation(_t3[:, :], _t2[:, :], AF.Exp,
                                                                   scale=-1.0),
                        [t_r1, tmp_done.get(("t3", fsl))])
            t_r2 = b.act(lambda e, _t1=tmp1, _t2=tmp2: e.activation(_t2[:, :], _t1[:, :], AF.Relu),
                         [t_e])
            t_k = b.dve(lambda e, _fc=fc, _t2=tmp2, _t3=tmp3: e.tensor_add(
                k_fm[:, _fc * 512:(_fc + 1) * 512], _t2[:, :], _t3[:, :]), [t_r2, t_e, k_used])
            bank[pb_k] = t_kg
            tm = fproj(wq_ap, fc * 128, pb_q)
            t_r3 = b.act(lambda e, _fc=fc, _pb=pb_q, _t1=tmp1: e.activation(
                _t1[:, :], b.psum_f[_pb][:, :], AF.Relu, scale=-1.0,
                bias=st["nbq_col"][:, _fc:_fc + 1]), [tm, t_k])
            t_e2 = b.act(lambda e, _t1=tmp1, _t3=tmp3: e.activation(_t3[:, :], _t1[:, :], AF.Exp,
                                                                    scale=-1.0),
                         [t_r3, t_k])
            t_r4 = b.act(lambda e, _fc=fc, _pb=pb_q, _t1=tmp1: e.activation(
                _t1[:, :], b.psum_f[_pb][:, :], AF.Relu,
                bias=st["bq_col"][:, _fc:_fc + 1]), [t_e2])
            t_q = b.dve(lambda e, _fc=fc, _t1=tmp1, _t3=tmp3: e.tensor_add(
                q_fm[:, _fc * 512:(_fc + 1) * 512], _t1[:, :], _t3[:, :]), [t_r4, t_e2, q_used])
            bank[pb_q] = t_r4
            tmp_done[("t1", fsl)] = t_q
            tmp_done[("t2", fsl)] = t_k
            tmp_done[("t3", fsl)] = t_q
        t_qk = t_q

        for og in range(2):
            wsl = og
            tw_v = b.dma("scalar", b.aq[4], lambda e, _og=og, _w=wsl: e.dma_start(
                out=wv[_w][:, :].rearrange("p (d f) -> p d f", d=8),
                in_=wq_ap.rearrange("(d p) f -> p d f", p=128)[:, :, 2048 + _og * 512:2048 + (_og + 1) * 512]),
                deps=[wv_done[wsl], b.wcast.get("wqkv")])
            for tt in range(4):
                pb = 2 + tt % 2
                tm = None
                for dc in range(8):
                    tm = b.pe(lambda e, _dc=dc, _tt=tt, _pb=pb, _w=wsl, _x=xnf_g: e.matmul(
                        b.psum_f[_pb][:, :], _x[:, _dc * 512 + _tt * 128:_dc * 512 + _tt * 128 + 128],
                        wv[_w][:, _dc * 512:(_dc + 1) * 512],
                        start=(_dc == 0), stop=False),
                        deps=[t_fm, tw_v, bank[pb]], inc=False)
                tm = b.pe(lambda e, _og=og, _pb=pb: e.matmul(
                    b.psum_f[_pb][:, :], st["ones_row_b"][:, :],
                    st["bv_row_b"][:, _og * 512:(_og + 1) * 512],
                    start=False, stop=True), deps=[], inc=True)
                if tt == 3:
                    wv_done[wsl] = tm
                t_ve = b.act(lambda e, _tt=tt, _og=og, _pb=pb: e.activation(
                    v_ext[:, _tt * 1032 + _og * 512:_tt * 1032 + _og * 512 + 512],
                    b.psum_f[_pb][:, :], AF.Copy), [tm, vext_done, vones_tok])
                bank[pb] = t_ve
        t_v = t_ve

        t_av = []
        for stl in range(4):
            pb = stl % 2
            tm = None
            for dc in range(8):
                tm = b.pe(lambda e, _dc=dc, _stl=stl, _pb=pb: e.matmul(
                    b.psum_f[_pb][:, :], k_fm[:, _dc * 512 + _stl * 128:_dc * 512 + _stl * 128 + 128],
                    q_fm[:, _dc * 512:(_dc + 1) * 512],
                    start=(_dc == 0), stop=(_dc == 7)),
                    deps=[t_qk, bank[pb]], inc=(dc == 7))
            t_c2 = b.dve(lambda e, _stl=stl, _pb=pb: e.tensor_mul(
                av[:, _stl * 512:(_stl + 1) * 512], b.psum_f[_pb][:, :],
                st["mk_b"][:, _stl * 512:(_stl + 1) * 512]), [tm, av_done])
            bank[pb] = t_c2
            t_av.append(t_c2)

        tw_p = None
        for dc in range(0, 8, 4):
            tw_p = b.dma("sync", b.sq[6], lambda e, _dc=dc: e.dma_start(
                out=wp[:, _dc * 1024:(_dc + 4) * 1024].rearrange("p (c k) -> p c k", c=4),
                in_=wp_ap.rearrange("(c p) f -> p c f", p=128)[:, _dc:_dc + 4, :]),
                deps=[wp_done, b.wcast.get("wp")])

        out_mm_toks = []
        for tt in range(4):
            nsl = tt % 2
            groups = []
            tm4 = tm5 = tmd = None
            first = True
            for stl in range(tt + 1):
                lastin = (cin == 0 and stl == tt)
                tm4 = b.pe(lambda e, _stl=stl, _tt=tt, _f=first, _l=lastin: e.matmul(
                    b.psum_f[4][:, :], av[:, _stl * 512 + _tt * 128:_stl * 512 + _tt * 128 + 128],
                    v_ext[:, _stl * 1032:_stl * 1032 + 512],
                    start=_f, stop=_l), deps=[t_av[stl], t_v, bank[4]], inc=lastin)
                tm5 = b.pe(lambda e, _stl=stl, _tt=tt, _f=first, _l=lastin: e.matmul(
                    b.psum_f[5][:, :], av[:, _stl * 512 + _tt * 128:_stl * 512 + _tt * 128 + 128],
                    v_ext[:, _stl * 1032 + 512:_stl * 1032 + 1024],
                    start=_f, stop=_l), deps=[bank[5]], inc=lastin)
                tmd = b.pe(lambda e, _stl=stl, _tt=tt, _f=first, _l=lastin: e.matmul(
                    b.psum_f[3][:, 0:1], av[:, _stl * 512 + _tt * 128:_stl * 512 + _tt * 128 + 128],
                    v_ext[:, _stl * 1032 + 1024:_stl * 1032 + 1025],
                    start=_f, stop=_l), deps=[bank[3]], inc=lastin)
                first = False
            if cin > 0:
                for dc in range(8):
                    last = dc == 7
                    tm4 = b.pe(lambda e, _dc=dc, _tt=tt, _l=last: e.matmul(
                        b.psum_f[4][:, :], q_fm[:, _dc * 512 + _tt * 128:_dc * 512 + _tt * 128 + 128],
                        s_ext[:, _dc * 1025:_dc * 1025 + 512],
                        start=False, stop=_l), deps=[t_qk] + S_done, inc=last)
                    tm5 = b.pe(lambda e, _dc=dc, _tt=tt, _l=last: e.matmul(
                        b.psum_f[5][:, :], q_fm[:, _dc * 512 + _tt * 128:_dc * 512 + _tt * 128 + 128],
                        s_ext[:, _dc * 1025 + 512:_dc * 1025 + 1024],
                        start=False, stop=_l), deps=[], inc=last)
                    tmd = b.pe(lambda e, _dc=dc, _tt=tt, _l=last: e.matmul(
                        b.psum_f[3][:, 0:1], q_fm[:, _dc * 512 + _tt * 128:_dc * 512 + _tt * 128 + 128],
                        s_ext[:, _dc * 1025 + 1024:_dc * 1025 + 1025],
                        start=False, stop=_l), deps=[], inc=last)
            out_mm_toks += [tm4, tm5, tmd]
            t_d1 = b.dve(lambda e: e.tensor_single_scalar(sc[:, 0:1], b.psum_f[3][:, 0:1],
                                                          DEN_EPS, ALU.add), [tmd])
            t_rc = b.dve(lambda e: e.reciprocal(sc[:, 1:2], sc[:, 0:1]), [t_d1])
            t_n0 = b.dve(lambda e, _n=nsl: e.tensor_scalar(
                nrm[_n][:, 0:512], b.psum_f[4][:, :], sc[:, 1:2], None, ALU.mult),
                [t_rc, tm4, nrm_done[nsl]])
            t_n1 = b.dve(lambda e, _n=nsl: e.tensor_scalar(
                nrm[_n][:, 512:1024], b.psum_f[5][:, :], sc[:, 1:2], None, ALU.mult),
                [t_rc, tm5])
            bank[4] = t_n0
            bank[5] = t_n1
            bank[3] = t_d1
            tt_tr = None
            for ch in range(8):
                tt_tr = b.pe(lambda e, _ch=ch, _n=nsl: e.transpose(
                    b.psum_b[1][:, _ch * 128:(_ch + 1) * 128],
                    nrm[_n][:, _ch * 128:(_ch + 1) * 128], st["id_b"][:, :]),
                    deps=[t_n0, t_n1, b.pbd[1]], inc=(ch == 7))
            t_nf = b.dve(lambda e: e.tensor_copy(
                nrmf[:, :].rearrange("p (c t) -> p c t", c=8),
                b.psum_b[1][:, :].rearrange("p (c t) -> p c t", c=8)), [tt_tr, nrmf_done])
            b.pbd[1] = t_nf
            nrm_done[nsl] = t_nf
            trx = b.dma("sync", b.sq[2], lambda e, _c=c, _tt=tt: e.dma_start(
                out=rx[:, :], in_=x_in[_c * 512 + _tt * 128:_c * 512 + (_tt + 1) * 128, :]),
                deps=[rx_done])
            for og in range(2):
                pb = 4 if og == 0 else 5
                tm = None
                for dc in range(8):
                    tm = b.pe(lambda e, _dc=dc, _og=og, _pb=pb: e.matmul(
                        b.psum_f[_pb][:, :], nrmf[:, _dc * 128:(_dc + 1) * 128],
                        wp[:, _dc * 1024 + _og * 512:_dc * 1024 + _og * 512 + 512],
                        start=(_dc == 0), stop=False),
                        deps=[t_nf, tw_p, bank[pb]], inc=False)
                tm = b.pe(lambda e, _og=og, _pb=pb: e.matmul(
                    b.psum_f[_pb][:, :], st["ones_row_f"][:, :],
                    st["b_proj_row"][:, _og * 512:(_og + 1) * 512],
                    start=False, stop=True), deps=[], inc=True)
                wp_done = tm
                t_o = b.dve(lambda e, _og=og, _pb=pb: e.tensor_add(
                    rx[:, _og * 512:(_og + 1) * 512], b.psum_f[_pb][:, :],
                    rx[:, _og * 512:(_og + 1) * 512]), [tm, trx])
                bank[pb] = t_o
            tst = b.dma("scalar", b.aq[tt % 2], lambda e, _c=c, _tt=tt: e.dma_start(
                out=x_out[_c * 512 + _tt * 128:_c * 512 + (_tt + 1) * 128, :], in_=rx[:, :]),
                deps=[bank[4], bank[5]])
            rx_done = tst
            res_toks.append(tst)
            nrmf_done = tm

        if cin < GPS - 1:
            ktm_ev = []
            for stl in range(4):
                tt_tr = None
                for ch in range(8):
                    tt_tr = b.pe(lambda e, _ch=ch, _stl=stl: e.transpose(
                        b.psum_b[1][:, _ch * 128:(_ch + 1) * 128],
                        k_fm[:, _ch * 512 + _stl * 128:_ch * 512 + _stl * 128 + 128],
                        st["id_b"][:, :]), deps=[t_qk, b.pbd[1]], inc=(ch == 7))
                t_ke = b.dve(lambda e, _stl=stl: e.tensor_copy(
                    ktm[:, _stl * 1024:(_stl + 1) * 1024].rearrange("p (c t) -> p c t", c=8),
                    b.psum_b[1][:, :].rearrange("p (c t) -> p c t", c=8)), [tt_tr])
                b.pbd[1] = t_ke
                ktm_ev.append(t_ke)
            new_S_done = []
            for d1c in range(8):
                for og in range(2):
                    pb = 2 + (d1c * 2 + og) % 2
                    tm = None
                    for stl in range(4):
                        tm = b.pe(lambda e, _stl=stl, _d=d1c, _og=og, _pb=pb: e.matmul(
                            b.psum_f[_pb][:, :], ktm[:, _stl * 1024 + _d * 128:_stl * 1024 + _d * 128 + 128],
                            v_ext[:, _stl * 1032 + _og * 512:_stl * 1032 + _og * 512 + 512],
                            start=(_stl == 0), stop=(_stl == 3)),
                            deps=[ktm_ev[stl], t_v, bank[pb]] + out_mm_toks[-3:], inc=(stl == 3))
                    t_sa = b.dve(lambda e, _d=d1c, _og=og, _pb=pb: e.tensor_add(
                        S32[:, _d * 1024 + _og * 512:_d * 1024 + _og * 512 + 512],
                        S32[:, _d * 1024 + _og * 512:_d * 1024 + _og * 512 + 512],
                        b.psum_f[_pb][:, :]), [tm] + S_done)
                    t_sc = b.dve(lambda e, _d=d1c, _og=og: e.tensor_copy(
                        s_ext[:, _d * 1025 + _og * 512:_d * 1025 + _og * 512 + 512],
                        S32[:, _d * 1024 + _og * 512:_d * 1024 + _og * 512 + 512]), [t_sa])
                    bank[pb] = t_sa
                    new_S_done.append(t_sc)
            for dc in range(8):
                t_rs = b.dve(lambda e, _dc=dc: e.reduce_sum(
                    sc[:, 2:3], k_fm[:, _dc * 512:(_dc + 1) * 512], axis=AX.X), [t_qk])
                t_ka = b.dve(lambda e, _dc=dc: e.tensor_add(
                    kcs[:, _dc:_dc + 1], kcs[:, _dc:_dc + 1], sc[:, 2:3]), [t_rs] + S_done)
                t_kc = b.dve(lambda e, _dc=dc: e.tensor_copy(
                    s_ext[:, _dc * 1025 + 1024:_dc * 1025 + 1025], kcs[:, _dc:_dc + 1]), [t_ka])
                new_S_done.append(t_kc)
            S_done = new_S_done
            vext_done = tm
            k_used = tm
        else:
            vext_done = out_mm_toks[-3]
            k_used = t_av[-1]
        av_done = out_mm_toks[-3]
        q_used = out_mm_toks[-3]
        xnf_done[csl] = [t_v, t_qk]
    return res_toks



def emit_mla(b, x_in, x_out, W, scratch):
    nc = b.nc
    st = b.st
    cfg = b.cfg
    NG = cfg["NG"]
    SEQ = cfg["SEQ"]
    GPS = SEQ // 512
    NSEQ = cfg["TOK"] // SEQ
    NTILE_SEQ = SEQ // 128
    qscr = scratch["qscr"]

    b.arena_reset()
    wdkv = b.abf(8 * 256)
    wuk = b.abf(2 * 1024)
    wuv = b.abf(2 * 1024)
    wuq = [b.abf(8 * 128) for _ in range(2)]
    wo = b.abf(8 * 512)
    xnf = b.abf(8 * 512)
    ckv_g = b.abf(2 * 512)
    k_fm = b.abf(8 * SEQ)
    v_ext = b.abf(NTILE_SEQ * 1040)
    qld = b.abf(8 * 512)
    qst = [b.abf(512) for _ in range(2)]
    pT = [b.abf(512) for _ in range(3)]
    ctxn = b.abf(4 * 1024)
    ctxf = b.abf(1024)
    rx = [b.af32(1024) for _ in range(2)]
    sc = nc.alloc_sbuf_tensor("mla_sc", [128, 8], F32)

    ts = b.ts_bf
    pre = [st["tok_cst"], st["tok_ones"][1], st["tok_ones"][3]] + st["tok_id"] + st["tok_mk"]

    tws = []
    for dc in range(8):
        tws.append(b.dma("gpsimd", b.gq[6], lambda e, _dc=dc: e.dma_start(
            out=wdkv[:, _dc * 256:(_dc + 1) * 256],
            in_=W["L0_attn_W_DKV"][_dc * 128:(_dc + 1) * 128, :])))
    for lc in range(2):
        tws.append(b.dma("gpsimd", b.gq[6], lambda e, _lc=lc: e.dma_start(
            out=wuk[:, _lc * 1024:(_lc + 1) * 1024],
            in_=W["L0_attn_W_UK"][_lc * 128:(_lc + 1) * 128, :])))
        tws.append(b.dma("gpsimd", b.gq[6], lambda e, _lc=lc: e.dma_start(
            out=wuv[:, _lc * 1024:(_lc + 1) * 1024],
            in_=W["L0_attn_W_UV"][_lc * 128:(_lc + 1) * 128, :])))
    tw_res = tws[-1]

    bank = {"tr": None, 0: None, 1: None, 2: None, 3: None, 4: None, 5: None, "b1": None}
    xnf_done = []
    ckv_done = []
    kfm_done = []
    vext_done = []
    qst_done = [None, None]
    wuq_done = [None, None]
    qld_done = None
    pT_done = [None, None, None]
    ctxn_done = [None] * 4
    ctxf_done = None
    wo_done = None
    rx_done = [None, None]
    res_toks = []
    t_vones = b.gp(lambda e: e.memset(
        v_ext[:, :].rearrange("p (t h c) -> p t h c", t=NTILE_SEQ, h=16)[:, :, :, 64:65], 1.0), pre)

    for s in range(NSEQ):
        s0 = s * SEQ
        new_kfm_done = []
        new_vext_done = []
        new_ckv_done = []
        for g in range(GPS):
            t0 = s0 + g * 512
            xnf_r = xnf.rearrange("p (c t) -> p c t", c=8)
            for j in range(4):
                slot = j % 3
                xn, t_xn, xt, tl = ts.tile(x_in, t0 + j * 128, slot, extra_x_deps=pre)
                tev = ts.transpose_to(xn, t_xn, xnf_r[:, :, j * 128:(j + 1) * 128],
                                      None, extra=xnf_done, pb=j % 2)
                ts.xt_done[slot] = t_xn
                ts.xn_done[slot] = tev
                t_fm = tev
            t_ckv = None
            for lc in range(2):
                pb = lc
                tm = None
                for dc in range(8):
                    tm = b.pe(lambda e, _dc=dc, _lc=lc, _pb=pb: e.matmul(
                        b.psum_f[_pb][:, :], wdkv[:, _dc * 256 + _lc * 128:_dc * 256 + _lc * 128 + 128],
                        xnf[:, _dc * 512:(_dc + 1) * 512],
                        start=(_dc == 0), stop=(_dc == 7)),
                        deps=[t_fm, tw_res, bank[pb]] + ckv_done, inc=(dc == 7))
                t_ckv = b.dve(lambda e, _lc=lc, _pb=pb: e.tensor_copy(
                    ckv_g[:, _lc * 512:(_lc + 1) * 512], b.psum_f[_pb][:, :]), [tm])
                bank[pb] = t_ckv
            new_ckv_done = [t_ckv]
            for fc in range(8):
                pb = fc % 2
                tm = None
                for lc in range(2):
                    tm = b.pe(lambda e, _fc=fc, _lc=lc, _pb=pb: e.matmul(
                        b.psum_f[_pb][:, :], wuk[:, _lc * 1024 + _fc * 128:_lc * 1024 + _fc * 128 + 128],
                        ckv_g[:, _lc * 512:(_lc + 1) * 512],
                        start=(_lc == 0), stop=(_lc == 1)),
                        deps=[t_ckv, bank[pb]] + kfm_done, inc=(lc == 1))
                t_k = b.dve(lambda e, _fc=fc, _g=g, _pb=pb: e.tensor_copy(
                    k_fm[:, _fc * SEQ + _g * 512:_fc * SEQ + (_g + 1) * 512], b.psum_f[_pb][:, :]), [tm])
                bank[pb] = t_k
                new_kfm_done.append(t_k)
            for fc in range(8):
                wsl = fc % 2
                pb = fc % 3
                tw = b.dma("sync", b.sq[3], lambda e, _fc=fc, _w=wsl: e.dma_start(
                    out=wuq[_w][:, :].rearrange("p (d f) -> p d f", d=8),
                    in_=scratch["wuq_b"].rearrange("(d p) f -> p d f", p=128)[:, :, _fc * 128:(_fc + 1) * 128]),
                    deps=[wuq_done[wsl], b.wcast.get("wuq")])
                tm = None
                for dc in range(8):
                    tm = b.pe(lambda e, _dc=dc, _pb=pb, _w=wsl: e.matmul(
                        b.psum_f[_pb][:, :], wuq[_w][:, _dc * 128:(_dc + 1) * 128],
                        xnf[:, _dc * 512:(_dc + 1) * 512],
                        start=(_dc == 0), stop=(_dc == 7)),
                        deps=[t_fm, tw, bank[pb]], inc=(dc == 7))
                wuq_done[wsl] = tm
                t_q = b.act(lambda e, _pb=pb, _w=wsl: e.activation(
                    qst[_w][:, :], b.psum_f[_pb][:, :], AF.Copy), [tm, qst_done[wsl]])
                bank[pb] = t_q
                qst_done[wsl] = b.dma("sync", b.sq[4 + wsl], lambda e, _fc=fc, _g=g, _s=s, _w=wsl: e.dma_start(
                    out=qscr[_fc * 128:(_fc + 1) * 128, _s * SEQ + _g * 512:_s * SEQ + (_g + 1) * 512],
                    in_=qst[_w][:, :]), deps=[t_q])
            for tt in range(4):
                gt = g * 4 + tt
                pb = tt % 2
                for og in range(2):
                    tm = None
                    for lc in range(2):
                        tm = b.pe(lambda e, _tt=tt, _lc=lc, _og=og, _pb=pb: e.matmul(
                            b.psum_f[_pb][:, :], ckv_g[:, _lc * 512 + _tt * 128:_lc * 512 + _tt * 128 + 128],
                            wuv[:, _lc * 1024 + _og * 512:_lc * 1024 + _og * 512 + 512],
                            start=(_lc == 0), stop=(_lc == 1)),
                            deps=[t_ckv, bank[pb]] + vext_done, inc=(lc == 1))
                    t_v = b.act(lambda e, _gt=gt, _og=og, _pb=pb: e.activation(
                        v_ext[:, :].rearrange("p (t h c) -> p t h c", t=NTILE_SEQ, h=16)
                        [:, _gt, _og * 8:(_og + 1) * 8, 0:64],
                        b.psum_f[_pb][:, :].rearrange("p (h c) -> p h c", h=8), AF.Copy), [tm, t_vones])
                    bank[pb] = t_v
                    new_vext_done.append(t_v)
        xnf_done = [t_q, t_ckv]
        ckv_done = new_ckv_done
        kfm_done_prev = kfm_done
        kfm_done = new_kfm_done
        vext_done = new_vext_done
        t_kv_ready = [kfm_done[-1], new_vext_done[-1], t_vones]

        for qtg in range(GPS):
            tq = b.dma("sync", b.sq[6], lambda e, _s=s, _qtg=qtg: e.dma_start(
                out=qld[:, :].rearrange("p (c t) -> p c t", c=8),
                in_=qscr.rearrange("(c p) t -> p c t", p=128)[:, :, _s * SEQ + _qtg * 512:_s * SEQ + (_qtg + 1) * 512]),
                deps=[qld_done, qst_done[0], qst_done[1]])
            jmax = 4 * qtg + 3
            ctx_last = None
            ctx_banks_done = [bank[0], bank[1], bank[2], bank[3]]
            for h in range(16):
                hb = (h % 2) * 64
                fcq = h // 2
                pend = []

                def emit_ctx(pend):
                    nonlocal ctx_last
                    j, psl, t_e = pend
                    for tt in range(4):
                        if j > 4 * qtg + tt:
                            continue
                        first = (j == 0)
                        lastj = (j == 4 * qtg + tt)
                        tm_c = b.pe(lambda e, _h=h, _j=j, _tt=tt, _sl=psl, _f=first, _l=lastj: e.matmul(
                            b.psum_f[_tt][:, 0:65],
                            pT[_sl][:, _tt * 128:(_tt + 1) * 128],
                            v_ext[:, _j * 1040 + _h * 65:_j * 1040 + (_h + 1) * 65],
                            start=_f, stop=_l),
                            deps=[t_e] + ([ctx_banks_done[tt]] if first else []),
                            inc=lastj)
                        if lastj:
                            ctx_last = tm_c
                        pT_done[psl] = tm_c

                for j in range(jmax + 1):
                    skey = 4 + j % 2
                    tm_s = b.pe(lambda e, _h=h, _j=j, _hb=hb, _fcq=fcq, _pb=skey: e.matmul(
                        b.psum_f[_pb][:, :],
                        k_fm[_hb:_hb + 64, _fcq * SEQ + _j * 128:_fcq * SEQ + (_j + 1) * 128],
                        qld[_hb:_hb + 64, _fcq * 512:(_fcq + 1) * 512],
                        start=True, stop=True),
                        deps=[tq, bank[skey]] + t_kv_ready, inc=True)
                    psl = j % 3
                    t_e = b.act(lambda e, _pb=skey, _sl=psl: e.activation(
                        pT[_sl][:, :], b.psum_f[_pb][:, :], AF.Exp, scale=0.125),
                        [tm_s, pT_done[psl]])
                    bank[skey] = t_e
                    a = j - 4 * qtg
                    if 0 <= a <= 3:
                        t_e = b.dve(lambda e, _sl=psl, _a=a: e.tensor_mul(
                            pT[_sl][:, :], pT[_sl][:, :],
                            st["mk_b"][:, _a * 512:(_a + 1) * 512]), [t_e])
                    pend.append((j, psl, t_e))
                    if len(pend) > 2:
                        emit_ctx(pend.pop(0))
                for p_ in pend:
                    emit_ctx(p_)
                pend = []
                for tt in range(4):
                    t_d = b.dve(lambda e, _tt=tt: e.reciprocal(sc[:, _tt:_tt + 1],
                                                               b.psum_f[_tt][:, 64:65]), [ctx_last])
                    t_n = b.dve(lambda e, _h=h, _tt=tt: e.tensor_scalar(
                        ctxn[:, _tt * 1024 + _h * 64:_tt * 1024 + (_h + 1) * 64],
                        b.psum_f[_tt][:, 0:64], sc[:, _tt:_tt + 1], None, ALU.mult),
                        [t_d, ctxn_done[tt]])
                    ctx_banks_done[tt] = t_n
                    bank[tt] = t_n
            for tt in range(4):
                ttg = qtg * 4 + tt
                trc = None
                for ch in range(8):
                    trc = b.pe(lambda e, _ch=ch, _tt=tt: e.transpose(
                        b.psum_b[1][:, _ch * 128:(_ch + 1) * 128],
                        ctxn[:, _tt * 1024 + _ch * 128:_tt * 1024 + _ch * 128 + 128],
                        st["id_b"][:, :]), deps=[bank[tt], b.pbd[1]], inc=(ch == 7))
                t_cf = b.dve(lambda e: e.tensor_copy(
                    ctxf[:, :].rearrange("p (c t) -> p c t", c=8),
                    b.psum_b[1][:, :].rearrange("p (c t) -> p c t", c=8)), [trc, ctxf_done])
                b.pbd[1] = t_cf
                ctxn_done[tt] = t_cf
                rsl = tt % 2
                trx = b.dma("sync", b.sq[rsl], lambda e, _s=s, _ttg=ttg, _r=rsl: e.dma_start(
                    out=rx[_r][:, :], in_=x_in[_s * SEQ + _ttg * 128:_s * SEQ + (_ttg + 1) * 128, :]),
                    deps=[rx_done[rsl]])
                for og in range(2):
                    pb = 4 + og
                    tw_o = b.dma("scalar", b.aq[4], lambda e, _og=og: e.dma_start(
                        out=wo[:, :].rearrange("p (d f) -> p d f", d=8),
                        in_=scratch["wo_b"].rearrange("(d p) f -> p d f", p=128)[:, :, _og * 512:(_og + 1) * 512]),
                        deps=[wo_done, b.wcast.get("wo")])
                    tm = None
                    for dc in range(8):
                        tm = b.pe(lambda e, _dc=dc, _og=og, _pb=pb: e.matmul(
                            b.psum_f[_pb][:, :], ctxf[:, _dc * 128:(_dc + 1) * 128],
                            wo[:, _dc * 512:(_dc + 1) * 512],
                            start=(_dc == 0), stop=(_dc == 7)),
                            deps=[t_cf, tw_o, bank[pb]], inc=(dc == 7))
                    wo_done = tm
                    t_o = b.dve(lambda e, _og=og, _pb=pb, _r=rsl: e.tensor_add(
                        rx[_r][:, _og * 512:(_og + 1) * 512], b.psum_f[_pb][:, :],
                        rx[_r][:, _og * 512:(_og + 1) * 512]), [tm, trx])
                    bank[pb] = t_o
                tst = b.dma("scalar", b.aq[rsl], lambda e, _s=s, _ttg=ttg, _r=rsl: e.dma_start(
                    out=x_out[_s * SEQ + _ttg * 128:_s * SEQ + (_ttg + 1) * 128, :], in_=rx[_r][:, :]),
                    deps=[bank[4], bank[5]])
                rx_done[rsl] = tst
                ctxf_done = tm
                res_toks.append(tst)
            qld_done = ctx_last
    return res_toks
